# version 17
# baseline (speedup 1.0000x reference)
import numpy as np
import ml_dtypes
_E_STAGES = {}
_R_STAGES = {}
from contextlib import ExitStack
import concourse.bass as bass
import concourse.mybir as mybir
from concourse.bass_utils import run_bass_kernel_spmd

F32 = mybir.dt.float32
BF16 = mybir.dt.bfloat16
AF = mybir.ActivationFunctionType
ALU = mybir.AluOpType
AX = mybir.AxisListType

D = 1024
N_EXPERTS = 8
EXPERT_LIST = None
S = 8192
NCH = 16
TQ = 2048
NQT = 16
NEG = -3.0e38

ENGINES = ("sync", "scalar", "vector", "tensor", "gpsimd")


class Ctr:
    __slots__ = ("sem", "n", "name")
    def __init__(self, sem, name):
        self.sem = sem; self.n = 0; self.name = name


class PB:
    def __init__(self, nc):
        self.nc = nc
        self.q = {e: [] for e in ENGINES}
        self.waited = {e: {} for e in ENGINES}
        self.ctrs = []

    def ctr(self, stack, name):
        sem = stack.enter_context(self.nc.semaphore(name))
        c = Ctr(sem, name)
        self.ctrs.append(c)
        return c

    def emit(self, eng, fn, waits=(), inc=None, inck=1):
        w = [(c, v) for (c, v) in waits if v > 0 and v > self.waited[eng].get(c.name, 0)]
        for c, v in w:
            self.waited[eng][c.name] = v
        tok = None
        if inc is not None:
            inc.n += inck
            tok = inc.n
        def run(e, fn=fn, w=tuple(w), inc=inc, inck=inck):
            for c, v in w:
                e.wait_ge(c.sem, v)
            inst = fn(e)
            if inc is not None:
                inst.then_inc(inc.sem, inck)
        self.q[eng].append(run)
        return tok

    def wait(self, eng, *waits):
        w = [(c, v) for (c, v) in waits if v > 0 and v > self.waited[eng].get(c.name, 0)]
        for c, v in w:
            self.waited[eng][c.name] = v
        if not w:
            return
        def run(e, w=tuple(w)):
            for c, v in w:
                e.wait_ge(c.sem, v)
        self.q[eng].append(run)

    def sdma(self, ctr, fn, waits=()):
        self.wait("gpsimd", (ctr, ctr.n))
        return self.emit("gpsimd", fn, waits=waits, inc=ctr, inck=16)

    def barrier(self, skip=()):
        snap = [(c, c.n) for c in self.ctrs]
        for e in ENGINES:
            if e in skip:
                continue
            self.wait(e, *snap)

    def build(self):
        with self.nc.Block() as block:
            for eng in ENGINES:
                if not self.q[eng]:
                    continue
                def body(e, q=self.q[eng]):
                    for f in q:
                        f(e)
                getattr(block, eng)(body)


def build_nc(nphases=4, debug=False):
    nc = bass.Bass("TRN2", target_bir_lowering=False)
    pb = PB(nc)

    d_xq   = nc.dram_tensor("xq", [TQ, D], F32, kind="ExternalInput").ap()
    d_xk   = nc.dram_tensor("xk", [S, D], F32, kind="ExternalInput").ap()
    d_mask = nc.dram_tensor("masks", [NCH, 128, 4 * 512], BF16, kind="ExternalInput").ap()
    d_wdec = nc.dram_tensor("w_dec", [D, D], F32, kind="ExternalInput").ap()
    d_wout = nc.dram_tensor("w_out", [D, 2 * D], F32, kind="ExternalInput").ap()
    d_bout = nc.dram_tensor("b_out", [1, D], F32, kind="ExternalInput").ap()
    d_wg   = nc.dram_tensor("Wg", [8, D], F32, kind="ExternalInput").ap()
    d_w1   = nc.dram_tensor("W1", [8, D, 4096], F32, kind="ExternalInput").ap()
    d_b1   = nc.dram_tensor("b1", [8, 32, 128], F32, kind="ExternalInput").ap()
    d_w2   = nc.dram_tensor("W2", [8, 4096, D], F32, kind="ExternalInput").ap()
    d_b2   = nc.dram_tensor("b2", [8, D], F32, kind="ExternalInput").ap()
    d_idb  = nc.dram_tensor("ident_bf", [128, 128], BF16, kind="ExternalInput").ap()
    d_iota = nc.dram_tensor("iota768", [128, 512], F32, kind="ExternalInput").ap()
    d_cnst = nc.dram_tensor("cnst_bf", [128, 256], BF16, kind="ExternalInput").ap()
    d_idf  = nc.dram_tensor("ident_f32", [128, 128], F32, kind="ExternalInput").ap()
    d_y    = nc.dram_tensor("y", [TQ, D], F32, kind="ExternalOutput").ap()
    dbg = {}
    if debug:
        for nm, shp in [("dbg_xnfm", [128, 8 * 2048]), ("dbg_qfm", [128, 8 * 2048]),
                        ("dbg_gates", [128, 16 * 8]), ("dbg_ctx", [128, 16 * 1024]),
                        ("dbg_ml", [128, 32]), ("dbg_yffn", [128, 16 * 1024])]:
            dbg[nm] = nc.dram_tensor(nm, shp, F32, kind="ExternalOutput").ap()

    with ExitStack() as stack:
        e = stack.enter_context

        xn_fm = e(nc.sbuf_tensor("xn_fm", [128, 8 * 2048], BF16))
        s_sb  = e(nc.sbuf_tensor("s_sb", [128, 16 * 1024], BF16))
        q_h   = e(nc.sbuf_tensor("q_h", [128, 16 * 1024], BF16))
        big   = e(nc.sbuf_tensor("big", [128, 32 * 1024], BF16))
        keys  = e(nc.sbuf_tensor("keys", [128, 8 * 1024], BF16))
        stg   = e(nc.sbuf_tensor("stg", [128, 2 * 1024], F32))
        scr8  = e(nc.sbuf_tensor("scr8", [128, 8 * 1024], mybir.dt.uint8))
        mw    = e(nc.sbuf_tensor("mw", [128, 2 * 2048], BF16))
        mlbuf = e(nc.sbuf_tensor("mlbuf", [128, 64], F32))
        stats = e(nc.sbuf_tensor("stats", [128, 48], F32))
        gates = e(nc.sbuf_tensor("gates", [128, 16 * 8], F32))
        gatesb= e(nc.sbuf_tensor("gatesb", [128, 16 * 8], BF16))
        small = e(nc.sbuf_tensor("small", [128, 128], F32))
        logit2 = small[:, 0:128]
        b1T   = e(nc.sbuf_tensor("b1T", [128, 8 * 32], F32))
        wgT   = e(nc.sbuf_tensor("wgT", [128, 64], BF16))
        idb   = e(nc.sbuf_tensor("idb", [128, 128], BF16))
        gsc   = e(nc.sbuf_tensor("gsc", [128, 1024], BF16))
        iota_sb = e(nc.sbuf_tensor("iota_sb", [128, 512], F32))
        cnst_sb = e(nc.sbuf_tensor("cnst_sb", [128, 256], BF16))
        route  = small[:, 0:64]
        routeb = small[:, 64:96].bitcast(BF16)

        rA      = xn_fm
        rQ      = q_h
        xnfm    = rQ[:, :]
        xn_tm   = rA[:, :]
        q_fm    = rA[:, :]
        ctx_fm  = rA[:, :]
        h_sb    = rQ[:, :]
        w2res   = big[:, :]
        ctx_sb  = big[:, :].bitcast(F32)
        wdec_sb = big[:, 16 * 1024: 24 * 1024]
        wout_fm = big[:, 0:16 * 1024]
        y_all   = s_sb[:, :]
        w1bufs  = [keys[:, i * 4096:(i + 1) * 4096] for i in range(2)]
        xg      = mw[:, :]
        kfm_off = 0
        ktm_off = 4096
        wg_sb  = scr8[:, 0:2048].bitcast(BF16)
        idf    = scr8[:, 2048:2560].bitcast(F32)
        b1raw  = scr8[:, 2560:4608].bitcast(F32)
        gTt    = scr8[:, 4608:6656].bitcast(BF16)
        b2_sb  = scr8[:, 0:2048].bitcast(BF16)
        smb    = scr8[:, 0:4096].bitcast(F32)
        pbuf   = scr8[:, 4096:6144].bitcast(BF16)
        ptb    = scr8[:, 6144:8192].bitcast(BF16)
        wstg   = scr8[:, 0:4096].bitcast(BF16)
        bout_sb= scr8[:, 4096:6144].bitcast(BF16)
        stile_b= [scr8[:, 0:2048].bitcast(BF16), scr8[:, 6144:8192].bitcast(BF16)]
        gtile_b= None

        psA = e(nc.psum_tensor("psA", [128, 512], F32))
        psB = e(nc.psum_tensor("psB", [128, 512], F32))
        psC = e(nc.psum_tensor("psC", [128, 1024], F32))
        psD = e(nc.psum_tensor("psD", [128, 1024], F32))
        psE = e(nc.psum_tensor("psE", [128, 1024], BF16))
        psF = e(nc.psum_tensor("psF", [128, 1024], BF16))

        V = pb.ctr(stack, "V"); Sc = pb.ctr(stack, "S"); T = pb.ctr(stack, "T"); Pl = pb.ctr(stack, "P")
        dC = pb.ctr(stack, "dC")
        dX = [pb.ctr(stack, "dX0"), pb.ctr(stack, "dX1")]
        dM = [pb.ctr(stack, "dM0"), pb.ctr(stack, "dM1")]
        dO = [pb.ctr(stack, "dO0"), pb.ctr(stack, "dO1")]
        gW = [pb.ctr(stack, f"gW{i}") for i in range(4)]
        gV = [pb.ctr(stack, f"gV{i}") for i in range(4)]
        gD = pb.ctr(stack, "gD")

        psEF0_cons = [0, 0]
        pb.emit("sync", lambda s_: s_.dma_start(out=idb[:, :], in_=d_idb), inc=dC, inck=16)
        pb.emit("sync", lambda s_: s_.dma_start(out=idf[:, 0:128], in_=d_idf), inc=dC, inck=16)
        tk_id = dC.n
        t_wg = pb.sdma(gW[1], lambda g: g.dma_start(out=wg_sb[0:8, :], in_=d_wg))
        t_ones = 0

        for fc in range(8):
            pb.emit("tensor", lambda t, fc=fc: t.transpose(
                psE.bitcast(BF16)[:, fc * 8:(fc + 1) * 8],
                wg_sb[0:8, fc * 128:(fc + 1) * 128], idb[0:8, 0:8]),
                waits=[(gW[1], t_wg), (dC, tk_id)], inc=T)
        t_wgT_tr = T.n
        t_wgT = pb.emit("vector", lambda v: v.tensor_copy(wgT[:, :], psE.bitcast(BF16)[:, 0:64]),
                        waits=[(T, t_wgT_tr)], inc=V)
        psEF0_cons[1] = t_wgT

        prev_b1_tr = 0
        prev_b1_cp = 0
        for ex in range(8):
            tl = pb.emit("sync", lambda s_, ex=ex: s_.dma_start(out=b1raw[0:32, 0:128], in_=d_b1[ex]),
                         waits=[(T, prev_b1_tr), (dC, tk_id)], inc=dC, inck=16)
            prev_b1_tr = pb.emit("tensor", lambda t: t.transpose(psC[:, 0:32], b1raw[0:32, 0:128], idf[0:32, 0:32]),
                                 waits=[(dC, tl), (V, prev_b1_cp)], inc=T)
            prev_b1_cp = pb.emit("vector", lambda v, ex=ex: v.tensor_copy(b1T[:, ex * 32:(ex + 1) * 32], psC[:, 0:32]),
                                 waits=[(T, prev_b1_tr)], inc=V)
        b1_done_T = prev_b1_tr
        b1_done_V = prev_b1_cp

        ln_prev = [0, 0]
        slot_apply_v = [0, 0]
        gsc_cons_T = 0
        psF_cons_V = 0
        for tt in range(NQT):
            sl = tt % 2
            stg_t = stg[:, sl * 1024:(sl + 1) * 1024]
            tl = pb.emit("sync", lambda s_, tt=tt, stg_t=stg_t: s_.dma_start(
                out=stg_t, in_=d_xq[tt * 128:(tt + 1) * 128, :]),
                waits=[(V, slot_apply_v[sl])], inc=dX[sl], inck=16)
            so = (tt % 2) * 16
            t1 = pb.emit("vector", lambda v, stg_t=stg_t, so=so: v.bn_stats(stats[:, so:so+6], stg_t[:, 0:512]),
                         waits=[(dX[sl], tl), (V, ln_prev[tt % 2])], inc=V)
            t2 = pb.emit("vector", lambda v, stg_t=stg_t, so=so: v.bn_stats(stats[:, so+6:so+12], stg_t[:, 512:1024]),
                         waits=[(V, t1)], inc=V)
            t3 = pb.emit("vector", lambda v, so=so: v.bn_aggr(stats[:, so+12:so+14], stats[:, so:so+12]), waits=[(V, t2)], inc=V)
            t4 = pb.emit("vector", lambda v, so=so: v.tensor_scalar_add(stats[:, so+14:so+15], stats[:, so+13:so+14], 1e-5),
                         waits=[(V, t3)], inc=V)
            t5 = pb.emit("scalar", lambda s_, so=so: s_.activation(stats[:, so+14:so+15], stats[:, so+14:so+15], AF.Ln),
                         waits=[(V, t4)], inc=Sc)
            t6 = pb.emit("scalar", lambda s_, so=so: s_.activation(stats[:, so+14:so+15], stats[:, so+14:so+15], AF.Exp, scale=-0.5),
                         waits=[(Sc, t5)], inc=Sc)
            xnt = xn_tm[:, tt * 1024:(tt + 1) * 1024]
            t7 = pb.emit("vector", lambda v, stg_t=stg_t, so=so, xnt=xnt: v.tensor_scalar(
                xnt, stg_t, stats[:, so+12:so+13], stats[:, so+14:so+15], ALU.subtract, ALU.mult),
                waits=[(Sc, t6), (T, gsc_cons_T)], inc=V)
            ln_prev[tt % 2] = t7
            slot_apply_v[sl] = t7
            pse0 = psF if tt % 2 == 0 else psE
            for fc in range(8):
                pb.emit("tensor", lambda t, fc=fc, xnt=xnt, pse0=pse0: t.transpose(
                    pse0.bitcast(BF16)[:, fc * 128:(fc + 1) * 128],
                    xnt[:, fc * 128:(fc + 1) * 128], idb[:, :]),
                    waits=[(V, t7), (dC, tk_id), (V, psEF0_cons[tt % 2])], inc=T)
            gsc_cons_T = T.n
            psEF0_cons[tt % 2] = pb.emit("vector", lambda v, tt=tt, pse0=pse0: v.tensor_copy(
                xnfm.rearrange("p (c n) -> p c n", c=8)[:, :, tt * 128:(tt + 1) * 128],
                pse0.rearrange("p (c n) -> p c n", c=8)),
                waits=[(T, gsc_cons_T)], inc=V)
        xn_done_v = V.n

        psC_cons_v = b1_done_V
        gates_last_v = 0
        for qt in range(NQT):
            for fc in range(8):
                pb.emit("tensor", lambda t, qt=qt, fc=fc: t.matmul(
                    psC[:, 0:8],
                    xnfm[:, fc * 2048 + qt * 128: fc * 2048 + (qt + 1) * 128],
                    wgT[:, fc * 8:(fc + 1) * 8],
                    start=(fc == 0), stop=(fc == 7)),
                    waits=[(V, xn_done_v), (V, t_wgT), (V, psC_cons_v), (T, b1_done_T)] if fc == 0 else (),
                    inc=(T if fc == 7 else None))
            tmm = T.n
            lg = gates[:, qt * 8:(qt + 1) * 8]
            l2 = logit2[:, qt * 8:(qt + 1) * 8]
            c0 = pb.emit("vector", lambda v, lg=lg: v.tensor_copy(lg, psC[:, 0:8]),
                         waits=[(T, tmm), (V, gates_last_v), (V, max(ln_prev))], inc=V)
            psC_cons_v = c0
            c1 = pb.emit("vector", lambda v, lg=lg: v.reduce_max(stats[:, 0:1], lg, axis=AX.X),
                         waits=[(V, c0)], inc=V)
            c2 = pb.emit("vector", lambda v, lg=lg, l2=l2: v.tensor_single_scalar(l2, lg, stats[:, 0:1], ALU.is_ge),
                         waits=[(V, c1)], inc=V)
            c3 = pb.emit("vector", lambda v, lg=lg, l2=l2: v.scalar_tensor_tensor(l2, l2, -1.0e30, lg, ALU.mult, ALU.add),
                         waits=[(V, c2)], inc=V)
            c4 = pb.emit("vector", lambda v, l2=l2: v.reduce_max(stats[:, 1:2], l2, axis=AX.X),
                         waits=[(V, c3)], inc=V)
            c5 = pb.emit("vector", lambda v: v.tensor_sub(stats[:, 2:3], stats[:, 1:2], stats[:, 0:1]),
                         waits=[(V, c4)], inc=V)
            c6 = pb.emit("scalar", lambda s_: s_.activation(stats[:, 3:4], stats[:, 2:3], AF.Exp),
                         waits=[(V, c5)], inc=Sc)
            c7 = pb.emit("vector", lambda v: v.tensor_scalar_add(stats[:, 3:4], stats[:, 3:4], 1.0),
                         waits=[(Sc, c6)], inc=V)
            c8 = pb.emit("vector", lambda v: v.reciprocal(stats[:, 4:5], stats[:, 3:4]), waits=[(V, c7)], inc=V)
            c9 = pb.emit("vector", lambda v: v.tensor_scalar(stats[:, 5:6], stats[:, 4:5], -1.0, 1.0, ALU.mult, ALU.add),
                         waits=[(V, c8)], inc=V)
            ca = pb.emit("vector", lambda v, l2=l2: v.tensor_single_scalar(l2, l2, stats[:, 1:2], ALU.is_ge),
                         waits=[(V, c9)], inc=V)
            cb = pb.emit("vector", lambda v, lg=lg: v.tensor_single_scalar(lg, lg, stats[:, 0:1], ALU.is_ge),
                         waits=[(V, ca)], inc=V)
            cc = pb.emit("vector", lambda v, lg=lg: v.tensor_scalar_mul(lg, lg, stats[:, 4:5]),
                         waits=[(V, cb)], inc=V)
            cd = pb.emit("vector", lambda v, lg=lg, l2=l2: v.scalar_tensor_tensor(lg, l2, stats[:, 5:6], lg, ALU.mult, ALU.add),
                         waits=[(V, cc)], inc=V)
            ce = pb.emit("vector", lambda v, qt=qt, lg=lg: v.tensor_copy(gatesb[:, qt * 8:(qt + 1) * 8], lg),
                         waits=[(V, cd)], inc=V)
            gates_last_v = ce

        gatesT = mw[:, :]
        psE_gc = 0
        for grp in range(2):
            for k in range(8):
                qt = grp * 8 + k
                pb.emit("tensor", lambda t, qt=qt, k=k: t.transpose(
                    psE.bitcast(BF16)[0:8, k * 128:(k + 1) * 128],
                    gatesb[:, qt * 8:(qt + 1) * 8], idb[:, :]),
                    waits=[(V, gates_last_v), (V, psE_gc), (V, psEF0_cons[1])] if k == 0 else (),
                    inc=(T if k == 7 else None))
            ttr = T.n
            psE_gc = pb.emit("vector", lambda v, grp=grp: v.tensor_copy(
                gatesT[0:8, grp * 1024:(grp + 1) * 1024], psE.bitcast(BF16)[0:8, 0:1024]),
                waits=[(T, ttr)], inc=V)
        gatesT_done_v = psE_gc

        pb.barrier(skip=("gpsimd", "sync"))

        if nphases >= 2:
            CAP = 768
            pb.emit("sync", lambda s_: s_.dma_start(out=iota_sb[:, :], in_=d_iota), inc=dC, inck=16)
            t_cn = pb.emit("sync", lambda s_: s_.dma_start(out=cnst_sb[:, :], in_=d_cnst), inc=dC, inck=16)
            t_io = t_cn
            tri = cnst_sb[:, 0:128]
            allo = cnst_sb[:, 128:256]
            t_b2 = pb.sdma(gW[3], lambda g: g.dma_start(out=b2_sb[0:8, 0:1024], in_=d_b2),
                           waits=[(T, t_wgT_tr), (T, b1_done_T)])
            cd_cons_V = [0, 0]
            y_tok = [0] * NQT
            for gt in range(NQT):
                cdb = psC if gt % 2 == 0 else psD
                for dh in range(2):
                    pb.emit("tensor", lambda t, gt=gt, dh=dh, cdb=cdb: t.matmul(
                        cdb[:, dh * 512:(dh + 1) * 512],
                        gatesT[0:8, gt * 128:(gt + 1) * 128],
                        b2_sb[0:8, dh * 512:(dh + 1) * 512], start=True, stop=True),
                        waits=[(V, gatesT_done_v), (gW[3], t_b2), (V, cd_cons_V[gt % 2])] if dh == 0 else (),
                        inc=(T if dh == 1 else None))
                tmm = T.n
                cd_cons_V[gt % 2] = pb.emit("vector", lambda v, gt=gt, cdb=cdb: v.tensor_copy(
                    y_all[:, gt * 1024:(gt + 1) * 1024], cdb[:, :]),
                    waits=[(T, tmm), (V, y_tok[gt])], inc=V)
                y_tok[gt] = cd_cons_V[gt % 2]
            gatesT_mm_T = T.n
            P_ = Pl
            shift_cons_P = [0]

            hps = [psA, psB]
            hps_cons_S = [0, 0]
            w1_cons_T = [0, 0]
            w2_cons_T = [0, 0]
            xg_cons_T = 0
            pg_cons_T = 0
            osb_cons_T = 0
            ps_cons_T = 0
            ps_cons2 = [0, 0]
            psEF_c = [0, 0]
            pg_tt_T = [0] * NQT
            pos_cons = 0
            route_S = 0
            cd3_cons = [0, 0]
            psE_c = 0

            def o_slot(c):
                if c < 4:
                    return stg[:, :].bitcast(BF16)[:, c * 1024:(c + 1) * 1024]
                return scr8[:, 2048:6144].bitcast(BF16)[:, (c - 4) * 1024:(c - 3) * 1024]

            ex_idx = [0]
            for ex in (EXPERT_LIST if EXPERT_LIST is not None else range(N_EXPERTS)):
                _stg = _E_STAGES.get(ex_idx[0], 99)
                w2_tok = [[0, 0], [0, 0]]
                for half in range(2 if _stg >= 1 else 0):
                    for j in range(2):
                        w2_tok[half][j] = pb.sdma(gW[half * 2 + j], lambda g, ex=ex, half=half, j=j: g.dma_start(
                            out=w2res[:, (half * 2 + j) * 8192:(half * 2 + j + 1) * 8192].rearrange("p (c n) -> p c n", c=8),
                            in_=d_w2[ex][(half * 2 + j) * 1024:(half * 2 + j + 1) * 1024, :].rearrange(
                                "(c p) n -> p c n", p=128)),
                            waits=[(T, w2_cons_T[half])])
                _rstg = _R_STAGES.get(ex_idx[0], 99) if _R_STAGES else 99
                gsl = gates.rearrange("p (t e) -> p t e", e=8)[:, :, ex]
                _rt = stats[:, 32:48] if _rstg == 1 else route[:, 0:16]
                r0 = pb.emit("vector", lambda v, gsl=gsl, _rt=_rt: v.tensor_single_scalar(_rt, gsl, 0.0, ALU.is_gt),
                             waits=[(V, max(cd3_cons)), (V, pos_cons), (V, gates_last_v)], inc=V)
                if _rstg == 1:
                    pos_cons = r0
                    ex_idx[0] += 1
                    continue
                r1 = pb.emit("vector", lambda v: v.tensor_copy(routeb[:, 0:16], route[:, 0:16]),
                             waits=[(V, r0)], inc=V)
                if _rstg < 2:
                    pos_cons = r1
                    ex_idx[0] += 1
                    continue
                mA = pb.emit("tensor", lambda t: t.matmul(psC[:, 0:16], tri, routeb[:, 0:16],
                             start=True, stop=False),
                             waits=[(V, r1), (dC, t_cn), (V, cd3_cons[0]), (Sc, route_S)], inc=T)
                if _rstg < 3:
                    pb.emit("tensor", lambda t: t.matmul(psC[:, 0:16], tri[:, 0:16], routeb[:, 0:16],
                                 start=False, stop=True), inc=T)
                    pos_cons = r1
                    ex_idx[0] += 1
                    continue
                mB = pb.emit("tensor", lambda t: t.matmul(psD[0:1, 0:16], allo[:, 0:1], routeb[:, 0:16],
                             start=True, stop=True), waits=[(V, cd3_cons[1])], inc=T)
                sC = pb.emit("scalar", lambda s_: s_.copy(routeb[0:1, 16:32], psD[0:1, 0:16]),
                             waits=[(T, mB)], inc=Sc)
                mC = pb.emit("tensor", lambda t: t.transpose(psE.bitcast(BF16)[0:16, 0:1], routeb[0:1, 16:32],
                             idb[0:1, 0:1]), waits=[(Sc, sC), (Sc, psE_c), (Sc, psEF_c[0])], inc=T)
                sD = pb.emit("scalar", lambda s_: s_.copy(small[0:16, 96:97], psE.bitcast(BF16)[0:16, 0:1]),
                             waits=[(T, mC)], inc=Sc)
                vB = pb.emit("vector", lambda v: v.tensor_scalar_mul(routeb[0:16, 34:50], tri[0:16, 0:16],
                             small[0:16, 96:97]), waits=[(Sc, sD)], inc=V)
                mD = pb.emit("tensor", lambda t: t.matmul(psC[:, 0:16], allo[0:16, 0:128], routeb[0:16, 34:50],
                             start=False, stop=True), waits=[(V, vB)], inc=T)
                vP = pb.emit("vector", lambda v: v.tensor_copy(route[:, 16:32], psC[:, 0:16]),
                             waits=[(T, mD)], inc=V)
                v1 = pb.emit("vector", lambda v: v.tensor_scalar_add(route[:, 32:48], route[:, 16:32], 1.0),
                             waits=[(V, vP), (V, shift_cons_P[0])], inc=V)
                v2 = pb.emit("vector", lambda v: v.tensor_mul(route[:, 32:48], route[:, 32:48], route[:, 0:16]),
                             waits=[(V, v1)], inc=V)
                v3 = pb.emit("vector", lambda v: v.tensor_scalar_add(route[:, 32:48], route[:, 32:48], -1.0),
                             waits=[(V, v2)], inc=V)
                cd3_cons[0] = vP
                pos_cons = v3

                xg_tok_S = [0, 0]

                def gather_ct(ct):
                    nonlocal pg_cons_T, xg_cons_T
                    vsh = pb.emit("vector", lambda v, ct=ct: v.tensor_scalar(
                        route[:, 48:64], route[:, 32:48], float(ct * 256), None, ALU.subtract, ALU.bypass),
                        waits=[(V, pos_cons), (V, shift_cons_P[0])], inc=V)
                    for tt in range(NQT):
                        pb.emit("vector", lambda v, tt=tt: v.tensor_single_scalar(
                            stg[:, :].bitcast(BF16)[:, tt * 256:(tt + 1) * 256], iota_sb[:, 0:256],
                            route[:, 48 + tt:49 + tt], ALU.is_equal),
                            waits=[(V, vsh), (T, pg_tt_T[tt] if pg_tt_T[tt] else pg_cons_T),
                                   (T, osb_cons_T), (dC, t_io)],
                            inc=(V if tt == NQT - 1 else None))
                    pg_v = V.n
                    shift_cons_P[0] = pg_v
                    cc, sub = (0, ct) if ct < 2 else (1, 0)
                    for fc in range(8):
                        bank = hps[fc % 2]
                        for tt in range(NQT):
                            last_fc = (fc == 7)
                            pb.emit("tensor", lambda t, fc=fc, tt=tt, bank=bank: t.matmul(
                                bank[:, 0:256],
                                xn_tm[:, tt * 1024 + fc * 128: tt * 1024 + (fc + 1) * 128],
                                stg[:, :].bitcast(BF16)[:, tt * 256:(tt + 1) * 256],
                                start=(tt == 0), stop=(tt == 15)),
                                waits=[(V, pg_v), (Sc, hps_cons_S[fc % 2]), (T, gatesT_mm_T)] if tt == 0 else (),
                                inc=(T if (tt == 15 or last_fc) else None))
                            if last_fc:
                                pg_tt_T[tt] = T.n
                        tmm = T.n
                        pg_cons_T = tmm
                        if cc == 0:
                            dst = xg[:, fc * 512 + sub * 256: fc * 512 + (sub + 1) * 256]
                        else:
                            dst = xg[:, fc * 256:(fc + 1) * 256]
                        hps_cons_S[fc % 2] = pb.emit("scalar", lambda s_, dst=dst, bank=bank: s_.copy(
                            dst, bank[:, 0:256]),
                            waits=[(T, tmm), (T, xg_cons_T), (T, ps_cons_T)], inc=Sc)
                    xg_tok_S[cc] = Sc.n

                def h_stage(cc):
                    nonlocal xg_cons_T
                    clen = 512 if cc == 0 else 256
                    for j8 in range(8):
                        sl = j8 % 2
                        tw1 = pb.sdma(gV[j8 % 4], lambda g, ex=ex, j8=j8, sl=sl: g.dma_start(
                            out=w1bufs[sl].rearrange("p (c i) -> p c i", c=8),
                            in_=d_w1[ex][:, j8 * 512:(j8 + 1) * 512].rearrange("(c p) i -> p c i", p=128)),
                            waits=[(T, w1_cons_T[sl])])
                        for k in range(4):
                            ich = j8 * 4 + k
                            hb = hps[ich % 2]
                            for fc in range(8):
                                pb.emit("tensor", lambda t, sl=sl, k=k, fc=fc, hb=hb, clen=clen: t.matmul(
                                    hb[:, 0:clen],
                                    w1bufs[sl][:, fc * 512 + k * 128: fc * 512 + (k + 1) * 128],
                                    xg[:, fc * clen:(fc + 1) * clen],
                                    start=(fc == 0), stop=(fc == 7)),
                                    waits=[(gV[j8 % 4], tw1), (Sc, hps_cons_S[ich % 2]), (Sc, xg_tok_S[cc])] if fc == 0 else (),
                                    inc=(T if fc == 7 else None))
                            tmm = T.n
                            if k == 3:
                                w1_cons_T[sl] = tmm
                            xg_cons_T = tmm
                            hps_cons_S[ich % 2] = pb.emit("scalar", lambda s_, ex=ex, ich=ich, hb=hb, cc=cc, clen=clen: s_.activation(
                                h_sb[:, ich * clen:(ich + 1) * clen], hb[:, 0:clen], AF.Gelu,
                                bias=b1T[:, ex * 32 + ich: ex * 32 + ich + 1]),
                                waits=[(T, tmm)], inc=Sc)
                    return Sc.n

                def o_stage(cc, h_S):
                    nonlocal osb_cons_T
                    clen = 512 if cc == 0 else 256
                    cchs = range(4) if cc == 0 else range(2)
                    toks = []
                    for cl in cchs:
                        cglob = cl if cc == 0 else 4 + cl
                        cdb = psC if cl % 2 == 0 else psD
                        for ich in range(32):
                            half = ich // 16
                            for dh in range(2):
                                if ich == 0 and dh == 0:
                                    w = [(Sc, h_S), (V, cd3_cons[cl % 2]),
                                         (gW[0], w2_tok[0][0]), (gW[1], w2_tok[0][1]), (Sc, o_evac_S[cl % 2])]
                                elif ich == 16 and dh == 0:
                                    w = [(gW[2], w2_tok[1][0]), (gW[3], w2_tok[1][1])]
                                else:
                                    w = ()
                                pb.emit("tensor", lambda t, ich=ich, dh=dh, cl=cl, cdb=cdb, clen=clen: t.matmul(
                                    cdb[:, dh * 512:(dh + 1) * 512],
                                    h_sb[:, ich * clen + cl * 128: ich * clen + (cl + 1) * 128],
                                    w2res[:, ich * 1024 + dh * 512: ich * 1024 + (dh + 1) * 512],
                                    start=(ich == 0), stop=(ich == 31)),
                                    waits=w, inc=(T if (ich == 31 and dh == 1) else None))
                        tmm = T.n
                        if cc == 1 and cl == 1:
                            w2_cons_T[0] = tmm
                            w2_cons_T[1] = tmm
                        o_evac_S[cl % 2] = pb.emit("scalar", lambda s_, cglob=cglob, cdb=cdb: s_.copy(
                            o_slot(cglob), cdb[:, :]),
                            waits=[(T, tmm), (T, osb_cons_T)], inc=Sc)
                        toks.append(o_evac_S[cl % 2])
                    return toks

                def scatter_pass(p, cchs, o_toks):
                    nonlocal ps_cons_T, osb_cons_T, psE_c, route_S
                    ncch = len(cchs)
                    wide = ncch * 128
                    base_c = cchs[0] * 128
                    vsh = pb.emit("vector", lambda v, base_c=base_c: v.tensor_scalar(
                        route[:, 48:64], route[:, 32:48], float(base_c), None, ALU.subtract, ALU.bypass),
                        waits=[(V, pos_cons), (V, shift_cons_P[0])], inc=V)
                    def sc_build(tt):
                        par = tt % 2
                        ps_t = xg[:, par * 1024: par * 1024 + wide]
                        psT_t = xg[:, 2048 + par * 1024: 2048 + par * 1024 + wide]
                        c1 = pb.emit("vector", lambda v, tt=tt, ps_t=ps_t, wide=wide: v.tensor_single_scalar(
                            ps_t, iota_sb[:, 0:wide], route[:, 48 + tt:49 + tt], ALU.is_equal),
                            waits=[(V, vsh), (T, ps_cons2[par]), (T, xg_cons_T)], inc=V)
                        c2 = pb.emit("vector", lambda v, tt=tt, ps_t=ps_t, ex=ex: v.tensor_scalar_mul(
                            ps_t, ps_t, gates[:, tt * 8 + ex: tt * 8 + ex + 1]),
                            waits=[(V, c1)], inc=V)
                        shift_cons_P[0] = c1
                        pse = psE if par == 0 else psF
                        for k in range(ncch):
                            pb.emit("tensor", lambda t, k=k, ps_t=ps_t, pse=pse: t.transpose(
                                pse.bitcast(BF16)[:, k * 128:(k + 1) * 128],
                                ps_t[:, k * 128:(k + 1) * 128], idb[:, :]),
                                waits=[(V, c2), (Sc, psEF_c[par])] if k == 0 else (),
                                inc=(T if k == ncch - 1 else None))
                        t_tr = T.n
                        psEF_c[par] = pb.emit("scalar", lambda s_, psT_t=psT_t, wide=wide, pse=pse: s_.copy(
                            psT_t, pse.bitcast(BF16)[:, 0:wide]),
                            waits=[(T, t_tr), (T, ps_cons2[par])], inc=Sc)
                        return (tt, psT_t, psEF_c[par])

                    def sc_mm(st2):
                        nonlocal osb_cons_T
                        tt, psT_t, copS = st2
                        par = tt % 2
                        cdb = psC if par == 0 else psD
                        for k in range(ncch):
                            cglob = cchs[k]
                            for dh in range(2):
                                w = ([(Sc, copS), (V, cd3_cons[par])] + [(Sc, tok) for tok in o_toks]) if (k == 0 and dh == 0) else ()
                                pb.emit("tensor", lambda t, k=k, dh=dh, cglob=cglob, cdb=cdb, psT_t=psT_t: t.matmul(
                                    cdb[:, dh * 512:(dh + 1) * 512],
                                    psT_t[:, k * 128:(k + 1) * 128],
                                    o_slot(cglob)[:, dh * 512:(dh + 1) * 512],
                                    start=(k == 0), stop=(k == ncch - 1)),
                                    waits=w, inc=(T if (k == ncch - 1 and dh == 1) else None))
                        tmm = T.n
                        ps_cons2[par] = tmm
                        osb_cons_T = tmm
                        return (tt, tmm, cdb)

                    def sc_yadd(st3):
                        pgt, ptmm, pcdb = st3
                        cd3_cons[pgt % 2] = pb.emit("vector", lambda v, pgt=pgt, pcdb=pcdb, ptmm=ptmm: v.tensor_tensor(
                            y_all[:, pgt * 1024:(pgt + 1) * 1024], pcdb[:, :],
                            y_all[:, pgt * 1024:(pgt + 1) * 1024], ALU.add),
                            waits=[(T, ptmm), (V, y_tok[pgt])], inc=V)
                        y_tok[pgt] = cd3_cons[pgt % 2]

                    built = None
                    mmres = None
                    for tt in range(NQT):
                        b2_ = sc_build(tt)
                        if mmres is not None:
                            sc_yadd(mmres)
                            mmres = None
                        if built is not None:
                            mmres = sc_mm(built)
                        built = b2_
                    if mmres is not None:
                        sc_yadd(mmres)
                    mmres = sc_mm(built)
                    sc_yadd(mmres)
                    ps_cons_T = max(ps_cons2)

                o_evac_S = [0, 0]
                stages = _E_STAGES.get(ex_idx[0], 99)
                if stages >= 1:
                    gather_ct(0)
                    gather_ct(1)
                if stages >= 2:
                    hS0 = h_stage(0)
                if stages >= 3:
                    toks0 = o_stage(0, hS0)
                if stages >= 4:
                    scatter_pass(0, [0, 1, 2, 3], toks0)
                if stages >= 5:
                    gather_ct(2)
                    hS1 = h_stage(1)
                    toks1 = o_stage(1, hS1)
                if stages >= 6:
                    scatter_pass(1, [4, 5], toks1)
                ex_idx[0] += 1

            if not _E_STAGES:
                t_wdec = pb.sdma(gW[0], lambda g: g.dma_start(
                    out=wdec_sb.rearrange("p (c n) -> p c n", c=8),
                    in_=d_wdec.rearrange("(c p) n -> p c n", p=128)),
                    waits=[(T, T.n)])
            pb.barrier()

        if nphases >= 3:
            rb_cons = [0, 0]
            for tt in range(NQT):
                pse0 = psF if tt % 2 == 0 else psE
                for fc in range(8):
                    pb.emit("tensor", lambda t, tt=tt, fc=fc, pse0=pse0: t.transpose(
                        pse0.bitcast(BF16)[:, fc * 128:(fc + 1) * 128],
                        xn_tm[:, tt * 1024 + fc * 128: tt * 1024 + (fc + 1) * 128], idb[:, :]),
                        waits=[(V, rb_cons[tt % 2])] if fc == 0 else (), inc=(T if fc == 7 else None))
                ttr = T.n
                rb_cons[tt % 2] = pb.emit("vector", lambda v, tt=tt, pse0=pse0: v.tensor_copy(
                    xnfm.rearrange("p (c n) -> p c n", c=8)[:, :, tt * 128:(tt + 1) * 128],
                    pse0.rearrange("p (c n) -> p c n", c=8)),
                    waits=[(T, ttr)], inc=V)
            xnfm_ready_v = V.n
            ab_cons_v = [0, 0]
            for i in range(32):
                fcp, ts4 = divmod(i, 4)
                bank = psA if i % 2 == 0 else psB
                for fc in range(8):
                    pb.emit("tensor", lambda t, fc=fc, fcp=fcp, ts4=ts4, bank=bank: t.matmul(
                        bank[:, :],
                        wdec_sb[:, fc * 1024 + fcp * 128: fc * 1024 + (fcp + 1) * 128],
                        xnfm[:, fc * 2048 + ts4 * 512: fc * 2048 + (ts4 + 1) * 512],
                        start=(fc == 0), stop=(fc == 7)),
                        waits=[(gW[0], t_wdec), (V, ab_cons_v[i % 2]), (V, xnfm_ready_v)] if fc == 0 else (),
                        inc=(T if fc == 7 else None))
                tmm = T.n
                ab_cons_v[i % 2] = pb.emit("vector", lambda v, fcp=fcp, ts4=ts4, bank=bank: v.tensor_copy(
                    q_fm[:, fcp * 2048 + ts4 * 512: fcp * 2048 + (ts4 + 1) * 512], bank[:, :]),
                    waits=[(T, tmm)], inc=V)
            q_done_v = V.n
            qproj_done_T = T.n
            first_pair = [True] * NQT
            prep_done_V = 0
            keybuf_cons_T = 0
            keybuf_kt_T = [0, 0, 0, 0]
            mask_loaded = [0, 0]
            mask_cons_V = [0, 0]
            ln_slot_v = [0, 0]
            ln_prev = [0, 0]
            sc_cons = [(0, 0), (0, 0)]
            delta_cons = [0, 0]
            pt_copy_S = 0
            psF_cons = psF_cons_V
            sm_cons_S = [0, 0]
            p_cons_T = [0, 0, 0, 0]
            pt_cons_T = [0, 0]
            pbufs = [pbuf[:, 0:512], pbuf[:, 512:1024], gsc[:, 0:512], gsc[:, 512:1024]]
            scr_cons = [(0, 0)] * 4
            qt_state_v = [0] * NQT
            pending = None

            def prep_chunk(ch):
                nonlocal prep_done_V, psF_cons
                mb = ch % 2
                mask_loaded[mb] = pb.emit("sync", lambda s_, ch=ch, mb=mb: s_.dma_start(
                    out=mw[:, mb * 2048:(mb + 1) * 2048], in_=d_mask[ch]),
                    waits=[(V, mask_cons_V[mb])], inc=dM[mb], inck=16)
                for kt in range(4):
                    sl = kt % 2
                    stg_t = stg[:, sl * 1024:(sl + 1) * 1024]
                    tl = pb.emit("sync", lambda s_, ch=ch, kt=kt, stg_t=stg_t: s_.dma_start(
                        out=stg_t, in_=d_xk[ch * 512 + kt * 128: ch * 512 + (kt + 1) * 128, :]),
                        waits=[(V, ln_slot_v[sl])], inc=dX[sl], inck=16)
                    so = (kt % 2) * 16
                    t1 = pb.emit("vector", lambda v, stg_t=stg_t, so=so: v.bn_stats(stats[:, so:so+6], stg_t[:, 0:512]),
                                 waits=[(dX[sl], tl), (V, ln_prev[kt % 2])], inc=V)
                    t2 = pb.emit("vector", lambda v, stg_t=stg_t, so=so: v.bn_stats(stats[:, so+6:so+12], stg_t[:, 512:1024]),
                                 waits=[(V, t1)], inc=V)
                    t3 = pb.emit("vector", lambda v, so=so: v.bn_aggr(stats[:, so+12:so+14], stats[:, so:so+12]),
                                 waits=[(V, t2)], inc=V)
                    t4 = pb.emit("vector", lambda v, so=so: v.tensor_scalar_add(stats[:, so+14:so+15], stats[:, so+13:so+14], 1e-5),
                                 waits=[(V, t3)], inc=V)
                    t5 = pb.emit("scalar", lambda s_, so=so: s_.activation(stats[:, so+14:so+15], stats[:, so+14:so+15], AF.Ln),
                                 waits=[(V, t4)], inc=Sc)
                    t6 = pb.emit("scalar", lambda s_, so=so: s_.activation(stats[:, so+14:so+15], stats[:, so+14:so+15], AF.Exp, scale=-0.5),
                                 waits=[(Sc, t5)], inc=Sc)
                    t7 = pb.emit("vector", lambda v, stg_t=stg_t, kt=kt, so=so: v.tensor_scalar(
                        keys[:, ktm_off + kt * 1024: ktm_off + (kt + 1) * 1024], stg_t,
                        stats[:, so+12:so+13], stats[:, so+14:so+15], ALU.subtract, ALU.mult),
                        waits=[(Sc, t6), (T, keybuf_kt_T[kt] if keybuf_kt_T[kt] else keybuf_cons_T)], inc=V)
                    ln_prev[kt % 2] = t7
                    ln_slot_v[sl] = t7
                    for fc in range(8):
                        pb.emit("tensor", lambda t, fc=fc, kt=kt: t.transpose(
                            psF.bitcast(BF16)[:, fc * 128:(fc + 1) * 128],
                            keys[:, ktm_off + kt * 1024 + fc * 128: ktm_off + kt * 1024 + (fc + 1) * 128],
                            idb[:, :]),
                            waits=[(V, t7), (Sc, psF_cons)], inc=T)
                    ttr = T.n
                    psF_cons = pb.emit("scalar", lambda s_, kt=kt: s_.copy(
                        keys[:, kfm_off: kfm_off + 4096].rearrange("p (c n) -> p c n", c=8)[:, :, kt * 128:(kt + 1) * 128],
                        psF.rearrange("p (c n) -> p c n", c=8)),
                        waits=[(T, ttr)], inc=Sc)
                prep_done_V = psF_cons

            def group_scores(ch, qts, slot, gi):
                maskable = (ch // 4 == slot)
                mb = ch % 2
                out = []
                banks = [psA, psB]
                for j, qt in enumerate(qts):
                    sbank = banks[j]
                    w0 = [(Sc, prep_done_V), (V, q_done_v),
                          (V, sc_cons[j][0]), (Sc, sc_cons[j][1])]
                    for fc in range(8):
                        pb.emit("tensor", lambda t, fc=fc, qt=qt, sbank=sbank: t.matmul(
                            sbank[:, :],
                            q_fm[:, fc * 2048 + qt * 128: fc * 2048 + (qt + 1) * 128],
                            keys[:, kfm_off + fc * 512: kfm_off + (fc + 1) * 512],
                            start=(fc == 0), stop=(fc == 7)),
                            waits=w0 if fc == 0 else (), inc=(T if fc == 7 else None))
                    out.append(dict(qt=qt, j=j, t_sc=T.n))
                nonlocal keybuf_cons_T
                keybuf_cons_T = T.n
                return dict(sts=out, gi=gi, maskable=maskable, mb=mb)

            def group_softmax(grp):
                gi = grp["gi"]; maskable = grp["maskable"]; mb = grp["mb"]
                banks = [psA, psB]
                for st in grp["sts"]:
                    j, qt, t_sc = st["j"], st["qt"], st["t_sc"]
                    sbank = banks[j]
                    sub = qt % 4
                    sm_t = smb[:, j * 512:(j + 1) * 512]
                    psl = (gi % 2) * 2 + j
                    p_t = pbufs[psl]
                    scr = 32 + ((gi * 2 + j) % 4) * 8
                    scw = [(V, scr_cons[(gi * 2 + j) % 4][0]), (Sc, scr_cons[(gi * 2 + j) % 4][1])]
                    if maskable:
                        t_ma = pb.emit("vector", lambda v, sm_t=sm_t, sub=sub, mb=mb, sbank=sbank: v.scalar_tensor_tensor(
                            sm_t, sbank[:, :], 1.0,
                            mw[:, mb * 2048 + sub * 512: mb * 2048 + (sub + 1) * 512], ALU.mult, ALU.add),
                            waits=[(T, t_sc), (dM[mb], mask_loaded[mb]), (Sc, sm_cons_S[j])], inc=V)
                        mask_cons_V[mb] = t_ma
                        src = sm_t
                        t_mx = pb.emit("vector", lambda v, sm_t=sm_t, scr=scr: v.reduce_max(
                            mlbuf[:, scr:scr + 1], sm_t, axis=AX.X), waits=[(V, t_ma)] + scw, inc=V)
                        sc_v = t_ma
                    else:
                        src = sbank[:, :]
                        t_mx = pb.emit("vector", lambda v, sbank=sbank, scr=scr: v.reduce_max(
                            mlbuf[:, scr:scr + 1], sbank[:, :], axis=AX.X),
                            waits=[(T, t_sc)] + scw, inc=V)
                        sc_v = t_mx
                    mcol = mlbuf[:, qt:qt + 1]
                    lcol = mlbuf[:, 16 + qt:17 + qt]
                    fp = first_pair[qt]
                    if fp:
                        first_pair[qt] = False
                        t_m = pb.emit("vector", lambda v, mcol=mcol, scr=scr: v.tensor_copy(mcol, mlbuf[:, scr:scr + 1]),
                                      waits=[(V, t_mx), (V, qt_state_v[qt])], inc=V)
                        t_n = pb.emit("scalar", lambda s_, scr=scr, mcol=mcol: s_.mul(
                            mlbuf[:, scr + 1:scr + 2], mcol, -1.0), waits=[(V, t_m)], inc=Sc)
                        t_e = pb.emit("scalar", lambda s_, src=src, p_t=p_t, scr=scr, lcol=lcol: s_.activation(
                            p_t, src, AF.Exp, bias=mlbuf[:, scr + 1:scr + 2], accum_out=lcol),
                            waits=[(Sc, t_n), (T, p_cons_T[psl])], inc=Sc)
                        t_l = 0
                    else:
                        t_d = pb.emit("vector", lambda v, mcol=mcol, scr=scr: v.tensor_scalar(
                            mlbuf[:, scr + 2:scr + 3], mcol, mlbuf[:, scr:scr + 1], 0.0, ALU.subtract, ALU.min),
                            waits=[(V, t_mx), (V, qt_state_v[qt])], inc=V)
                        t_cm = pb.emit("vector", lambda v, mcol=mcol, scr=scr: v.tensor_single_scalar(
                            mcol, mcol, mlbuf[:, scr + 2:scr + 3], ALU.subtract), waits=[(V, t_d)], inc=V)
                        t_al = pb.emit("scalar", lambda s_, scr=scr: s_.activation(
                            mlbuf[:, scr + 3:scr + 4], mlbuf[:, scr + 2:scr + 3], AF.Exp),
                            waits=[(V, t_d)], inc=Sc)
                        t_n = pb.emit("scalar", lambda s_, mcol=mcol, scr=scr: s_.mul(
                            mlbuf[:, scr:scr + 1], mcol, -1.0),
                            waits=[(Sc, t_al), (V, t_cm)], inc=Sc)
                        t_e = pb.emit("scalar", lambda s_, src=src, p_t=p_t, scr=scr: s_.activation(
                            p_t, src, AF.Exp, bias=mlbuf[:, scr:scr + 1], accum_out=mlbuf[:, scr + 4:scr + 5]),
                            waits=[(Sc, t_n), (T, p_cons_T[psl])], inc=Sc)
                        t_l = pb.emit("vector", lambda v, lcol=lcol, scr=scr: v.tensor_scalar(
                            lcol, lcol, mlbuf[:, scr + 3:scr + 4], mlbuf[:, scr + 4:scr + 5], ALU.mult, ALU.add),
                            waits=[(Sc, t_e)], inc=V)
                    sc_cons[j] = (sc_v, t_e)
                    if maskable:
                        sm_cons_S[j] = t_e
                    st.update(fp=fp, scr=scr, p_t=p_t, psl=psl, t_e=t_e, t_l=t_l)

            def group_tailA(grp):
                nonlocal pt_copy_S
                sts = grp["sts"]
                for st in sts:
                    j, t_e, p_t = st["j"], st["t_e"], st["p_t"]
                    for kt in range(4):
                        pb.emit("tensor", lambda t, kt=kt, p_t=p_t, j=j: t.transpose(
                            psE.bitcast(BF16)[:, j * 512 + kt * 128: j * 512 + (kt + 1) * 128],
                            p_t[:, kt * 128:(kt + 1) * 128], idb[:, :]),
                            waits=[(Sc, t_e), (Sc, pt_copy_S)] if kt == 0 else (),
                            inc=(T if (kt == 3 and j == 1) else None))
                t_tr = T.n
                for st in sts:
                    p_cons_T[st["psl"]] = t_tr
                pt_copy_S = pb.emit("scalar", lambda s_: s_.copy(ptb[:, 0:1024], psE.bitcast(BF16)[:, 0:1024]),
                                    waits=[(T, t_tr), (T, pt_cons_T[0]), (T, pt_cons_T[1])], inc=Sc)
                grp["copyS"] = pt_copy_S

            def group_tailB(grp, release_kts=False):
                nonlocal keybuf_cons_T
                sts = grp["sts"]
                dbanks = [psC, psD]
                for st in sts:
                    j, qt, scr, fp = st["j"], st["qt"], st["scr"], st["fp"]
                    dbank = dbanks[j]
                    pt_t = ptb[:, j * 512:(j + 1) * 512]
                    for kt in range(4):
                        for dh in range(2):
                            rel = release_kts and j == 1 and dh == 1
                            pb.emit("tensor", lambda t, kt=kt, dh=dh, pt_t=pt_t, dbank=dbank: t.matmul(
                                dbank[:, dh * 512:(dh + 1) * 512],
                                pt_t[:, kt * 128:(kt + 1) * 128],
                                keys[:, ktm_off + kt * 1024 + dh * 512: ktm_off + kt * 1024 + (dh + 1) * 512],
                                start=(kt == 0), stop=(kt == 3)),
                                waits=[(Sc, grp["copyS"]), (V, delta_cons[j])] if (kt == 0 and dh == 0) else (),
                                inc=(T if (rel or (kt == 3 and dh == 1)) else None))
                            if rel:
                                keybuf_kt_T[kt] = T.n
                    t_dl = T.n
                    pt_cons_T[j] = t_dl
                    keybuf_cons_T = t_dl
                    ctx_t = ctx_sb[:, qt * 1024:(qt + 1) * 1024]
                    if fp:
                        t_cx = pb.emit("vector", lambda v, ctx_t=ctx_t, dbank=dbank: v.tensor_copy(ctx_t, dbank[:, :]),
                                       waits=[(T, t_dl), (V, st["t_l"]), (T, qproj_done_T)], inc=V)
                    else:
                        t_cx = pb.emit("vector", lambda v, ctx_t=ctx_t, dbank=dbank, scr=scr: v.scalar_tensor_tensor(
                            ctx_t, ctx_t, mlbuf[:, scr + 3:scr + 4], dbank[:, :], ALU.mult, ALU.add),
                            waits=[(T, t_dl), (V, st["t_l"])], inc=V)
                    delta_cons[j] = t_cx
                    scr_cons[(grp["gi"] * 2 + j) % 4] = (t_cx, st["t_e"])
                    qt_state_v[qt] = t_cx

            gi = 0
            for ch in range(NCH):
                prep_chunk(ch)
                groups = [(slot * 4 + 2 * sp, slot * 4 + 2 * sp + 1, slot)
                          for slot in range(ch // 4, 4) for sp in range(2)]
                for idx, (qa, qb, slot) in enumerate(groups):
                    grp = group_scores(ch, (qa, qb), slot, gi)
                    if pending is not None:
                        group_tailA(pending)
                    group_softmax(grp)
                    if pending is not None:
                        group_tailB(pending)
                    pending = grp
                    gi += 1
                group_tailA(pending)
                group_tailB(pending, release_kts=True)
                pending = None
            pb.barrier()

        if nphases >= 4:
            t_ri = pb.emit("vector", lambda v: v.reciprocal(mlbuf[:, 48:64], mlbuf[:, 16:32]), inc=V)
            gsc_cons_T2 = [0, 0]
            ef_cons = [0, 0]
            psEF = [psE, psF]
            gscs = [gsc[:, :], mw[:, 0:1024]]
            for qt in range(NQT):
                gb = gscs[qt % 2]
                t_s2 = pb.emit("vector", lambda v, qt=qt, gb=gb: v.tensor_scalar_mul(
                    gb, ctx_sb[:, qt * 1024:(qt + 1) * 1024], mlbuf[:, 48 + qt:49 + qt]),
                    waits=[(V, t_ri), (T, gsc_cons_T2[qt % 2])], inc=V)
                pe = psEF[qt % 2]
                for fc in range(8):
                    pb.emit("tensor", lambda t, fc=fc, pe=pe, gb=gb: t.transpose(
                        pe.bitcast(BF16)[:, fc * 128:(fc + 1) * 128],
                        gb[:, fc * 128:(fc + 1) * 128], idb[:, :]),
                        waits=[(V, t_s2), (V, ef_cons[qt % 2])] if fc == 0 else (), inc=(T if fc == 7 else None))
                gsc_cons_T2[qt % 2] = T.n
                ef_cons[qt % 2] = pb.emit("vector", lambda v, qt=qt, pe=pe: v.tensor_copy(
                    ctx_fm.rearrange("p (c n) -> p c n", c=8)[:, :, qt * 128:(qt + 1) * 128],
                    pe.rearrange("p (c n) -> p c n", c=8)),
                    waits=[(T, gsc_cons_T2[qt % 2])], inc=V)
            ctxfm_done_v = V.n

            t_bout = pb.sdma(gW[0], lambda g: g.dma_start(out=bout_sb[0:1, 0:1024], in_=d_bout),
                             waits=[(V, ctxfm_done_v)])
            wstg_cons_T = [0, 0]
            wef_cons = [ef_cons[0], ef_cons[1]]
            wstgs = [wstg, mw[:, 1024:3072]]
            for dt in range(8):
                wsl = dt % 2
                wst = wstgs[wsl]
                tw = pb.sdma(gV[dt % 4], lambda g, dt=dt, wst=wst: g.dma_start(
                    out=wst[:, 0:2048], in_=d_wout[dt * 128:(dt + 1) * 128, :]),
                    waits=[(T, wstg_cons_T[wsl]), (V, ctxfm_done_v)])
                for half in range(2):
                    pe = psEF[half]
                    for k in range(8):
                        f2c = half * 8 + k
                        pb.emit("tensor", lambda t, f2c=f2c, k=k, pe=pe, tw=tw, dt=dt, wst=wst: t.transpose(
                            pe.bitcast(BF16)[:, k * 128:(k + 1) * 128],
                            wst[:, f2c * 128:(f2c + 1) * 128], idb[:, :]),
                            waits=[(gV[dt % 4], tw), (V, wef_cons[half])] if k == 0 else (),
                            inc=(T if k == 7 else None))
                    t_tr = T.n
                    wef_cons[half] = pb.emit("vector", lambda v, dt=dt, half=half, pe=pe: v.tensor_copy(
                        wout_fm.rearrange("p (c n) -> p c n", c=16)[:, half * 8:(half + 1) * 8, dt * 128:(dt + 1) * 128],
                        pe.rearrange("p (c n) -> p c n", c=8)),
                        waits=[(T, t_tr)], inc=V)
                wstg_cons_T[wsl] = T.n
            wout_ready_v = V.n

            gtiles = [scr8[:, 2048:4096].bitcast(BF16), gsc[:, :]]
            cd_cons_S = [0, 0]
            stile_cons_V = [0, 0]
            gtile_cons_V = [0, 0]
            stg_slot_v = [0, 0]
            ost_dma = [0, 0]
            for qt in range(NQT):
                sl = qt % 2
                cd = psC if qt % 2 == 0 else psD
                for dh in range(2):
                    for fc in range(16):
                        lhs = (xnfm[:, fc * 2048 + qt * 128: fc * 2048 + (qt + 1) * 128] if fc < 8
                               else ctx_fm[:, (fc - 8) * 2048 + qt * 128: (fc - 8) * 2048 + (qt + 1) * 128])
                        pb.emit("tensor", lambda t, lhs=lhs, dh=dh, fc=fc, cd=cd: t.matmul(
                            cd[:, dh * 512:(dh + 1) * 512], lhs,
                            wout_fm[:, fc * 1024 + dh * 512: fc * 1024 + (dh + 1) * 512],
                            start=(fc == 0), stop=False),
                            waits=[(V, ctxfm_done_v), (V, wout_ready_v), (V, t_ones),
                                   (gW[0], t_bout), (Sc, cd_cons_S[qt % 2])] if (fc == 0 and dh == 0) else (),
                            inc=None)
                    pb.emit("tensor", lambda t, dh=dh, cd=cd: t.matmul(
                        cd[:, dh * 512:(dh + 1) * 512], cnst_sb[0:1, 128:256],
                        bout_sb[0:1, dh * 512:(dh + 1) * 512], start=False, stop=True),
                        inc=(T if dh == 1 else None))
                tmm = T.n
                tl = pb.emit("sync", lambda s_, qt=qt, sl=sl: s_.dma_start(
                    out=stg[:, sl * 1024:(sl + 1) * 1024], in_=d_xq[qt * 128:(qt + 1) * 128, :]),
                    waits=[(V, stg_slot_v[sl]), (dO[sl], ost_dma[sl])], inc=dX[sl], inck=16)
                t_ta = pb.emit("scalar", lambda s_, cd=cd, sl=sl: s_.activation(
                    stile_b[sl][:, 0:1024], cd[:, :], AF.Tanh),
                    waits=[(T, tmm), (V, stile_cons_V[sl])], inc=Sc)
                cd_cons_S[qt % 2] = t_ta
                t_si = pb.emit("scalar", lambda s_, qt=qt, sl=sl: s_.activation(
                    gtiles[sl][:, 0:1024], y_all[:, qt * 1024:(qt + 1) * 1024], AF.Silu),
                    waits=[(V, gtile_cons_V[sl])], inc=Sc)
                t_mu = pb.emit("vector", lambda v, sl=sl: v.tensor_mul(
                    stile_b[sl][:, 0:1024], stile_b[sl][:, 0:1024], gtiles[sl][:, 0:1024]),
                    waits=[(Sc, t_ta), (Sc, t_si)], inc=V)
                gtile_cons_V[sl] = t_mu
                t_ad = pb.emit("vector", lambda v, sl=sl: v.tensor_add(
                    stg[:, sl * 1024:(sl + 1) * 1024], stg[:, sl * 1024:(sl + 1) * 1024], stile_b[sl][:, 0:1024]),
                    waits=[(V, t_mu), (dX[sl], tl)], inc=V)
                stile_cons_V[sl] = t_ad
                ost_dma[sl] = pb.emit("sync", lambda s_, qt=qt, sl=sl: s_.dma_start(
                    out=d_y[qt * 128:(qt + 1) * 128, :], in_=stg[:, sl * 1024:(sl + 1) * 1024]),
                    waits=[(V, t_ad)], inc=dO[sl], inck=16)
                stg_slot_v[sl] = t_ad
            pb.wait("sync", (dO[0], dO[0].n), (dO[1], dO[1].n))
            pb.barrier()

        if debug:
            pb.barrier()
            def dump(name, src):
                pb.sdma(gD, lambda g, name=name, src=src: g.dma_start(out=dbg[name], in_=src))
            dump("dbg_xnfm", xn_fm[:, :])
            if nphases == 3:
                dump("dbg_qfm", q_fm[:, :])
            dump("dbg_gates", gates[:, :])
            if nphases in (3,):
                dump("dbg_ctx", ctx_sb[:, :])
                dump("dbg_ml", mlbuf[:, 0:32])
            if nphases >= 2:
                dump("dbg_yffn", y_all[:, :])

            pb.wait("gpsimd", (gD, gD.n))

        pb.build()
    return nc


def _host_inputs(inputs):
    x = np.ascontiguousarray(np.asarray(inputs["L0_attn_hidden_states"]), dtype=np.float32)
    shared = {
        "w_dec": np.ascontiguousarray(np.asarray(inputs["L0_attn_w_dec_attn"], np.float32)),
        "w_out": np.ascontiguousarray(np.asarray(inputs["L0_attn_w_out"], np.float32)),
        "b_out": np.ascontiguousarray(np.asarray(inputs["L0_attn_b_out"], np.float32)).reshape(1, D),
        "Wg": np.ascontiguousarray(np.asarray(inputs["L0_ffn_Wg"], np.float32)),
        "W1": np.ascontiguousarray(np.asarray(inputs["L0_ffn_W1"], np.float32)),
        "b1": np.ascontiguousarray(np.asarray(inputs["L0_ffn_b1"], np.float32)).reshape(8, 32, 128),
        "W2": np.ascontiguousarray(np.asarray(inputs["L0_ffn_W2"], np.float32)),
        "b2": np.ascontiguousarray(np.asarray(inputs["L0_ffn_b2"], np.float32)),
        "ident_bf": np.eye(128, dtype=ml_dtypes.bfloat16),
        "ident_f32": np.eye(128, dtype=np.float32),
        "iota768": np.broadcast_to(np.arange(512, dtype=np.float32), (128, 512)).copy(),
        "cnst_bf": np.concatenate([
            np.triu(np.ones((128, 128), dtype=np.float32), k=1),
            np.ones((128, 128), dtype=np.float32)], axis=1).astype(ml_dtypes.bfloat16),
    }
    in_maps = []
    jj = np.arange(512)
    for core in range(8):
        s, m = core // 4, core % 4
        blocks = [m, 7 - m, 8 + m, 15 - m]
        xq = np.concatenate([x[s, b * 512:(b + 1) * 512] for b in blocks], axis=0)
        masks = np.zeros((NCH, 128, 4 * 512), dtype=ml_dtypes.bfloat16)
        for ch in range(NCH):
            slot = ch // 4
            b = blocks[slot]
            for sub in range(4):
                i_glob = b * 512 + sub * 128 + np.arange(128)
                j_glob = ch * 512 + jj
                allow = j_glob[None, :] < i_glob[:, None]
                if b == 0 and sub == 0 and ch == 0:
                    allow[0, :] = False
                    allow[0, 0] = True
                masks[ch, :, sub * 512:(sub + 1) * 512] = np.where(
                    allow, np.float32(0.0), np.float32(NEG)).astype(ml_dtypes.bfloat16)
        im = dict(shared)
        im["xq"] = np.ascontiguousarray(xq)
        im["xk"] = np.ascontiguousarray(x[s])
        im["masks"] = masks
        in_maps.append(im)
    return in_maps


_nc_cache = {}

def kernel(**inputs):
    if "full" not in _nc_cache:
        _nc_cache["full"] = build_nc()
    nc = _nc_cache["full"]
    in_maps = _host_inputs(inputs)
    res = run_bass_kernel_spmd(nc, in_maps, core_ids=list(range(8)))
    out = np.empty((2, S, D), dtype=np.float32)
    for core in range(8):
        s, m = core // 4, core % 4
        blocks = [m, 7 - m, 8 + m, 15 - m]
        shard = res.results[core]["y"]
        for k, b in enumerate(blocks):
            out[s, b * 512:(b + 1) * 512] = shard[k * 512:(k + 1) * 512]
    return out


# version 18
# speedup vs baseline: 1.0044x; 1.0044x over previous
import numpy as np
import ml_dtypes
_E_STAGES = {}
_R_STAGES = {}
from contextlib import ExitStack
import concourse.bass as bass
import concourse.mybir as mybir
from concourse.bass_utils import run_bass_kernel_spmd

F32 = mybir.dt.float32
BF16 = mybir.dt.bfloat16
AF = mybir.ActivationFunctionType
ALU = mybir.AluOpType
AX = mybir.AxisListType

D = 1024
N_EXPERTS = 8
EXPERT_LIST = None
S = 8192
NCH = 16
TQ = 2048
NQT = 16
NEG = -3.0e38

ENGINES = ("sync", "scalar", "vector", "tensor", "gpsimd")


class Ctr:
    __slots__ = ("sem", "n", "name")
    def __init__(self, sem, name):
        self.sem = sem; self.n = 0; self.name = name


class PB:
    def __init__(self, nc):
        self.nc = nc
        self.q = {e: [] for e in ENGINES}
        self.waited = {e: {} for e in ENGINES}
        self.ctrs = []

    def ctr(self, stack, name):
        sem = stack.enter_context(self.nc.semaphore(name))
        c = Ctr(sem, name)
        self.ctrs.append(c)
        return c

    def emit(self, eng, fn, waits=(), inc=None, inck=1):
        w = [(c, v) for (c, v) in waits if v > 0 and v > self.waited[eng].get(c.name, 0)]
        for c, v in w:
            self.waited[eng][c.name] = v
        tok = None
        if inc is not None:
            inc.n += inck
            tok = inc.n
        def run(e, fn=fn, w=tuple(w), inc=inc, inck=inck):
            for c, v in w:
                e.wait_ge(c.sem, v)
            inst = fn(e)
            if inc is not None:
                inst.then_inc(inc.sem, inck)
        self.q[eng].append(run)
        return tok

    def wait(self, eng, *waits):
        w = [(c, v) for (c, v) in waits if v > 0 and v > self.waited[eng].get(c.name, 0)]
        for c, v in w:
            self.waited[eng][c.name] = v
        if not w:
            return
        def run(e, w=tuple(w)):
            for c, v in w:
                e.wait_ge(c.sem, v)
        self.q[eng].append(run)

    def sdma(self, ctr, fn, waits=()):
        self.wait("gpsimd", (ctr, ctr.n))
        return self.emit("gpsimd", fn, waits=waits, inc=ctr, inck=16)

    def barrier(self, skip=()):
        snap = [(c, c.n) for c in self.ctrs]
        for e in ENGINES:
            if e in skip:
                continue
            self.wait(e, *snap)

    def build(self):
        with self.nc.Block() as block:
            for eng in ENGINES:
                if not self.q[eng]:
                    continue
                def body(e, q=self.q[eng]):
                    for f in q:
                        f(e)
                getattr(block, eng)(body)


def build_nc(nphases=4, debug=False):
    nc = bass.Bass("TRN2", target_bir_lowering=False)
    pb = PB(nc)

    d_xq   = nc.dram_tensor("xq", [TQ, D], F32, kind="ExternalInput").ap()
    d_xk   = nc.dram_tensor("xk", [S, D], F32, kind="ExternalInput").ap()
    d_mask = nc.dram_tensor("masks", [NCH, 128, 4 * 512], BF16, kind="ExternalInput").ap()
    d_wdec = nc.dram_tensor("w_dec", [D, D], F32, kind="ExternalInput").ap()
    d_wout = nc.dram_tensor("w_out", [D, 2 * D], F32, kind="ExternalInput").ap()
    d_bout = nc.dram_tensor("b_out", [1, D], F32, kind="ExternalInput").ap()
    d_wg   = nc.dram_tensor("Wg", [8, D], F32, kind="ExternalInput").ap()
    d_w1   = nc.dram_tensor("W1", [8, D, 4096], F32, kind="ExternalInput").ap()
    d_b1   = nc.dram_tensor("b1", [8, 32, 128], F32, kind="ExternalInput").ap()
    d_w2   = nc.dram_tensor("W2", [8, 4096, D], F32, kind="ExternalInput").ap()
    d_b2   = nc.dram_tensor("b2", [8, D], F32, kind="ExternalInput").ap()
    d_idb  = nc.dram_tensor("ident_bf", [128, 128], BF16, kind="ExternalInput").ap()
    d_iota = nc.dram_tensor("iota768", [128, 512], F32, kind="ExternalInput").ap()
    d_cnst = nc.dram_tensor("cnst_bf", [128, 256], BF16, kind="ExternalInput").ap()
    d_idf  = nc.dram_tensor("ident_f32", [128, 128], F32, kind="ExternalInput").ap()
    d_y    = nc.dram_tensor("y", [TQ, D], F32, kind="ExternalOutput").ap()
    dbg = {}
    if debug:
        for nm, shp in [("dbg_xnfm", [128, 8 * 2048]), ("dbg_qfm", [128, 8 * 2048]),
                        ("dbg_gates", [128, 16 * 8]), ("dbg_ctx", [128, 16 * 1024]),
                        ("dbg_ml", [128, 32]), ("dbg_yffn", [128, 16 * 1024])]:
            dbg[nm] = nc.dram_tensor(nm, shp, F32, kind="ExternalOutput").ap()

    with ExitStack() as stack:
        e = stack.enter_context

        xn_fm = e(nc.sbuf_tensor("xn_fm", [128, 8 * 2048], BF16))
        s_sb  = e(nc.sbuf_tensor("s_sb", [128, 16 * 1024], BF16))
        q_h   = e(nc.sbuf_tensor("q_h", [128, 16 * 1024], BF16))
        big   = e(nc.sbuf_tensor("big", [128, 32 * 1024], BF16))
        keys  = e(nc.sbuf_tensor("keys", [128, 8 * 1024], BF16))
        stg   = e(nc.sbuf_tensor("stg", [128, 2 * 1024], F32))
        scr8  = e(nc.sbuf_tensor("scr8", [128, 8 * 1024], mybir.dt.uint8))
        mw    = e(nc.sbuf_tensor("mw", [128, 2 * 2048], BF16))
        mlbuf = e(nc.sbuf_tensor("mlbuf", [128, 64], F32))
        stats = e(nc.sbuf_tensor("stats", [128, 48], F32))
        gates = e(nc.sbuf_tensor("gates", [128, 16 * 8], F32))
        gatesb= e(nc.sbuf_tensor("gatesb", [128, 16 * 8], BF16))
        small = e(nc.sbuf_tensor("small", [128, 128], F32))
        logit2 = small[:, 0:128]
        b1T   = e(nc.sbuf_tensor("b1T", [128, 8 * 32], F32))
        wgT   = e(nc.sbuf_tensor("wgT", [128, 64], BF16))
        idb   = e(nc.sbuf_tensor("idb", [128, 128], BF16))
        gsc   = e(nc.sbuf_tensor("gsc", [128, 1024], BF16))
        iota_sb = e(nc.sbuf_tensor("iota_sb", [128, 512], F32))
        cnst_sb = e(nc.sbuf_tensor("cnst_sb", [128, 256], BF16))
        route  = small[:, 0:64]
        routeb = small[:, 64:96].bitcast(BF16)

        rA      = xn_fm
        rQ      = q_h
        xnfm    = rQ[:, :]
        xn_tm   = rA[:, :]
        q_fm    = rA[:, :]
        ctx_fm  = rA[:, :]
        h_sb    = rQ[:, :]
        w2res   = big[:, :]
        ctx_sb  = big[:, :].bitcast(F32)
        wdec_sb = big[:, 16 * 1024: 24 * 1024]
        wout_fm = big[:, 0:16 * 1024]
        y_all   = s_sb[:, :]
        w1bufs  = [keys[:, i * 4096:(i + 1) * 4096] for i in range(2)]
        xg      = mw[:, :]
        kfm_off = 0
        ktm_off = 4096
        wg_sb  = scr8[:, 0:2048].bitcast(BF16)
        idf    = scr8[:, 2048:2560].bitcast(F32)
        b1raw  = scr8[:, 2560:4608].bitcast(F32)
        gTt    = scr8[:, 4608:6656].bitcast(BF16)
        b2_sb  = scr8[:, 0:2048].bitcast(BF16)
        smb    = scr8[:, 0:4096].bitcast(F32)
        pbuf   = scr8[:, 4096:6144].bitcast(BF16)
        ptb    = scr8[:, 6144:8192].bitcast(BF16)
        wstg   = scr8[:, 0:4096].bitcast(BF16)
        bout_sb= scr8[:, 4096:6144].bitcast(BF16)
        stile_b= [scr8[:, 0:2048].bitcast(BF16), scr8[:, 6144:8192].bitcast(BF16)]
        gtile_b= None

        psA = e(nc.psum_tensor("psA", [128, 512], F32))
        psB = e(nc.psum_tensor("psB", [128, 512], F32))
        psC = e(nc.psum_tensor("psC", [128, 1024], F32))
        psD = e(nc.psum_tensor("psD", [128, 1024], F32))
        psE = e(nc.psum_tensor("psE", [128, 1024], BF16))
        psF = e(nc.psum_tensor("psF", [128, 1024], BF16))

        V = pb.ctr(stack, "V"); Sc = pb.ctr(stack, "S"); T = pb.ctr(stack, "T"); Pl = pb.ctr(stack, "P")
        dC = pb.ctr(stack, "dC")
        dX = [pb.ctr(stack, "dX0"), pb.ctr(stack, "dX1")]
        dM = [pb.ctr(stack, "dM0"), pb.ctr(stack, "dM1")]
        dO = [pb.ctr(stack, "dO0"), pb.ctr(stack, "dO1")]
        gW = [pb.ctr(stack, f"gW{i}") for i in range(4)]
        gV = [pb.ctr(stack, f"gV{i}") for i in range(4)]
        gD = pb.ctr(stack, "gD")

        psEF0_cons = [(V, 0), (V, 0)]
        pb.emit("sync", lambda s_: s_.dma_start(out=idb[:, :], in_=d_idb), inc=dC, inck=16)
        pb.emit("sync", lambda s_: s_.dma_start(out=idf[:, 0:128], in_=d_idf), inc=dC, inck=16)
        tk_id = dC.n
        t_wg = pb.sdma(gW[1], lambda g: g.dma_start(out=wg_sb[0:8, :], in_=d_wg))
        t_ones = 0

        for fc in range(8):
            pb.emit("tensor", lambda t, fc=fc: t.transpose(
                psE.bitcast(BF16)[:, fc * 8:(fc + 1) * 8],
                wg_sb[0:8, fc * 128:(fc + 1) * 128], idb[0:8, 0:8]),
                waits=[(gW[1], t_wg), (dC, tk_id)], inc=T)
        t_wgT_tr = T.n
        t_wgT = pb.emit("vector", lambda v: v.tensor_copy(wgT[:, :], psE.bitcast(BF16)[:, 0:64]),
                        waits=[(T, t_wgT_tr)], inc=V)
        psEF0_cons[1] = (V, t_wgT)

        prev_b1_tr = 0
        prev_b1_cp = 0
        for ex in range(8):
            tl = pb.emit("sync", lambda s_, ex=ex: s_.dma_start(out=b1raw[0:32, 0:128], in_=d_b1[ex]),
                         waits=[(T, prev_b1_tr), (dC, tk_id)], inc=dC, inck=16)
            prev_b1_tr = pb.emit("tensor", lambda t: t.transpose(psC[:, 0:32], b1raw[0:32, 0:128], idf[0:32, 0:32]),
                                 waits=[(dC, tl), (V, prev_b1_cp)], inc=T)
            prev_b1_cp = pb.emit("vector", lambda v, ex=ex: v.tensor_copy(b1T[:, ex * 32:(ex + 1) * 32], psC[:, 0:32]),
                                 waits=[(T, prev_b1_tr)], inc=V)
        b1_done_T = prev_b1_tr
        b1_done_V = prev_b1_cp

        ln_prev = [0, 0]
        slot_apply_v = [0, 0]
        gsc_cons_T = 0
        psF_cons_V = 0
        psCD_cons = [(V, b1_done_V), (V, 0)]
        gates_last_v = 0
        chain_pend = None

        def gates_chain(qt, tmm):
            nonlocal gates_last_v
            lg = gates[:, qt * 8:(qt + 1) * 8]
            l2 = logit2[:, qt * 8:(qt + 1) * 8]
            cdb = psC if qt % 2 == 0 else psD
            c0 = pb.emit("vector", lambda v, lg=lg, cdb=cdb: v.tensor_copy(lg, cdb[:, 0:8]),
                         waits=[(T, tmm), (V, gates_last_v)], inc=V)
            psCD_cons[qt % 2] = (V, c0)
            c1 = pb.emit("vector", lambda v, lg=lg: v.reduce_max(stats[:, 32:33], lg, axis=AX.X),
                         waits=[(V, c0)], inc=V)
            c2 = pb.emit("vector", lambda v, lg=lg, l2=l2: v.tensor_single_scalar(l2, lg, stats[:, 32:33], ALU.is_ge),
                         waits=[(V, c1)], inc=V)
            c3 = pb.emit("vector", lambda v, lg=lg, l2=l2: v.scalar_tensor_tensor(l2, l2, -1.0e30, lg, ALU.mult, ALU.add),
                         waits=[(V, c2)], inc=V)
            c4 = pb.emit("vector", lambda v, l2=l2: v.reduce_max(stats[:, 33:34], l2, axis=AX.X),
                         waits=[(V, c3)], inc=V)
            c5 = pb.emit("vector", lambda v: v.tensor_sub(stats[:, 34:35], stats[:, 33:34], stats[:, 32:33]),
                         waits=[(V, c4)], inc=V)
            c6 = pb.emit("scalar", lambda s_: s_.activation(stats[:, 35:36], stats[:, 34:35], AF.Exp),
                         waits=[(V, c5)], inc=Sc)
            c7 = pb.emit("vector", lambda v: v.tensor_scalar_add(stats[:, 35:36], stats[:, 35:36], 1.0),
                         waits=[(Sc, c6)], inc=V)
            c8 = pb.emit("vector", lambda v: v.reciprocal(stats[:, 36:37], stats[:, 35:36]), waits=[(V, c7)], inc=V)
            c9 = pb.emit("vector", lambda v: v.tensor_scalar(stats[:, 37:38], stats[:, 36:37], -1.0, 1.0, ALU.mult, ALU.add),
                         waits=[(V, c8)], inc=V)
            ca = pb.emit("vector", lambda v, l2=l2: v.tensor_single_scalar(l2, l2, stats[:, 33:34], ALU.is_ge),
                         waits=[(V, c9)], inc=V)
            cb = pb.emit("vector", lambda v, lg=lg: v.tensor_single_scalar(lg, lg, stats[:, 32:33], ALU.is_ge),
                         waits=[(V, ca)], inc=V)
            cc = pb.emit("vector", lambda v, lg=lg: v.tensor_scalar_mul(lg, lg, stats[:, 36:37]),
                         waits=[(V, cb)], inc=V)
            cd = pb.emit("vector", lambda v, lg=lg, l2=l2: v.scalar_tensor_tensor(lg, l2, stats[:, 37:38], lg, ALU.mult, ALU.add),
                         waits=[(V, cc)], inc=V)
            ce = pb.emit("vector", lambda v, qt=qt, lg=lg: v.tensor_copy(gatesb[:, qt * 8:(qt + 1) * 8], lg),
                         waits=[(V, cd)], inc=V)
            gates_last_v = ce

        for tt in range(NQT):
            sl = tt % 2
            stg_t = stg[:, sl * 1024:(sl + 1) * 1024]
            tl = pb.emit("sync", lambda s_, tt=tt, stg_t=stg_t: s_.dma_start(
                out=stg_t, in_=d_xq[tt * 128:(tt + 1) * 128, :]),
                waits=[(V, slot_apply_v[sl])], inc=dX[sl], inck=16)
            so = (tt % 2) * 16
            t1 = pb.emit("vector", lambda v, stg_t=stg_t, so=so: v.bn_stats(stats[:, so:so+6], stg_t[:, 0:512]),
                         waits=[(dX[sl], tl), (V, ln_prev[tt % 2])], inc=V)
            t2 = pb.emit("vector", lambda v, stg_t=stg_t, so=so: v.bn_stats(stats[:, so+6:so+12], stg_t[:, 512:1024]),
                         waits=[(V, t1)], inc=V)
            t3 = pb.emit("vector", lambda v, so=so: v.bn_aggr(stats[:, so+12:so+14], stats[:, so:so+12]), waits=[(V, t2)], inc=V)
            t4 = pb.emit("vector", lambda v, so=so: v.tensor_scalar_add(stats[:, so+14:so+15], stats[:, so+13:so+14], 1e-5),
                         waits=[(V, t3)], inc=V)
            t5 = pb.emit("scalar", lambda s_, so=so: s_.activation(stats[:, so+14:so+15], stats[:, so+14:so+15], AF.Ln),
                         waits=[(V, t4)], inc=Sc)
            t6 = pb.emit("scalar", lambda s_, so=so: s_.activation(stats[:, so+14:so+15], stats[:, so+14:so+15], AF.Exp, scale=-0.5),
                         waits=[(Sc, t5)], inc=Sc)
            xnt = xn_tm[:, tt * 1024:(tt + 1) * 1024]
            t7 = pb.emit("vector", lambda v, stg_t=stg_t, so=so, xnt=xnt: v.tensor_scalar(
                xnt, stg_t, stats[:, so+12:so+13], stats[:, so+14:so+15], ALU.subtract, ALU.mult),
                waits=[(Sc, t6), (T, gsc_cons_T)], inc=V)
            ln_prev[tt % 2] = t7
            slot_apply_v[sl] = t7
            pse0 = psF if tt % 2 == 0 else psE
            for fc in range(8):
                pb.emit("tensor", lambda t, fc=fc, xnt=xnt, pse0=pse0: t.transpose(
                    pse0.bitcast(BF16)[:, fc * 128:(fc + 1) * 128],
                    xnt[:, fc * 128:(fc + 1) * 128], idb[:, :]),
                    waits=[(V, t7), (dC, tk_id), psEF0_cons[tt % 2]], inc=T)
            gsc_cons_T = T.n
            psEF0_cons[tt % 2] = (Sc, pb.emit("scalar", lambda s_, tt=tt, pse0=pse0: s_.copy(
                xnfm.rearrange("p (c n) -> p c n", c=8)[:, :, tt * 128:(tt + 1) * 128],
                pse0.rearrange("p (c n) -> p c n", c=8)),
                waits=[(T, gsc_cons_T)], inc=Sc))
            cdb = psC if tt % 2 == 0 else psD
            for fc in range(8):
                pb.emit("tensor", lambda t, tt=tt, fc=fc, cdb=cdb: t.matmul(
                    cdb[:, 0:8],
                    xnfm[:, fc * 2048 + tt * 128: fc * 2048 + (tt + 1) * 128],
                    wgT[:, fc * 8:(fc + 1) * 8],
                    start=(fc == 0), stop=(fc == 7)),
                    waits=[psEF0_cons[tt % 2], (V, t_wgT), psCD_cons[tt % 2], (T, b1_done_T)] if fc == 0 else (),
                    inc=(T if fc == 7 else None))
            tmm = T.n
            if chain_pend is not None:
                gates_chain(*chain_pend)
            chain_pend = (tt, tmm)
        gates_chain(*chain_pend)
        chain_pend = None

        gatesT = mw[:, :]
        psE_gc = 0
        for grp in range(2):
            for k in range(8):
                qt = grp * 8 + k
                pb.emit("tensor", lambda t, qt=qt, k=k: t.transpose(
                    psE.bitcast(BF16)[0:8, k * 128:(k + 1) * 128],
                    gatesb[:, qt * 8:(qt + 1) * 8], idb[:, :]),
                    waits=[(V, gates_last_v), (V, psE_gc), psEF0_cons[1]] if k == 0 else (),
                    inc=(T if k == 7 else None))
            ttr = T.n
            psE_gc = pb.emit("vector", lambda v, grp=grp: v.tensor_copy(
                gatesT[0:8, grp * 1024:(grp + 1) * 1024], psE.bitcast(BF16)[0:8, 0:1024]),
                waits=[(T, ttr)], inc=V)
        gatesT_done_v = psE_gc

        pb.barrier(skip=("gpsimd", "sync"))

        if nphases >= 2:
            CAP = 768
            pb.emit("sync", lambda s_: s_.dma_start(out=iota_sb[:, :], in_=d_iota), inc=dC, inck=16)
            t_cn = pb.emit("sync", lambda s_: s_.dma_start(out=cnst_sb[:, :], in_=d_cnst), inc=dC, inck=16)
            t_io = t_cn
            tri = cnst_sb[:, 0:128]
            allo = cnst_sb[:, 128:256]
            t_b2 = pb.sdma(gW[3], lambda g: g.dma_start(out=b2_sb[0:8, 0:1024], in_=d_b2),
                           waits=[(T, t_wgT_tr), (T, b1_done_T)])
            cd_cons_V = [0, 0]
            y_tok = [0] * NQT
            for gt in range(NQT):
                cdb = psC if gt % 2 == 0 else psD
                for dh in range(2):
                    pb.emit("tensor", lambda t, gt=gt, dh=dh, cdb=cdb: t.matmul(
                        cdb[:, dh * 512:(dh + 1) * 512],
                        gatesT[0:8, gt * 128:(gt + 1) * 128],
                        b2_sb[0:8, dh * 512:(dh + 1) * 512], start=True, stop=True),
                        waits=[(V, gatesT_done_v), (gW[3], t_b2), (V, cd_cons_V[gt % 2])] if dh == 0 else (),
                        inc=(T if dh == 1 else None))
                tmm = T.n
                cd_cons_V[gt % 2] = pb.emit("vector", lambda v, gt=gt, cdb=cdb: v.tensor_copy(
                    y_all[:, gt * 1024:(gt + 1) * 1024], cdb[:, :]),
                    waits=[(T, tmm), (V, y_tok[gt])], inc=V)
                y_tok[gt] = cd_cons_V[gt % 2]
            gatesT_mm_T = T.n
            P_ = Pl
            shift_cons_P = [0]

            hps = [psA, psB]
            hps_cons_S = [0, 0]
            w1_cons_T = [0, 0]
            w2_cons_T = [0, 0]
            xg_cons_T = 0
            pg_cons_T = 0
            osb_cons_T = 0
            ps_cons_T = 0
            ps_cons2 = [0, 0]
            psEF_c = [0, 0]
            pg_tt_T = [0] * NQT
            pos_cons = 0
            route_S = 0
            cd3_cons = [0, 0]
            psE_c = 0

            def o_slot(c):
                if c < 4:
                    return stg[:, :].bitcast(BF16)[:, c * 1024:(c + 1) * 1024]
                return scr8[:, 2048:6144].bitcast(BF16)[:, (c - 4) * 1024:(c - 3) * 1024]

            ex_idx = [0]
            for ex in (EXPERT_LIST if EXPERT_LIST is not None else range(N_EXPERTS)):
                _stg = _E_STAGES.get(ex_idx[0], 99)
                w2_tok = [[0, 0], [0, 0]]
                for half in range(2 if _stg >= 1 else 0):
                    for j in range(2):
                        w2_tok[half][j] = pb.sdma(gW[half * 2 + j], lambda g, ex=ex, half=half, j=j: g.dma_start(
                            out=w2res[:, (half * 2 + j) * 8192:(half * 2 + j + 1) * 8192].rearrange("p (c n) -> p c n", c=8),
                            in_=d_w2[ex][(half * 2 + j) * 1024:(half * 2 + j + 1) * 1024, :].rearrange(
                                "(c p) n -> p c n", p=128)),
                            waits=[(T, w2_cons_T[half])])
                _rstg = _R_STAGES.get(ex_idx[0], 99) if _R_STAGES else 99
                gsl = gates.rearrange("p (t e) -> p t e", e=8)[:, :, ex]
                _rt = stats[:, 32:48] if _rstg == 1 else route[:, 0:16]
                r0 = pb.emit("vector", lambda v, gsl=gsl, _rt=_rt: v.tensor_single_scalar(_rt, gsl, 0.0, ALU.is_gt),
                             waits=[(V, max(cd3_cons)), (V, pos_cons), (V, gates_last_v)], inc=V)
                if _rstg == 1:
                    pos_cons = r0
                    ex_idx[0] += 1
                    continue
                r1 = pb.emit("vector", lambda v: v.tensor_copy(routeb[:, 0:16], route[:, 0:16]),
                             waits=[(V, r0)], inc=V)
                if _rstg < 2:
                    pos_cons = r1
                    ex_idx[0] += 1
                    continue
                mA = pb.emit("tensor", lambda t: t.matmul(psC[:, 0:16], tri, routeb[:, 0:16],
                             start=True, stop=False),
                             waits=[(V, r1), (dC, t_cn), (V, cd3_cons[0]), (Sc, route_S)], inc=T)
                if _rstg < 3:
                    pb.emit("tensor", lambda t: t.matmul(psC[:, 0:16], tri[:, 0:16], routeb[:, 0:16],
                                 start=False, stop=True), inc=T)
                    pos_cons = r1
                    ex_idx[0] += 1
                    continue
                mB = pb.emit("tensor", lambda t: t.matmul(psD[0:1, 0:16], allo[:, 0:1], routeb[:, 0:16],
                             start=True, stop=True), waits=[(V, cd3_cons[1])], inc=T)
                sC = pb.emit("scalar", lambda s_: s_.copy(routeb[0:1, 16:32], psD[0:1, 0:16]),
                             waits=[(T, mB)], inc=Sc)
                mC = pb.emit("tensor", lambda t: t.transpose(psE.bitcast(BF16)[0:16, 0:1], routeb[0:1, 16:32],
                             idb[0:1, 0:1]), waits=[(Sc, sC), (Sc, psE_c), (Sc, psEF_c[0])], inc=T)
                sD = pb.emit("scalar", lambda s_: s_.copy(small[0:16, 96:97], psE.bitcast(BF16)[0:16, 0:1]),
                             waits=[(T, mC)], inc=Sc)
                vB = pb.emit("vector", lambda v: v.tensor_scalar_mul(routeb[0:16, 34:50], tri[0:16, 0:16],
                             small[0:16, 96:97]), waits=[(Sc, sD)], inc=V)
                mD = pb.emit("tensor", lambda t: t.matmul(psC[:, 0:16], allo[0:16, 0:128], routeb[0:16, 34:50],
                             start=False, stop=True), waits=[(V, vB)], inc=T)
                vP = pb.emit("vector", lambda v: v.tensor_copy(route[:, 16:32], psC[:, 0:16]),
                             waits=[(T, mD)], inc=V)
                v1 = pb.emit("vector", lambda v: v.tensor_scalar_add(route[:, 32:48], route[:, 16:32], 1.0),
                             waits=[(V, vP), (V, shift_cons_P[0])], inc=V)
                v2 = pb.emit("vector", lambda v: v.tensor_mul(route[:, 32:48], route[:, 32:48], route[:, 0:16]),
                             waits=[(V, v1)], inc=V)
                v3 = pb.emit("vector", lambda v: v.tensor_scalar_add(route[:, 32:48], route[:, 32:48], -1.0),
                             waits=[(V, v2)], inc=V)
                cd3_cons[0] = vP
                pos_cons = v3

                xg_tok_S = [0, 0]

                def gather_ct(ct):
                    nonlocal pg_cons_T, xg_cons_T
                    vsh = pb.emit("vector", lambda v, ct=ct: v.tensor_scalar(
                        route[:, 48:64], route[:, 32:48], float(ct * 256), None, ALU.subtract, ALU.bypass),
                        waits=[(V, pos_cons), (V, shift_cons_P[0])], inc=V)
                    for tt in range(NQT):
                        pb.emit("vector", lambda v, tt=tt: v.tensor_single_scalar(
                            stg[:, :].bitcast(BF16)[:, tt * 256:(tt + 1) * 256], iota_sb[:, 0:256],
                            route[:, 48 + tt:49 + tt], ALU.is_equal),
                            waits=[(V, vsh), (T, pg_tt_T[tt] if pg_tt_T[tt] else pg_cons_T),
                                   (T, osb_cons_T), (dC, t_io)],
                            inc=(V if tt == NQT - 1 else None))
                    pg_v = V.n
                    shift_cons_P[0] = pg_v
                    cc, sub = (0, ct) if ct < 2 else (1, 0)
                    for fc in range(8):
                        bank = hps[fc % 2]
                        for tt in range(NQT):
                            last_fc = (fc == 7)
                            pb.emit("tensor", lambda t, fc=fc, tt=tt, bank=bank: t.matmul(
                                bank[:, 0:256],
                                xn_tm[:, tt * 1024 + fc * 128: tt * 1024 + (fc + 1) * 128],
                                stg[:, :].bitcast(BF16)[:, tt * 256:(tt + 1) * 256],
                                start=(tt == 0), stop=(tt == 15)),
                                waits=[(V, pg_v), (Sc, hps_cons_S[fc % 2]), (T, gatesT_mm_T)] if tt == 0 else (),
                                inc=(T if (tt == 15 or last_fc) else None))
                            if last_fc:
                                pg_tt_T[tt] = T.n
                        tmm = T.n
                        pg_cons_T = tmm
                        if cc == 0:
                            dst = xg[:, fc * 512 + sub * 256: fc * 512 + (sub + 1) * 256]
                        else:
                            dst = xg[:, fc * 256:(fc + 1) * 256]
                        hps_cons_S[fc % 2] = pb.emit("scalar", lambda s_, dst=dst, bank=bank: s_.copy(
                            dst, bank[:, 0:256]),
                            waits=[(T, tmm), (T, xg_cons_T), (T, ps_cons_T)], inc=Sc)
                    xg_tok_S[cc] = Sc.n

                def h_stage(cc):
                    nonlocal xg_cons_T
                    clen = 512 if cc == 0 else 256
                    for j8 in range(8):
                        sl = j8 % 2
                        tw1 = pb.sdma(gV[j8 % 4], lambda g, ex=ex, j8=j8, sl=sl: g.dma_start(
                            out=w1bufs[sl].rearrange("p (c i) -> p c i", c=8),
                            in_=d_w1[ex][:, j8 * 512:(j8 + 1) * 512].rearrange("(c p) i -> p c i", p=128)),
                            waits=[(T, w1_cons_T[sl])])
                        for k in range(4):
                            ich = j8 * 4 + k
                            hb = hps[ich % 2]
                            for fc in range(8):
                                pb.emit("tensor", lambda t, sl=sl, k=k, fc=fc, hb=hb, clen=clen: t.matmul(
                                    hb[:, 0:clen],
                                    w1bufs[sl][:, fc * 512 + k * 128: fc * 512 + (k + 1) * 128],
                                    xg[:, fc * clen:(fc + 1) * clen],
                                    start=(fc == 0), stop=(fc == 7)),
                                    waits=[(gV[j8 % 4], tw1), (Sc, hps_cons_S[ich % 2]), (Sc, xg_tok_S[cc])] if fc == 0 else (),
                                    inc=(T if fc == 7 else None))
                            tmm = T.n
                            if k == 3:
                                w1_cons_T[sl] = tmm
                            xg_cons_T = tmm
                            hps_cons_S[ich % 2] = pb.emit("scalar", lambda s_, ex=ex, ich=ich, hb=hb, cc=cc, clen=clen: s_.activation(
                                h_sb[:, ich * clen:(ich + 1) * clen], hb[:, 0:clen], AF.Gelu,
                                bias=b1T[:, ex * 32 + ich: ex * 32 + ich + 1]),
                                waits=[(T, tmm)], inc=Sc)
                    return Sc.n

                def o_stage(cc, h_S):
                    nonlocal osb_cons_T
                    clen = 512 if cc == 0 else 256
                    cchs = range(4) if cc == 0 else range(2)
                    toks = []
                    for cl in cchs:
                        cglob = cl if cc == 0 else 4 + cl
                        cdb = psC if cl % 2 == 0 else psD
                        for ich in range(32):
                            half = ich // 16
                            for dh in range(2):
                                if ich == 0 and dh == 0:
                                    w = [(Sc, h_S), (V, cd3_cons[cl % 2]),
                                         (gW[0], w2_tok[0][0]), (gW[1], w2_tok[0][1]), (Sc, o_evac_S[cl % 2])]
                                elif ich == 16 and dh == 0:
                                    w = [(gW[2], w2_tok[1][0]), (gW[3], w2_tok[1][1])]
                                else:
                                    w = ()
                                pb.emit("tensor", lambda t, ich=ich, dh=dh, cl=cl, cdb=cdb, clen=clen: t.matmul(
                                    cdb[:, dh * 512:(dh + 1) * 512],
                                    h_sb[:, ich * clen + cl * 128: ich * clen + (cl + 1) * 128],
                                    w2res[:, ich * 1024 + dh * 512: ich * 1024 + (dh + 1) * 512],
                                    start=(ich == 0), stop=(ich == 31)),
                                    waits=w, inc=(T if (ich == 31 and dh == 1) else None))
                        tmm = T.n
                        if cc == 1 and cl == 1:
                            w2_cons_T[0] = tmm
                            w2_cons_T[1] = tmm
                        o_evac_S[cl % 2] = pb.emit("scalar", lambda s_, cglob=cglob, cdb=cdb: s_.copy(
                            o_slot(cglob), cdb[:, :]),
                            waits=[(T, tmm), (T, osb_cons_T)], inc=Sc)
                        toks.append(o_evac_S[cl % 2])
                    return toks

                def scatter_pass(p, cchs, o_toks):
                    nonlocal ps_cons_T, osb_cons_T, psE_c, route_S
                    ncch = len(cchs)
                    wide = ncch * 128
                    base_c = cchs[0] * 128
                    vsh = pb.emit("vector", lambda v, base_c=base_c: v.tensor_scalar(
                        route[:, 48:64], route[:, 32:48], float(base_c), None, ALU.subtract, ALU.bypass),
                        waits=[(V, pos_cons), (V, shift_cons_P[0])], inc=V)
                    def sc_build(tt):
                        par = tt % 2
                        ps_t = xg[:, par * 1024: par * 1024 + wide]
                        psT_t = xg[:, 2048 + par * 1024: 2048 + par * 1024 + wide]
                        c1 = pb.emit("vector", lambda v, tt=tt, ps_t=ps_t, wide=wide: v.tensor_single_scalar(
                            ps_t, iota_sb[:, 0:wide], route[:, 48 + tt:49 + tt], ALU.is_equal),
                            waits=[(V, vsh), (T, ps_cons2[par]), (T, xg_cons_T)], inc=V)
                        c2 = pb.emit("vector", lambda v, tt=tt, ps_t=ps_t, ex=ex: v.tensor_scalar_mul(
                            ps_t, ps_t, gates[:, tt * 8 + ex: tt * 8 + ex + 1]),
                            waits=[(V, c1)], inc=V)
                        shift_cons_P[0] = c1
                        pse = psE if par == 0 else psF
                        for k in range(ncch):
                            pb.emit("tensor", lambda t, k=k, ps_t=ps_t, pse=pse: t.transpose(
                                pse.bitcast(BF16)[:, k * 128:(k + 1) * 128],
                                ps_t[:, k * 128:(k + 1) * 128], idb[:, :]),
                                waits=[(V, c2), (Sc, psEF_c[par])] if k == 0 else (),
                                inc=(T if k == ncch - 1 else None))
                        t_tr = T.n
                        psEF_c[par] = pb.emit("scalar", lambda s_, psT_t=psT_t, wide=wide, pse=pse: s_.copy(
                            psT_t, pse.bitcast(BF16)[:, 0:wide]),
                            waits=[(T, t_tr), (T, ps_cons2[par])], inc=Sc)
                        return (tt, psT_t, psEF_c[par])

                    def sc_mm(st2):
                        nonlocal osb_cons_T
                        tt, psT_t, copS = st2
                        par = tt % 2
                        cdb = psC if par == 0 else psD
                        for k in range(ncch):
                            cglob = cchs[k]
                            for dh in range(2):
                                w = ([(Sc, copS), (V, cd3_cons[par])] + [(Sc, tok) for tok in o_toks]) if (k == 0 and dh == 0) else ()
                                pb.emit("tensor", lambda t, k=k, dh=dh, cglob=cglob, cdb=cdb, psT_t=psT_t: t.matmul(
                                    cdb[:, dh * 512:(dh + 1) * 512],
                                    psT_t[:, k * 128:(k + 1) * 128],
                                    o_slot(cglob)[:, dh * 512:(dh + 1) * 512],
                                    start=(k == 0), stop=(k == ncch - 1)),
                                    waits=w, inc=(T if (k == ncch - 1 and dh == 1) else None))
                        tmm = T.n
                        ps_cons2[par] = tmm
                        osb_cons_T = tmm
                        return (tt, tmm, cdb)

                    def sc_yadd(st3):
                        pgt, ptmm, pcdb = st3
                        cd3_cons[pgt % 2] = pb.emit("vector", lambda v, pgt=pgt, pcdb=pcdb, ptmm=ptmm: v.tensor_tensor(
                            y_all[:, pgt * 1024:(pgt + 1) * 1024], pcdb[:, :],
                            y_all[:, pgt * 1024:(pgt + 1) * 1024], ALU.add),
                            waits=[(T, ptmm), (V, y_tok[pgt])], inc=V)
                        y_tok[pgt] = cd3_cons[pgt % 2]

                    built = None
                    mmres = None
                    for tt in range(NQT):
                        b2_ = sc_build(tt)
                        if mmres is not None:
                            sc_yadd(mmres)
                            mmres = None
                        if built is not None:
                            mmres = sc_mm(built)
                        built = b2_
                    if mmres is not None:
                        sc_yadd(mmres)
                    mmres = sc_mm(built)
                    sc_yadd(mmres)
                    ps_cons_T = max(ps_cons2)

                o_evac_S = [0, 0]
                stages = _E_STAGES.get(ex_idx[0], 99)
                if stages >= 1:
                    gather_ct(0)
                    gather_ct(1)
                if stages >= 2:
                    hS0 = h_stage(0)
                if stages >= 3:
                    toks0 = o_stage(0, hS0)
                if stages >= 4:
                    scatter_pass(0, [0, 1, 2, 3], toks0)
                if stages >= 5:
                    gather_ct(2)
                    hS1 = h_stage(1)
                    toks1 = o_stage(1, hS1)
                if stages >= 6:
                    scatter_pass(1, [4, 5], toks1)
                ex_idx[0] += 1

            if not _E_STAGES:
                t_wdec = pb.sdma(gW[0], lambda g: g.dma_start(
                    out=wdec_sb.rearrange("p (c n) -> p c n", c=8),
                    in_=d_wdec.rearrange("(c p) n -> p c n", p=128)),
                    waits=[(T, T.n)])
            pb.barrier()

        if nphases >= 3:
            rb_cons = [0, 0]
            for tt in range(NQT):
                pse0 = psF if tt % 2 == 0 else psE
                for fc in range(8):
                    pb.emit("tensor", lambda t, tt=tt, fc=fc, pse0=pse0: t.transpose(
                        pse0.bitcast(BF16)[:, fc * 128:(fc + 1) * 128],
                        xn_tm[:, tt * 1024 + fc * 128: tt * 1024 + (fc + 1) * 128], idb[:, :]),
                        waits=[(V, rb_cons[tt % 2])] if fc == 0 else (), inc=(T if fc == 7 else None))
                ttr = T.n
                rb_cons[tt % 2] = pb.emit("vector", lambda v, tt=tt, pse0=pse0: v.tensor_copy(
                    xnfm.rearrange("p (c n) -> p c n", c=8)[:, :, tt * 128:(tt + 1) * 128],
                    pse0.rearrange("p (c n) -> p c n", c=8)),
                    waits=[(T, ttr)], inc=V)
            xnfm_ready_v = V.n
            ab_cons_v = [0, 0]
            for i in range(32):
                fcp, ts4 = divmod(i, 4)
                bank = psA if i % 2 == 0 else psB
                for fc in range(8):
                    pb.emit("tensor", lambda t, fc=fc, fcp=fcp, ts4=ts4, bank=bank: t.matmul(
                        bank[:, :],
                        wdec_sb[:, fc * 1024 + fcp * 128: fc * 1024 + (fcp + 1) * 128],
                        xnfm[:, fc * 2048 + ts4 * 512: fc * 2048 + (ts4 + 1) * 512],
                        start=(fc == 0), stop=(fc == 7)),
                        waits=[(gW[0], t_wdec), (V, ab_cons_v[i % 2]), (V, xnfm_ready_v)] if fc == 0 else (),
                        inc=(T if fc == 7 else None))
                tmm = T.n
                ab_cons_v[i % 2] = pb.emit("vector", lambda v, fcp=fcp, ts4=ts4, bank=bank: v.tensor_copy(
                    q_fm[:, fcp * 2048 + ts4 * 512: fcp * 2048 + (ts4 + 1) * 512], bank[:, :]),
                    waits=[(T, tmm)], inc=V)
            q_done_v = V.n
            qproj_done_T = T.n
            first_pair = [True] * NQT
            prep_done_V = 0
            keybuf_cons_T = 0
            keybuf_kt_T = [0, 0, 0, 0]
            mask_loaded = [0, 0]
            mask_cons_V = [0, 0]
            ln_slot_v = [0, 0]
            ln_prev = [0, 0]
            sc_cons = [(0, 0), (0, 0)]
            delta_cons = [0, 0]
            pt_copy_S = 0
            psF_cons = psF_cons_V
            sm_cons_S = [0, 0]
            p_cons_T = [0, 0, 0, 0]
            pt_cons_T = [0, 0]
            pbufs = [pbuf[:, 0:512], pbuf[:, 512:1024], gsc[:, 0:512], gsc[:, 512:1024]]
            scr_cons = [(0, 0)] * 4
            qt_state_v = [0] * NQT
            pending = None

            def prep_chunk(ch):
                nonlocal prep_done_V, psF_cons
                mb = ch % 2
                mask_loaded[mb] = pb.emit("sync", lambda s_, ch=ch, mb=mb: s_.dma_start(
                    out=mw[:, mb * 2048:(mb + 1) * 2048], in_=d_mask[ch]),
                    waits=[(V, mask_cons_V[mb])], inc=dM[mb], inck=16)
                for kt in range(4):
                    sl = kt % 2
                    stg_t = stg[:, sl * 1024:(sl + 1) * 1024]
                    tl = pb.emit("sync", lambda s_, ch=ch, kt=kt, stg_t=stg_t: s_.dma_start(
                        out=stg_t, in_=d_xk[ch * 512 + kt * 128: ch * 512 + (kt + 1) * 128, :]),
                        waits=[(V, ln_slot_v[sl])], inc=dX[sl], inck=16)
                    so = (kt % 2) * 16
                    t1 = pb.emit("vector", lambda v, stg_t=stg_t, so=so: v.bn_stats(stats[:, so:so+6], stg_t[:, 0:512]),
                                 waits=[(dX[sl], tl), (V, ln_prev[kt % 2])], inc=V)
                    t2 = pb.emit("vector", lambda v, stg_t=stg_t, so=so: v.bn_stats(stats[:, so+6:so+12], stg_t[:, 512:1024]),
                                 waits=[(V, t1)], inc=V)
                    t3 = pb.emit("vector", lambda v, so=so: v.bn_aggr(stats[:, so+12:so+14], stats[:, so:so+12]),
                                 waits=[(V, t2)], inc=V)
                    t4 = pb.emit("vector", lambda v, so=so: v.tensor_scalar_add(stats[:, so+14:so+15], stats[:, so+13:so+14], 1e-5),
                                 waits=[(V, t3)], inc=V)
                    t5 = pb.emit("scalar", lambda s_, so=so: s_.activation(stats[:, so+14:so+15], stats[:, so+14:so+15], AF.Ln),
                                 waits=[(V, t4)], inc=Sc)
                    t6 = pb.emit("scalar", lambda s_, so=so: s_.activation(stats[:, so+14:so+15], stats[:, so+14:so+15], AF.Exp, scale=-0.5),
                                 waits=[(Sc, t5)], inc=Sc)
                    t7 = pb.emit("vector", lambda v, stg_t=stg_t, kt=kt, so=so: v.tensor_scalar(
                        keys[:, ktm_off + kt * 1024: ktm_off + (kt + 1) * 1024], stg_t,
                        stats[:, so+12:so+13], stats[:, so+14:so+15], ALU.subtract, ALU.mult),
                        waits=[(Sc, t6), (T, keybuf_kt_T[kt] if keybuf_kt_T[kt] else keybuf_cons_T)], inc=V)
                    ln_prev[kt % 2] = t7
                    ln_slot_v[sl] = t7
                    for fc in range(8):
                        pb.emit("tensor", lambda t, fc=fc, kt=kt: t.transpose(
                            psF.bitcast(BF16)[:, fc * 128:(fc + 1) * 128],
                            keys[:, ktm_off + kt * 1024 + fc * 128: ktm_off + kt * 1024 + (fc + 1) * 128],
                            idb[:, :]),
                            waits=[(V, t7), (Sc, psF_cons)], inc=T)
                    ttr = T.n
                    psF_cons = pb.emit("scalar", lambda s_, kt=kt: s_.copy(
                        keys[:, kfm_off: kfm_off + 4096].rearrange("p (c n) -> p c n", c=8)[:, :, kt * 128:(kt + 1) * 128],
                        psF.rearrange("p (c n) -> p c n", c=8)),
                        waits=[(T, ttr)], inc=Sc)
                prep_done_V = psF_cons

            def group_scores(ch, qts, slot, gi):
                maskable = (ch // 4 == slot)
                mb = ch % 2
                out = []
                banks = [psA, psB]
                for j, qt in enumerate(qts):
                    sbank = banks[j]
                    w0 = [(Sc, prep_done_V), (V, q_done_v),
                          (V, sc_cons[j][0]), (Sc, sc_cons[j][1])]
                    for fc in range(8):
                        pb.emit("tensor", lambda t, fc=fc, qt=qt, sbank=sbank: t.matmul(
                            sbank[:, :],
                            q_fm[:, fc * 2048 + qt * 128: fc * 2048 + (qt + 1) * 128],
                            keys[:, kfm_off + fc * 512: kfm_off + (fc + 1) * 512],
                            start=(fc == 0), stop=(fc == 7)),
                            waits=w0 if fc == 0 else (), inc=(T if fc == 7 else None))
                    out.append(dict(qt=qt, j=j, t_sc=T.n))
                nonlocal keybuf_cons_T
                keybuf_cons_T = T.n
                return dict(sts=out, gi=gi, maskable=maskable, mb=mb)

            def group_softmax(grp):
                gi = grp["gi"]; maskable = grp["maskable"]; mb = grp["mb"]
                banks = [psA, psB]
                for st in grp["sts"]:
                    j, qt, t_sc = st["j"], st["qt"], st["t_sc"]
                    sbank = banks[j]
                    sub = qt % 4
                    sm_t = smb[:, j * 512:(j + 1) * 512]
                    psl = (gi % 2) * 2 + j
                    p_t = pbufs[psl]
                    scr = 32 + ((gi * 2 + j) % 4) * 8
                    scw = [(V, scr_cons[(gi * 2 + j) % 4][0]), (Sc, scr_cons[(gi * 2 + j) % 4][1])]
                    if maskable:
                        t_ma = pb.emit("vector", lambda v, sm_t=sm_t, sub=sub, mb=mb, sbank=sbank: v.scalar_tensor_tensor(
                            sm_t, sbank[:, :], 1.0,
                            mw[:, mb * 2048 + sub * 512: mb * 2048 + (sub + 1) * 512], ALU.mult, ALU.add),
                            waits=[(T, t_sc), (dM[mb], mask_loaded[mb]), (Sc, sm_cons_S[j])], inc=V)
                        mask_cons_V[mb] = t_ma
                        src = sm_t
                        t_mx = pb.emit("vector", lambda v, sm_t=sm_t, scr=scr: v.reduce_max(
                            mlbuf[:, scr:scr + 1], sm_t, axis=AX.X), waits=[(V, t_ma)] + scw, inc=V)
                        sc_v = t_ma
                    else:
                        src = sbank[:, :]
                        t_mx = pb.emit("vector", lambda v, sbank=sbank, scr=scr: v.reduce_max(
                            mlbuf[:, scr:scr + 1], sbank[:, :], axis=AX.X),
                            waits=[(T, t_sc)] + scw, inc=V)
                        sc_v = t_mx
                    mcol = mlbuf[:, qt:qt + 1]
                    lcol = mlbuf[:, 16 + qt:17 + qt]
                    fp = first_pair[qt]
                    if fp:
                        first_pair[qt] = False
                        t_m = pb.emit("vector", lambda v, mcol=mcol, scr=scr: v.tensor_copy(mcol, mlbuf[:, scr:scr + 1]),
                                      waits=[(V, t_mx), (V, qt_state_v[qt])], inc=V)
                        t_n = pb.emit("scalar", lambda s_, scr=scr, mcol=mcol: s_.mul(
                            mlbuf[:, scr + 1:scr + 2], mcol, -1.0), waits=[(V, t_m)], inc=Sc)
                        t_e = pb.emit("scalar", lambda s_, src=src, p_t=p_t, scr=scr, lcol=lcol: s_.activation(
                            p_t, src, AF.Exp, bias=mlbuf[:, scr + 1:scr + 2], accum_out=lcol),
                            waits=[(Sc, t_n), (T, p_cons_T[psl])], inc=Sc)
                        t_l = 0
                    else:
                        t_d = pb.emit("vector", lambda v, mcol=mcol, scr=scr: v.tensor_scalar(
                            mlbuf[:, scr + 2:scr + 3], mcol, mlbuf[:, scr:scr + 1], 0.0, ALU.subtract, ALU.min),
                            waits=[(V, t_mx), (V, qt_state_v[qt])], inc=V)
                        t_cm = pb.emit("vector", lambda v, mcol=mcol, scr=scr: v.tensor_single_scalar(
                            mcol, mcol, mlbuf[:, scr + 2:scr + 3], ALU.subtract), waits=[(V, t_d)], inc=V)
                        t_al = pb.emit("scalar", lambda s_, scr=scr: s_.activation(
                            mlbuf[:, scr + 3:scr + 4], mlbuf[:, scr + 2:scr + 3], AF.Exp),
                            waits=[(V, t_d)], inc=Sc)
                        t_n = pb.emit("scalar", lambda s_, mcol=mcol, scr=scr: s_.mul(
                            mlbuf[:, scr:scr + 1], mcol, -1.0),
                            waits=[(Sc, t_al), (V, t_cm)], inc=Sc)
                        t_e = pb.emit("scalar", lambda s_, src=src, p_t=p_t, scr=scr: s_.activation(
                            p_t, src, AF.Exp, bias=mlbuf[:, scr:scr + 1], accum_out=mlbuf[:, scr + 4:scr + 5]),
                            waits=[(Sc, t_n), (T, p_cons_T[psl])], inc=Sc)
                        t_l = pb.emit("vector", lambda v, lcol=lcol, scr=scr: v.tensor_scalar(
                            lcol, lcol, mlbuf[:, scr + 3:scr + 4], mlbuf[:, scr + 4:scr + 5], ALU.mult, ALU.add),
                            waits=[(Sc, t_e)], inc=V)
                    sc_cons[j] = (sc_v, t_e)
                    if maskable:
                        sm_cons_S[j] = t_e
                    st.update(fp=fp, scr=scr, p_t=p_t, psl=psl, t_e=t_e, t_l=t_l)

            def group_tailA(grp):
                nonlocal pt_copy_S
                sts = grp["sts"]
                for st in sts:
                    j, t_e, p_t = st["j"], st["t_e"], st["p_t"]
                    for kt in range(4):
                        pb.emit("tensor", lambda t, kt=kt, p_t=p_t, j=j: t.transpose(
                            psE.bitcast(BF16)[:, j * 512 + kt * 128: j * 512 + (kt + 1) * 128],
                            p_t[:, kt * 128:(kt + 1) * 128], idb[:, :]),
                            waits=[(Sc, t_e), (Sc, pt_copy_S)] if kt == 0 else (),
                            inc=(T if (kt == 3 and j == 1) else None))
                t_tr = T.n
                for st in sts:
                    p_cons_T[st["psl"]] = t_tr
                pt_copy_S = pb.emit("scalar", lambda s_: s_.copy(ptb[:, 0:1024], psE.bitcast(BF16)[:, 0:1024]),
                                    waits=[(T, t_tr), (T, pt_cons_T[0]), (T, pt_cons_T[1])], inc=Sc)
                grp["copyS"] = pt_copy_S

            def group_tailB(grp, release_kts=False):
                nonlocal keybuf_cons_T
                sts = grp["sts"]
                dbanks = [psC, psD]
                for st in sts:
                    j, qt, scr, fp = st["j"], st["qt"], st["scr"], st["fp"]
                    dbank = dbanks[j]
                    pt_t = ptb[:, j * 512:(j + 1) * 512]
                    for kt in range(4):
                        for dh in range(2):
                            rel = release_kts and j == 1 and dh == 1
                            pb.emit("tensor", lambda t, kt=kt, dh=dh, pt_t=pt_t, dbank=dbank: t.matmul(
                                dbank[:, dh * 512:(dh + 1) * 512],
                                pt_t[:, kt * 128:(kt + 1) * 128],
                                keys[:, ktm_off + kt * 1024 + dh * 512: ktm_off + kt * 1024 + (dh + 1) * 512],
                                start=(kt == 0), stop=(kt == 3)),
                                waits=[(Sc, grp["copyS"]), (V, delta_cons[j])] if (kt == 0 and dh == 0) else (),
                                inc=(T if (rel or (kt == 3 and dh == 1)) else None))
                            if rel:
                                keybuf_kt_T[kt] = T.n
                    t_dl = T.n
                    pt_cons_T[j] = t_dl
                    keybuf_cons_T = t_dl
                    ctx_t = ctx_sb[:, qt * 1024:(qt + 1) * 1024]
                    if fp:
                        t_cx = pb.emit("vector", lambda v, ctx_t=ctx_t, dbank=dbank: v.tensor_copy(ctx_t, dbank[:, :]),
                                       waits=[(T, t_dl), (V, st["t_l"]), (T, qproj_done_T)], inc=V)
                    else:
                        t_cx = pb.emit("vector", lambda v, ctx_t=ctx_t, dbank=dbank, scr=scr: v.scalar_tensor_tensor(
                            ctx_t, ctx_t, mlbuf[:, scr + 3:scr + 4], dbank[:, :], ALU.mult, ALU.add),
                            waits=[(T, t_dl), (V, st["t_l"])], inc=V)
                    delta_cons[j] = t_cx
                    scr_cons[(grp["gi"] * 2 + j) % 4] = (t_cx, st["t_e"])
                    qt_state_v[qt] = t_cx

            gi = 0
            for ch in range(NCH):
                prep_chunk(ch)
                groups = [(slot * 4 + 2 * sp, slot * 4 + 2 * sp + 1, slot)
                          for slot in range(ch // 4, 4) for sp in range(2)]
                for idx, (qa, qb, slot) in enumerate(groups):
                    grp = group_scores(ch, (qa, qb), slot, gi)
                    if pending is not None:
                        group_tailA(pending)
                    group_softmax(grp)
                    if pending is not None:
                        group_tailB(pending)
                    pending = grp
                    gi += 1
                group_tailA(pending)
                group_tailB(pending, release_kts=True)
                pending = None
            pb.barrier()

        if nphases >= 4:
            t_ri = pb.emit("vector", lambda v: v.reciprocal(mlbuf[:, 48:64], mlbuf[:, 16:32]), inc=V)
            gsc_cons_T2 = [0, 0]
            ef_cons = [0, 0]
            psEF = [psE, psF]
            gscs = [gsc[:, :], mw[:, 0:1024]]
            for qt in range(NQT):
                gb = gscs[qt % 2]
                t_s2 = pb.emit("vector", lambda v, qt=qt, gb=gb: v.tensor_scalar_mul(
                    gb, ctx_sb[:, qt * 1024:(qt + 1) * 1024], mlbuf[:, 48 + qt:49 + qt]),
                    waits=[(V, t_ri), (T, gsc_cons_T2[qt % 2])], inc=V)
                pe = psEF[qt % 2]
                for fc in range(8):
                    pb.emit("tensor", lambda t, fc=fc, pe=pe, gb=gb: t.transpose(
                        pe.bitcast(BF16)[:, fc * 128:(fc + 1) * 128],
                        gb[:, fc * 128:(fc + 1) * 128], idb[:, :]),
                        waits=[(V, t_s2), (V, ef_cons[qt % 2])] if fc == 0 else (), inc=(T if fc == 7 else None))
                gsc_cons_T2[qt % 2] = T.n
                ef_cons[qt % 2] = pb.emit("vector", lambda v, qt=qt, pe=pe: v.tensor_copy(
                    ctx_fm.rearrange("p (c n) -> p c n", c=8)[:, :, qt * 128:(qt + 1) * 128],
                    pe.rearrange("p (c n) -> p c n", c=8)),
                    waits=[(T, gsc_cons_T2[qt % 2])], inc=V)
            ctxfm_done_v = V.n

            t_bout = pb.sdma(gW[0], lambda g: g.dma_start(out=bout_sb[0:1, 0:1024], in_=d_bout),
                             waits=[(V, ctxfm_done_v)])
            wstg_cons_T = [0, 0]
            wef_cons = [ef_cons[0], ef_cons[1]]
            wstgs = [wstg, mw[:, 1024:3072]]
            for dt in range(8):
                wsl = dt % 2
                wst = wstgs[wsl]
                tw = pb.sdma(gV[dt % 4], lambda g, dt=dt, wst=wst: g.dma_start(
                    out=wst[:, 0:2048], in_=d_wout[dt * 128:(dt + 1) * 128, :]),
                    waits=[(T, wstg_cons_T[wsl]), (V, ctxfm_done_v)])
                for half in range(2):
                    pe = psEF[half]
                    for k in range(8):
                        f2c = half * 8 + k
                        pb.emit("tensor", lambda t, f2c=f2c, k=k, pe=pe, tw=tw, dt=dt, wst=wst: t.transpose(
                            pe.bitcast(BF16)[:, k * 128:(k + 1) * 128],
                            wst[:, f2c * 128:(f2c + 1) * 128], idb[:, :]),
                            waits=[(gV[dt % 4], tw), (V, wef_cons[half])] if k == 0 else (),
                            inc=(T if k == 7 else None))
                    t_tr = T.n
                    wef_cons[half] = pb.emit("vector", lambda v, dt=dt, half=half, pe=pe: v.tensor_copy(
                        wout_fm.rearrange("p (c n) -> p c n", c=16)[:, half * 8:(half + 1) * 8, dt * 128:(dt + 1) * 128],
                        pe.rearrange("p (c n) -> p c n", c=8)),
                        waits=[(T, t_tr)], inc=V)
                wstg_cons_T[wsl] = T.n
            wout_ready_v = V.n

            gtiles = [scr8[:, 2048:4096].bitcast(BF16), gsc[:, :]]
            cd_cons_S = [0, 0]
            stile_cons_V = [0, 0]
            gtile_cons_V = [0, 0]
            stg_slot_v = [0, 0]
            ost_dma = [0, 0]
            for qt in range(NQT):
                sl = qt % 2
                cd = psC if qt % 2 == 0 else psD
                for dh in range(2):
                    for fc in range(16):
                        lhs = (xnfm[:, fc * 2048 + qt * 128: fc * 2048 + (qt + 1) * 128] if fc < 8
                               else ctx_fm[:, (fc - 8) * 2048 + qt * 128: (fc - 8) * 2048 + (qt + 1) * 128])
                        pb.emit("tensor", lambda t, lhs=lhs, dh=dh, fc=fc, cd=cd: t.matmul(
                            cd[:, dh * 512:(dh + 1) * 512], lhs,
                            wout_fm[:, fc * 1024 + dh * 512: fc * 1024 + (dh + 1) * 512],
                            start=(fc == 0), stop=False),
                            waits=[(V, ctxfm_done_v), (V, wout_ready_v), (V, t_ones),
                                   (gW[0], t_bout), (Sc, cd_cons_S[qt % 2])] if (fc == 0 and dh == 0) else (),
                            inc=None)
                    pb.emit("tensor", lambda t, dh=dh, cd=cd: t.matmul(
                        cd[:, dh * 512:(dh + 1) * 512], cnst_sb[0:1, 128:256],
                        bout_sb[0:1, dh * 512:(dh + 1) * 512], start=False, stop=True),
                        inc=(T if dh == 1 else None))
                tmm = T.n
                tl = pb.emit("sync", lambda s_, qt=qt, sl=sl: s_.dma_start(
                    out=stg[:, sl * 1024:(sl + 1) * 1024], in_=d_xq[qt * 128:(qt + 1) * 128, :]),
                    waits=[(V, stg_slot_v[sl]), (dO[sl], ost_dma[sl])], inc=dX[sl], inck=16)
                t_ta = pb.emit("scalar", lambda s_, cd=cd, sl=sl: s_.activation(
                    stile_b[sl][:, 0:1024], cd[:, :], AF.Tanh),
                    waits=[(T, tmm), (V, stile_cons_V[sl])], inc=Sc)
                cd_cons_S[qt % 2] = t_ta
                t_si = pb.emit("scalar", lambda s_, qt=qt, sl=sl: s_.activation(
                    gtiles[sl][:, 0:1024], y_all[:, qt * 1024:(qt + 1) * 1024], AF.Silu),
                    waits=[(V, gtile_cons_V[sl])], inc=Sc)
                t_mu = pb.emit("vector", lambda v, sl=sl: v.tensor_mul(
                    stile_b[sl][:, 0:1024], stile_b[sl][:, 0:1024], gtiles[sl][:, 0:1024]),
                    waits=[(Sc, t_ta), (Sc, t_si)], inc=V)
                gtile_cons_V[sl] = t_mu
                t_ad = pb.emit("vector", lambda v, sl=sl: v.tensor_add(
                    stg[:, sl * 1024:(sl + 1) * 1024], stg[:, sl * 1024:(sl + 1) * 1024], stile_b[sl][:, 0:1024]),
                    waits=[(V, t_mu), (dX[sl], tl)], inc=V)
                stile_cons_V[sl] = t_ad
                ost_dma[sl] = pb.emit("sync", lambda s_, qt=qt, sl=sl: s_.dma_start(
                    out=d_y[qt * 128:(qt + 1) * 128, :], in_=stg[:, sl * 1024:(sl + 1) * 1024]),
                    waits=[(V, t_ad)], inc=dO[sl], inck=16)
                stg_slot_v[sl] = t_ad
            pb.wait("sync", (dO[0], dO[0].n), (dO[1], dO[1].n))
            pb.barrier()

        if debug:
            pb.barrier()
            def dump(name, src):
                pb.sdma(gD, lambda g, name=name, src=src: g.dma_start(out=dbg[name], in_=src))
            dump("dbg_xnfm", xn_fm[:, :])
            if nphases == 3:
                dump("dbg_qfm", q_fm[:, :])
            dump("dbg_gates", gates[:, :])
            if nphases in (3,):
                dump("dbg_ctx", ctx_sb[:, :])
                dump("dbg_ml", mlbuf[:, 0:32])
            if nphases >= 2:
                dump("dbg_yffn", y_all[:, :])

            pb.wait("gpsimd", (gD, gD.n))

        pb.build()
    return nc


def _host_inputs(inputs):
    x = np.ascontiguousarray(np.asarray(inputs["L0_attn_hidden_states"]), dtype=np.float32)
    shared = {
        "w_dec": np.ascontiguousarray(np.asarray(inputs["L0_attn_w_dec_attn"], np.float32)),
        "w_out": np.ascontiguousarray(np.asarray(inputs["L0_attn_w_out"], np.float32)),
        "b_out": np.ascontiguousarray(np.asarray(inputs["L0_attn_b_out"], np.float32)).reshape(1, D),
        "Wg": np.ascontiguousarray(np.asarray(inputs["L0_ffn_Wg"], np.float32)),
        "W1": np.ascontiguousarray(np.asarray(inputs["L0_ffn_W1"], np.float32)),
        "b1": np.ascontiguousarray(np.asarray(inputs["L0_ffn_b1"], np.float32)).reshape(8, 32, 128),
        "W2": np.ascontiguousarray(np.asarray(inputs["L0_ffn_W2"], np.float32)),
        "b2": np.ascontiguousarray(np.asarray(inputs["L0_ffn_b2"], np.float32)),
        "ident_bf": np.eye(128, dtype=ml_dtypes.bfloat16),
        "ident_f32": np.eye(128, dtype=np.float32),
        "iota768": np.broadcast_to(np.arange(512, dtype=np.float32), (128, 512)).copy(),
        "cnst_bf": np.concatenate([
            np.triu(np.ones((128, 128), dtype=np.float32), k=1),
            np.ones((128, 128), dtype=np.float32)], axis=1).astype(ml_dtypes.bfloat16),
    }
    in_maps = []
    jj = np.arange(512)
    for core in range(8):
        s, m = core // 4, core % 4
        blocks = [m, 7 - m, 8 + m, 15 - m]
        xq = np.concatenate([x[s, b * 512:(b + 1) * 512] for b in blocks], axis=0)
        masks = np.zeros((NCH, 128, 4 * 512), dtype=ml_dtypes.bfloat16)
        for ch in range(NCH):
            slot = ch // 4
            b = blocks[slot]
            for sub in range(4):
                i_glob = b * 512 + sub * 128 + np.arange(128)
                j_glob = ch * 512 + jj
                allow = j_glob[None, :] < i_glob[:, None]
                if b == 0 and sub == 0 and ch == 0:
                    allow[0, :] = False
                    allow[0, 0] = True
                masks[ch, :, sub * 512:(sub + 1) * 512] = np.where(
                    allow, np.float32(0.0), np.float32(NEG)).astype(ml_dtypes.bfloat16)
        im = dict(shared)
        im["xq"] = np.ascontiguousarray(xq)
        im["xk"] = np.ascontiguousarray(x[s])
        im["masks"] = masks
        in_maps.append(im)
    return in_maps


_nc_cache = {}

def kernel(**inputs):
    if "full" not in _nc_cache:
        _nc_cache["full"] = build_nc()
    nc = _nc_cache["full"]
    in_maps = _host_inputs(inputs)
    res = run_bass_kernel_spmd(nc, in_maps, core_ids=list(range(8)))
    out = np.empty((2, S, D), dtype=np.float32)
    for core in range(8):
        s, m = core // 4, core % 4
        blocks = [m, 7 - m, 8 + m, 15 - m]
        shard = res.results[core]["y"]
        for k, b in enumerate(blocks):
            out[s, b * 512:(b + 1) * 512] = shard[k * 512:(k + 1) * 512]
    return out


# version 19
# speedup vs baseline: 1.0185x; 1.0141x over previous
import numpy as np
import ml_dtypes
_E_STAGES = {}
_R_STAGES = {}
from contextlib import ExitStack
import concourse.bass as bass
import concourse.mybir as mybir
from concourse.bass_utils import run_bass_kernel_spmd

F32 = mybir.dt.float32
BF16 = mybir.dt.bfloat16
AF = mybir.ActivationFunctionType
ALU = mybir.AluOpType
AX = mybir.AxisListType

D = 1024
N_EXPERTS = 8
EXPERT_LIST = None
S = 8192
NCH = 16
TQ = 2048
NQT = 16
NEG = -3.0e38

ENGINES = ("sync", "scalar", "vector", "tensor", "gpsimd")


class Ctr:
    __slots__ = ("sem", "n", "name")
    def __init__(self, sem, name):
        self.sem = sem; self.n = 0; self.name = name


class PB:
    def __init__(self, nc):
        self.nc = nc
        self.q = {e: [] for e in ENGINES}
        self.waited = {e: {} for e in ENGINES}
        self.ctrs = []

    def ctr(self, stack, name):
        sem = stack.enter_context(self.nc.semaphore(name))
        c = Ctr(sem, name)
        self.ctrs.append(c)
        return c

    def emit(self, eng, fn, waits=(), inc=None, inck=1):
        w = [(c, v) for (c, v) in waits if v > 0 and v > self.waited[eng].get(c.name, 0)]
        for c, v in w:
            self.waited[eng][c.name] = v
        tok = None
        if inc is not None:
            inc.n += inck
            tok = inc.n
        def run(e, fn=fn, w=tuple(w), inc=inc, inck=inck):
            for c, v in w:
                e.wait_ge(c.sem, v)
            inst = fn(e)
            if inc is not None:
                inst.then_inc(inc.sem, inck)
        self.q[eng].append(run)
        return tok

    def wait(self, eng, *waits):
        w = [(c, v) for (c, v) in waits if v > 0 and v > self.waited[eng].get(c.name, 0)]
        for c, v in w:
            self.waited[eng][c.name] = v
        if not w:
            return
        def run(e, w=tuple(w)):
            for c, v in w:
                e.wait_ge(c.sem, v)
        self.q[eng].append(run)

    def sdma(self, ctr, fn, waits=()):
        self.wait("gpsimd", (ctr, ctr.n))
        return self.emit("gpsimd", fn, waits=waits, inc=ctr, inck=16)

    def barrier(self, skip=()):
        snap = [(c, c.n) for c in self.ctrs]
        for e in ENGINES:
            if e in skip:
                continue
            self.wait(e, *snap)

    def build(self):
        with self.nc.Block() as block:
            for eng in ENGINES:
                if not self.q[eng]:
                    continue
                def body(e, q=self.q[eng]):
                    for f in q:
                        f(e)
                getattr(block, eng)(body)


def build_nc(nphases=4, debug=False):
    nc = bass.Bass("TRN2", target_bir_lowering=False)
    pb = PB(nc)

    d_xq   = nc.dram_tensor("xq", [TQ, D], F32, kind="ExternalInput").ap()
    d_xk   = nc.dram_tensor("xk", [S, D], F32, kind="ExternalInput").ap()
    d_mask = nc.dram_tensor("masks", [NCH, 128, 4 * 512], BF16, kind="ExternalInput").ap()
    d_wdec = nc.dram_tensor("w_dec", [D, D], F32, kind="ExternalInput").ap()
    d_wout = nc.dram_tensor("w_out", [D, 2 * D], F32, kind="ExternalInput").ap()
    d_bout = nc.dram_tensor("b_out", [1, D], F32, kind="ExternalInput").ap()
    d_wg   = nc.dram_tensor("Wg", [8, D], F32, kind="ExternalInput").ap()
    d_w1   = nc.dram_tensor("W1", [8, D, 4096], F32, kind="ExternalInput").ap()
    d_b1   = nc.dram_tensor("b1", [8, 32, 128], F32, kind="ExternalInput").ap()
    d_w2   = nc.dram_tensor("W2", [8, 4096, D], F32, kind="ExternalInput").ap()
    d_b2   = nc.dram_tensor("b2", [8, D], F32, kind="ExternalInput").ap()
    d_idb  = nc.dram_tensor("ident_bf", [128, 128], BF16, kind="ExternalInput").ap()
    d_iota = nc.dram_tensor("iota768", [128, 512], F32, kind="ExternalInput").ap()
    d_cnst = nc.dram_tensor("cnst_bf", [128, 256], BF16, kind="ExternalInput").ap()
    d_idf  = nc.dram_tensor("ident_f32", [128, 128], F32, kind="ExternalInput").ap()
    d_y    = nc.dram_tensor("y", [TQ, D], F32, kind="ExternalOutput").ap()
    dbg = {}
    if debug:
        for nm, shp in [("dbg_xnfm", [128, 8 * 2048]), ("dbg_qfm", [128, 8 * 2048]),
                        ("dbg_gates", [128, 16 * 8]), ("dbg_ctx", [128, 16 * 1024]),
                        ("dbg_ml", [128, 32]), ("dbg_yffn", [128, 16 * 1024])]:
            dbg[nm] = nc.dram_tensor(nm, shp, F32, kind="ExternalOutput").ap()

    with ExitStack() as stack:
        e = stack.enter_context

        xn_fm = e(nc.sbuf_tensor("xn_fm", [128, 8 * 2048], BF16))
        s_sb  = e(nc.sbuf_tensor("s_sb", [128, 16 * 1024], BF16))
        q_h   = e(nc.sbuf_tensor("q_h", [128, 16 * 1024], BF16))
        big   = e(nc.sbuf_tensor("big", [128, 32 * 1024], BF16))
        keys  = e(nc.sbuf_tensor("keys", [128, 8 * 1024], BF16))
        stg   = e(nc.sbuf_tensor("stg", [128, 2 * 1024], F32))
        scr8  = e(nc.sbuf_tensor("scr8", [128, 8 * 1024], mybir.dt.uint8))
        mw    = e(nc.sbuf_tensor("mw", [128, 2 * 2048], BF16))
        mlbuf = e(nc.sbuf_tensor("mlbuf", [128, 64], F32))
        stats = e(nc.sbuf_tensor("stats", [128, 48], F32))
        gates = e(nc.sbuf_tensor("gates", [128, 16 * 8], F32))
        gatesb= e(nc.sbuf_tensor("gatesb", [128, 16 * 8], BF16))
        small = e(nc.sbuf_tensor("small", [128, 128], F32))
        logit2 = small[:, 0:128]
        b1T   = e(nc.sbuf_tensor("b1T", [128, 8 * 32], F32))
        wgT   = e(nc.sbuf_tensor("wgT", [128, 64], BF16))
        idb   = e(nc.sbuf_tensor("idb", [128, 128], BF16))
        gsc   = e(nc.sbuf_tensor("gsc", [128, 1024], BF16))
        iota_sb = e(nc.sbuf_tensor("iota_sb", [128, 512], F32))
        cnst_sb = e(nc.sbuf_tensor("cnst_sb", [128, 256], BF16))
        route  = small[:, 0:64]
        routeb = small[:, 64:96].bitcast(BF16)

        rA      = xn_fm
        rQ      = q_h
        xnfm    = rQ[:, :]
        xn_tm   = rA[:, :]
        q_fm    = rA[:, :]
        ctx_fm  = rA[:, :]
        h_sb    = rQ[:, :]
        w2res   = big[:, :]
        ctx_sb  = big[:, :].bitcast(F32)
        wdec_sb = big[:, 16 * 1024: 24 * 1024]
        wout_fm = big[:, 0:16 * 1024]
        y_all   = s_sb[:, :]
        w1bufs  = [keys[:, i * 4096:(i + 1) * 4096] for i in range(2)]
        xg      = mw[:, :]
        kfm_off = 0
        ktm_off = 4096
        wg_sb  = scr8[:, 0:2048].bitcast(BF16)
        idf    = scr8[:, 2048:2560].bitcast(F32)
        b1raw  = scr8[:, 2560:4608].bitcast(F32)
        gTt    = scr8[:, 4608:6656].bitcast(BF16)
        b2_sb  = scr8[:, 0:2048].bitcast(BF16)
        smb    = scr8[:, 0:4096].bitcast(F32)
        pbuf   = scr8[:, 4096:6144].bitcast(BF16)
        ptb    = scr8[:, 6144:8192].bitcast(BF16)
        wstg   = scr8[:, 0:4096].bitcast(BF16)
        bout_sb= scr8[:, 4096:6144].bitcast(BF16)
        stile_b= [scr8[:, 0:2048].bitcast(BF16), scr8[:, 6144:8192].bitcast(BF16)]
        gtile_b= None

        psA = e(nc.psum_tensor("psA", [128, 512], F32))
        psB = e(nc.psum_tensor("psB", [128, 512], F32))
        psC = e(nc.psum_tensor("psC", [128, 1024], F32))
        psD = e(nc.psum_tensor("psD", [128, 1024], F32))
        psE = e(nc.psum_tensor("psE", [128, 1024], BF16))
        psF = e(nc.psum_tensor("psF", [128, 1024], BF16))

        V = pb.ctr(stack, "V"); Sc = pb.ctr(stack, "S"); T = pb.ctr(stack, "T"); Pl = pb.ctr(stack, "P")
        dC = pb.ctr(stack, "dC")
        dX = [pb.ctr(stack, "dX0"), pb.ctr(stack, "dX1")]
        dM = [pb.ctr(stack, "dM0"), pb.ctr(stack, "dM1")]
        dO = [pb.ctr(stack, "dO0"), pb.ctr(stack, "dO1")]
        gW = [pb.ctr(stack, f"gW{i}") for i in range(4)]
        gV = [pb.ctr(stack, f"gV{i}") for i in range(4)]
        gD = pb.ctr(stack, "gD")

        psEF0_cons = [(V, 0), (V, 0)]
        pb.emit("sync", lambda s_: s_.dma_start(out=idb[:, :], in_=d_idb), inc=dC, inck=16)
        pb.emit("sync", lambda s_: s_.dma_start(out=idf[:, 0:128], in_=d_idf), inc=dC, inck=16)
        tk_id = dC.n
        t_wg = pb.sdma(gW[1], lambda g: g.dma_start(out=wg_sb[0:8, :], in_=d_wg))
        t_ones = 0

        for fc in range(8):
            pb.emit("tensor", lambda t, fc=fc: t.transpose(
                psE.bitcast(BF16)[:, fc * 8:(fc + 1) * 8],
                wg_sb[0:8, fc * 128:(fc + 1) * 128], idb[0:8, 0:8]),
                waits=[(gW[1], t_wg), (dC, tk_id)], inc=T)
        t_wgT_tr = T.n
        t_wgT = pb.emit("scalar", lambda s_: s_.copy(wgT[:, :], psE.bitcast(BF16)[:, 0:64]),
                        waits=[(T, t_wgT_tr)], inc=Sc)
        psEF0_cons[1] = (Sc, t_wgT)

        prev_b1_tr = 0
        prev_b1_cp = 0
        for ex in range(8):
            tl = pb.emit("sync", lambda s_, ex=ex: s_.dma_start(out=b1raw[0:32, 0:128], in_=d_b1[ex]),
                         waits=[(T, prev_b1_tr), (dC, tk_id)], inc=dC, inck=16)
            prev_b1_tr = pb.emit("tensor", lambda t: t.transpose(psC[:, 0:32], b1raw[0:32, 0:128], idf[0:32, 0:32]),
                                 waits=[(dC, tl), (Sc, prev_b1_cp)], inc=T)
            prev_b1_cp = pb.emit("scalar", lambda s_, ex=ex: s_.copy(b1T[:, ex * 32:(ex + 1) * 32], psC[:, 0:32]),
                                 waits=[(T, prev_b1_tr)], inc=Sc)
        b1_done_T = prev_b1_tr
        b1_done_V = prev_b1_cp

        ln_prev = [0, 0]
        slot_apply_v = [0, 0]
        gsc_cons_T = 0
        psF_cons_V = 0
        psCD_cons = [(Sc, b1_done_V), (V, 0)]
        gates_last_v = 0
        chain_pend = None

        def gates_chain(qt, tmm):
            nonlocal gates_last_v
            lg = gates[:, qt * 8:(qt + 1) * 8]
            l2 = logit2[:, qt * 8:(qt + 1) * 8]
            cdb = psC if qt % 2 == 0 else psD
            c0 = pb.emit("vector", lambda v, lg=lg, cdb=cdb: v.tensor_copy(lg, cdb[:, 0:8]),
                         waits=[(T, tmm), (V, gates_last_v)], inc=V)
            psCD_cons[qt % 2] = (V, c0)
            c1 = pb.emit("vector", lambda v, lg=lg: v.reduce_max(stats[:, 32:33], lg, axis=AX.X),
                         waits=[(V, c0)], inc=V)
            c2 = pb.emit("vector", lambda v, lg=lg, l2=l2: v.tensor_single_scalar(l2, lg, stats[:, 32:33], ALU.is_ge),
                         waits=[(V, c1)], inc=V)
            c3 = pb.emit("vector", lambda v, lg=lg, l2=l2: v.scalar_tensor_tensor(l2, l2, -1.0e30, lg, ALU.mult, ALU.add),
                         waits=[(V, c2)], inc=V)
            c4 = pb.emit("vector", lambda v, l2=l2: v.reduce_max(stats[:, 33:34], l2, axis=AX.X),
                         waits=[(V, c3)], inc=V)
            c5 = pb.emit("vector", lambda v: v.tensor_sub(stats[:, 34:35], stats[:, 33:34], stats[:, 32:33]),
                         waits=[(V, c4)], inc=V)
            c6 = pb.emit("scalar", lambda s_: s_.activation(stats[:, 35:36], stats[:, 34:35], AF.Exp),
                         waits=[(V, c5)], inc=Sc)
            c7 = pb.emit("vector", lambda v: v.tensor_scalar_add(stats[:, 35:36], stats[:, 35:36], 1.0),
                         waits=[(Sc, c6)], inc=V)
            c8 = pb.emit("vector", lambda v: v.reciprocal(stats[:, 36:37], stats[:, 35:36]), waits=[(V, c7)], inc=V)
            c9 = pb.emit("vector", lambda v: v.tensor_scalar(stats[:, 37:38], stats[:, 36:37], -1.0, 1.0, ALU.mult, ALU.add),
                         waits=[(V, c8)], inc=V)
            ca = pb.emit("vector", lambda v, l2=l2: v.tensor_single_scalar(l2, l2, stats[:, 33:34], ALU.is_ge),
                         waits=[(V, c9)], inc=V)
            cb = pb.emit("vector", lambda v, lg=lg: v.tensor_single_scalar(lg, lg, stats[:, 32:33], ALU.is_ge),
                         waits=[(V, ca)], inc=V)
            cc = pb.emit("vector", lambda v, lg=lg: v.tensor_scalar_mul(lg, lg, stats[:, 36:37]),
                         waits=[(V, cb)], inc=V)
            cd = pb.emit("vector", lambda v, lg=lg, l2=l2: v.scalar_tensor_tensor(lg, l2, stats[:, 37:38], lg, ALU.mult, ALU.add),
                         waits=[(V, cc)], inc=V)
            ce = pb.emit("vector", lambda v, qt=qt, lg=lg: v.tensor_copy(gatesb[:, qt * 8:(qt + 1) * 8], lg),
                         waits=[(V, cd)], inc=V)
            gates_last_v = ce

        for tt in range(NQT):
            sl = tt % 2
            stg_t = stg[:, sl * 1024:(sl + 1) * 1024]
            tl = pb.emit("sync", lambda s_, tt=tt, stg_t=stg_t: s_.dma_start(
                out=stg_t, in_=d_xq[tt * 128:(tt + 1) * 128, :]),
                waits=[(V, slot_apply_v[sl])], inc=dX[sl], inck=16)
            so = (tt % 2) * 16
            t1 = pb.emit("vector", lambda v, stg_t=stg_t, so=so: v.bn_stats(stats[:, so:so+6], stg_t[:, 0:512]),
                         waits=[(dX[sl], tl), (V, ln_prev[tt % 2])], inc=V)
            t2 = pb.emit("vector", lambda v, stg_t=stg_t, so=so: v.bn_stats(stats[:, so+6:so+12], stg_t[:, 512:1024]),
                         waits=[(V, t1)], inc=V)
            t3 = pb.emit("vector", lambda v, so=so: v.bn_aggr(stats[:, so+12:so+14], stats[:, so:so+12]), waits=[(V, t2)], inc=V)
            t4 = pb.emit("vector", lambda v, so=so: v.tensor_scalar_add(stats[:, so+14:so+15], stats[:, so+13:so+14], 1e-5),
                         waits=[(V, t3)], inc=V)
            t5 = pb.emit("scalar", lambda s_, so=so: s_.activation(stats[:, so+14:so+15], stats[:, so+14:so+15], AF.Ln),
                         waits=[(V, t4)], inc=Sc)
            t6 = pb.emit("scalar", lambda s_, so=so: s_.activation(stats[:, so+14:so+15], stats[:, so+14:so+15], AF.Exp, scale=-0.5),
                         waits=[(Sc, t5)], inc=Sc)
            xnt = xn_tm[:, tt * 1024:(tt + 1) * 1024]
            t7 = pb.emit("vector", lambda v, stg_t=stg_t, so=so, xnt=xnt: v.tensor_scalar(
                xnt, stg_t, stats[:, so+12:so+13], stats[:, so+14:so+15], ALU.subtract, ALU.mult),
                waits=[(Sc, t6), (T, gsc_cons_T)], inc=V)
            ln_prev[tt % 2] = t7
            slot_apply_v[sl] = t7
            pse0 = psF if tt % 2 == 0 else psE
            for fc in range(8):
                pb.emit("tensor", lambda t, fc=fc, xnt=xnt, pse0=pse0: t.transpose(
                    pse0.bitcast(BF16)[:, fc * 128:(fc + 1) * 128],
                    xnt[:, fc * 128:(fc + 1) * 128], idb[:, :]),
                    waits=[(V, t7), (dC, tk_id), psEF0_cons[tt % 2]], inc=T)
            gsc_cons_T = T.n
            psEF0_cons[tt % 2] = (Sc, pb.emit("scalar", lambda s_, tt=tt, pse0=pse0: s_.copy(
                xnfm.rearrange("p (c n) -> p c n", c=8)[:, :, tt * 128:(tt + 1) * 128],
                pse0.rearrange("p (c n) -> p c n", c=8)),
                waits=[(T, gsc_cons_T)], inc=Sc))
            cdb = psC if tt % 2 == 0 else psD
            for fc in range(8):
                pb.emit("tensor", lambda t, tt=tt, fc=fc, cdb=cdb: t.matmul(
                    cdb[:, 0:8],
                    xnfm[:, fc * 2048 + tt * 128: fc * 2048 + (tt + 1) * 128],
                    wgT[:, fc * 8:(fc + 1) * 8],
                    start=(fc == 0), stop=(fc == 7)),
                    waits=[psEF0_cons[tt % 2], (Sc, t_wgT), psCD_cons[tt % 2], (T, b1_done_T)] if fc == 0 else (),
                    inc=(T if fc == 7 else None))
            tmm = T.n
            if chain_pend is not None:
                gates_chain(*chain_pend)
            chain_pend = (tt, tmm)
        gates_chain(*chain_pend)
        chain_pend = None

        gatesT = mw[:, :]
        psE_gc = 0
        for grp in range(2):
            for k in range(8):
                qt = grp * 8 + k
                pb.emit("tensor", lambda t, qt=qt, k=k: t.transpose(
                    psE.bitcast(BF16)[0:8, k * 128:(k + 1) * 128],
                    gatesb[:, qt * 8:(qt + 1) * 8], idb[:, :]),
                    waits=[(V, gates_last_v), (V, psE_gc), psEF0_cons[1]] if k == 0 else (),
                    inc=(T if k == 7 else None))
            ttr = T.n
            psE_gc = pb.emit("vector", lambda v, grp=grp: v.tensor_copy(
                gatesT[0:8, grp * 1024:(grp + 1) * 1024], psE.bitcast(BF16)[0:8, 0:1024]),
                waits=[(T, ttr)], inc=V)
        gatesT_done_v = psE_gc

        pb.barrier(skip=("gpsimd", "sync"))

        if nphases >= 2:
            CAP = 768
            pb.emit("sync", lambda s_: s_.dma_start(out=iota_sb[:, :], in_=d_iota), inc=dC, inck=16)
            t_cn = pb.emit("sync", lambda s_: s_.dma_start(out=cnst_sb[:, :], in_=d_cnst), inc=dC, inck=16)
            t_io = t_cn
            tri = cnst_sb[:, 0:128]
            allo = cnst_sb[:, 128:256]
            t_b2 = pb.sdma(gW[3], lambda g: g.dma_start(out=b2_sb[0:8, 0:1024], in_=d_b2),
                           waits=[(T, t_wgT_tr), (T, b1_done_T)])
            cd_cons_V = [0, 0]
            y_tok = [0] * NQT
            for gt in range(NQT):
                cdb = psC if gt % 2 == 0 else psD
                for dh in range(2):
                    pb.emit("tensor", lambda t, gt=gt, dh=dh, cdb=cdb: t.matmul(
                        cdb[:, dh * 512:(dh + 1) * 512],
                        gatesT[0:8, gt * 128:(gt + 1) * 128],
                        b2_sb[0:8, dh * 512:(dh + 1) * 512], start=True, stop=True),
                        waits=[(V, gatesT_done_v), (gW[3], t_b2), (V, cd_cons_V[gt % 2])] if dh == 0 else (),
                        inc=(T if dh == 1 else None))
                tmm = T.n
                cd_cons_V[gt % 2] = pb.emit("vector", lambda v, gt=gt, cdb=cdb: v.tensor_copy(
                    y_all[:, gt * 1024:(gt + 1) * 1024], cdb[:, :]),
                    waits=[(T, tmm), (V, y_tok[gt])], inc=V)
                y_tok[gt] = cd_cons_V[gt % 2]
            gatesT_mm_T = T.n
            P_ = Pl
            shift_cons_P = [0]

            hps = [psA, psB]
            hps_cons_S = [0, 0]
            w1_cons_T = [0, 0]
            w2_cons_T = [0, 0]
            xg_cons_T = 0
            pg_cons_T = 0
            osb_cons_T = 0
            ps_cons_T = 0
            ps_cons2 = [0, 0]
            psEF_c = [0, 0]
            pg_tt_T = [0] * NQT
            pos_cons = 0
            route_S = 0
            cd3_cons = [0, 0]
            psE_c = 0

            def o_slot(c):
                if c < 4:
                    return stg[:, :].bitcast(BF16)[:, c * 1024:(c + 1) * 1024]
                return scr8[:, 2048:6144].bitcast(BF16)[:, (c - 4) * 1024:(c - 3) * 1024]

            ex_idx = [0]
            for ex in (EXPERT_LIST if EXPERT_LIST is not None else range(N_EXPERTS)):
                _stg = _E_STAGES.get(ex_idx[0], 99)
                w2_tok = [[0, 0], [0, 0]]
                for half in range(2 if _stg >= 1 else 0):
                    for j in range(2):
                        w2_tok[half][j] = pb.sdma(gW[half * 2 + j], lambda g, ex=ex, half=half, j=j: g.dma_start(
                            out=w2res[:, (half * 2 + j) * 8192:(half * 2 + j + 1) * 8192].rearrange("p (c n) -> p c n", c=8),
                            in_=d_w2[ex][(half * 2 + j) * 1024:(half * 2 + j + 1) * 1024, :].rearrange(
                                "(c p) n -> p c n", p=128)),
                            waits=[(T, w2_cons_T[half])])
                _rstg = _R_STAGES.get(ex_idx[0], 99) if _R_STAGES else 99
                gsl = gates.rearrange("p (t e) -> p t e", e=8)[:, :, ex]
                _rt = stats[:, 32:48] if _rstg == 1 else route[:, 0:16]
                r0 = pb.emit("vector", lambda v, gsl=gsl, _rt=_rt: v.tensor_single_scalar(_rt, gsl, 0.0, ALU.is_gt),
                             waits=[(V, max(cd3_cons)), (V, pos_cons), (V, gates_last_v)], inc=V)
                if _rstg == 1:
                    pos_cons = r0
                    ex_idx[0] += 1
                    continue
                r1 = pb.emit("vector", lambda v: v.tensor_copy(routeb[:, 0:16], route[:, 0:16]),
                             waits=[(V, r0)], inc=V)
                if _rstg < 2:
                    pos_cons = r1
                    ex_idx[0] += 1
                    continue
                mA = pb.emit("tensor", lambda t: t.matmul(psC[:, 0:16], tri, routeb[:, 0:16],
                             start=True, stop=False),
                             waits=[(V, r1), (dC, t_cn), (V, cd3_cons[0]), (Sc, route_S)], inc=T)
                if _rstg < 3:
                    pb.emit("tensor", lambda t: t.matmul(psC[:, 0:16], tri[:, 0:16], routeb[:, 0:16],
                                 start=False, stop=True), inc=T)
                    pos_cons = r1
                    ex_idx[0] += 1
                    continue
                mB = pb.emit("tensor", lambda t: t.matmul(psD[0:1, 0:16], allo[:, 0:1], routeb[:, 0:16],
                             start=True, stop=True), waits=[(V, cd3_cons[1])], inc=T)
                sC = pb.emit("scalar", lambda s_: s_.copy(routeb[0:1, 16:32], psD[0:1, 0:16]),
                             waits=[(T, mB)], inc=Sc)
                mC = pb.emit("tensor", lambda t: t.transpose(psE.bitcast(BF16)[0:16, 0:1], routeb[0:1, 16:32],
                             idb[0:1, 0:1]), waits=[(Sc, sC), (Sc, psE_c), (Sc, psEF_c[0])], inc=T)
                sD = pb.emit("scalar", lambda s_: s_.copy(small[0:16, 96:97], psE.bitcast(BF16)[0:16, 0:1]),
                             waits=[(T, mC)], inc=Sc)
                vB = pb.emit("vector", lambda v: v.tensor_scalar_mul(routeb[0:16, 34:50], tri[0:16, 0:16],
                             small[0:16, 96:97]), waits=[(Sc, sD)], inc=V)
                mD = pb.emit("tensor", lambda t: t.matmul(psC[:, 0:16], allo[0:16, 0:128], routeb[0:16, 34:50],
                             start=False, stop=True), waits=[(V, vB)], inc=T)
                vP = pb.emit("vector", lambda v: v.tensor_copy(route[:, 16:32], psC[:, 0:16]),
                             waits=[(T, mD)], inc=V)
                v1 = pb.emit("vector", lambda v: v.tensor_scalar_add(route[:, 32:48], route[:, 16:32], 1.0),
                             waits=[(V, vP), (V, shift_cons_P[0])], inc=V)
                v2 = pb.emit("vector", lambda v: v.tensor_mul(route[:, 32:48], route[:, 32:48], route[:, 0:16]),
                             waits=[(V, v1)], inc=V)
                v3 = pb.emit("vector", lambda v: v.tensor_scalar_add(route[:, 32:48], route[:, 32:48], -1.0),
                             waits=[(V, v2)], inc=V)
                cd3_cons[0] = vP
                pos_cons = v3

                xg_tok_S = [0, 0]

                def gather_ct(ct):
                    nonlocal pg_cons_T, xg_cons_T
                    vsh = pb.emit("vector", lambda v, ct=ct: v.tensor_scalar(
                        route[:, 48:64], route[:, 32:48], float(ct * 256), None, ALU.subtract, ALU.bypass),
                        waits=[(V, pos_cons), (V, shift_cons_P[0])], inc=V)
                    for tt in range(NQT):
                        pb.emit("vector", lambda v, tt=tt: v.tensor_single_scalar(
                            stg[:, :].bitcast(BF16)[:, tt * 256:(tt + 1) * 256], iota_sb[:, 0:256],
                            route[:, 48 + tt:49 + tt], ALU.is_equal),
                            waits=[(V, vsh), (T, pg_tt_T[tt] if pg_tt_T[tt] else pg_cons_T),
                                   (T, osb_cons_T), (dC, t_io)],
                            inc=(V if tt == NQT - 1 else None))
                    pg_v = V.n
                    shift_cons_P[0] = pg_v
                    cc, sub = (0, ct) if ct < 2 else (1, 0)
                    for fc in range(8):
                        bank = hps[fc % 2]
                        for tt in range(NQT):
                            last_fc = (fc == 7)
                            pb.emit("tensor", lambda t, fc=fc, tt=tt, bank=bank: t.matmul(
                                bank[:, 0:256],
                                xn_tm[:, tt * 1024 + fc * 128: tt * 1024 + (fc + 1) * 128],
                                stg[:, :].bitcast(BF16)[:, tt * 256:(tt + 1) * 256],
                                start=(tt == 0), stop=(tt == 15)),
                                waits=[(V, pg_v), (Sc, hps_cons_S[fc % 2]), (T, gatesT_mm_T)] if tt == 0 else (),
                                inc=(T if (tt == 15 or last_fc) else None))
                            if last_fc:
                                pg_tt_T[tt] = T.n
                        tmm = T.n
                        pg_cons_T = tmm
                        if cc == 0:
                            dst = xg[:, fc * 512 + sub * 256: fc * 512 + (sub + 1) * 256]
                        else:
                            dst = xg[:, fc * 256:(fc + 1) * 256]
                        hps_cons_S[fc % 2] = pb.emit("scalar", lambda s_, dst=dst, bank=bank: s_.copy(
                            dst, bank[:, 0:256]),
                            waits=[(T, tmm), (T, xg_cons_T), (T, ps_cons_T)], inc=Sc)
                    xg_tok_S[cc] = Sc.n

                def h_stage(cc):
                    nonlocal xg_cons_T
                    clen = 512 if cc == 0 else 256
                    for j8 in range(8):
                        sl = j8 % 2
                        tw1 = pb.sdma(gV[j8 % 4], lambda g, ex=ex, j8=j8, sl=sl: g.dma_start(
                            out=w1bufs[sl].rearrange("p (c i) -> p c i", c=8),
                            in_=d_w1[ex][:, j8 * 512:(j8 + 1) * 512].rearrange("(c p) i -> p c i", p=128)),
                            waits=[(T, w1_cons_T[sl])])
                        for k in range(4):
                            ich = j8 * 4 + k
                            hb = hps[ich % 2]
                            for fc in range(8):
                                pb.emit("tensor", lambda t, sl=sl, k=k, fc=fc, hb=hb, clen=clen: t.matmul(
                                    hb[:, 0:clen],
                                    w1bufs[sl][:, fc * 512 + k * 128: fc * 512 + (k + 1) * 128],
                                    xg[:, fc * clen:(fc + 1) * clen],
                                    start=(fc == 0), stop=(fc == 7)),
                                    waits=[(gV[j8 % 4], tw1), (Sc, hps_cons_S[ich % 2]), (Sc, xg_tok_S[cc])] if fc == 0 else (),
                                    inc=(T if fc == 7 else None))
                            tmm = T.n
                            if k == 3:
                                w1_cons_T[sl] = tmm
                            xg_cons_T = tmm
                            hps_cons_S[ich % 2] = pb.emit("scalar", lambda s_, ex=ex, ich=ich, hb=hb, cc=cc, clen=clen: s_.activation(
                                h_sb[:, ich * clen:(ich + 1) * clen], hb[:, 0:clen], AF.Gelu,
                                bias=b1T[:, ex * 32 + ich: ex * 32 + ich + 1]),
                                waits=[(T, tmm)], inc=Sc)
                    return Sc.n

                def o_stage(cc, h_S):
                    nonlocal osb_cons_T
                    clen = 512 if cc == 0 else 256
                    cchs = range(4) if cc == 0 else range(2)
                    toks = []
                    for cl in cchs:
                        cglob = cl if cc == 0 else 4 + cl
                        cdb = psC if cl % 2 == 0 else psD
                        for ich in range(32):
                            half = ich // 16
                            for dh in range(2):
                                if ich == 0 and dh == 0:
                                    w = [(Sc, h_S), (V, cd3_cons[cl % 2]),
                                         (gW[0], w2_tok[0][0]), (gW[1], w2_tok[0][1]), (Sc, o_evac_S[cl % 2])]
                                elif ich == 16 and dh == 0:
                                    w = [(gW[2], w2_tok[1][0]), (gW[3], w2_tok[1][1])]
                                else:
                                    w = ()
                                pb.emit("tensor", lambda t, ich=ich, dh=dh, cl=cl, cdb=cdb, clen=clen: t.matmul(
                                    cdb[:, dh * 512:(dh + 1) * 512],
                                    h_sb[:, ich * clen + cl * 128: ich * clen + (cl + 1) * 128],
                                    w2res[:, ich * 1024 + dh * 512: ich * 1024 + (dh + 1) * 512],
                                    start=(ich == 0), stop=(ich == 31)),
                                    waits=w, inc=(T if (ich == 31 and dh == 1) else None))
                        tmm = T.n
                        if cc == 1 and cl == 1:
                            w2_cons_T[0] = tmm
                            w2_cons_T[1] = tmm
                        o_evac_S[cl % 2] = pb.emit("scalar", lambda s_, cglob=cglob, cdb=cdb: s_.copy(
                            o_slot(cglob), cdb[:, :]),
                            waits=[(T, tmm), (T, osb_cons_T)], inc=Sc)
                        toks.append(o_evac_S[cl % 2])
                    return toks

                def scatter_pass(p, cchs, o_toks):
                    nonlocal ps_cons_T, osb_cons_T, psE_c, route_S
                    ncch = len(cchs)
                    wide = ncch * 128
                    base_c = cchs[0] * 128
                    vsh = pb.emit("vector", lambda v, base_c=base_c: v.tensor_scalar(
                        route[:, 48:64], route[:, 32:48], float(base_c), None, ALU.subtract, ALU.bypass),
                        waits=[(V, pos_cons), (V, shift_cons_P[0])], inc=V)
                    def sc_build(tt):
                        par = tt % 2
                        ps_t = xg[:, par * 1024: par * 1024 + wide]
                        psT_t = xg[:, 2048 + par * 1024: 2048 + par * 1024 + wide]
                        c1 = pb.emit("vector", lambda v, tt=tt, ps_t=ps_t, wide=wide: v.tensor_single_scalar(
                            ps_t, iota_sb[:, 0:wide], route[:, 48 + tt:49 + tt], ALU.is_equal),
                            waits=[(V, vsh), (T, ps_cons2[par]), (T, xg_cons_T)], inc=V)
                        c2 = pb.emit("vector", lambda v, tt=tt, ps_t=ps_t, ex=ex: v.tensor_scalar_mul(
                            ps_t, ps_t, gates[:, tt * 8 + ex: tt * 8 + ex + 1]),
                            waits=[(V, c1)], inc=V)
                        shift_cons_P[0] = c1
                        pse = psE if par == 0 else psF
                        for k in range(ncch):
                            pb.emit("tensor", lambda t, k=k, ps_t=ps_t, pse=pse: t.transpose(
                                pse.bitcast(BF16)[:, k * 128:(k + 1) * 128],
                                ps_t[:, k * 128:(k + 1) * 128], idb[:, :]),
                                waits=[(V, c2), (Sc, psEF_c[par])] if k == 0 else (),
                                inc=(T if k == ncch - 1 else None))
                        t_tr = T.n
                        psEF_c[par] = pb.emit("scalar", lambda s_, psT_t=psT_t, wide=wide, pse=pse: s_.copy(
                            psT_t, pse.bitcast(BF16)[:, 0:wide]),
                            waits=[(T, t_tr), (T, ps_cons2[par])], inc=Sc)
                        return (tt, psT_t, psEF_c[par])

                    def sc_mm(st2):
                        nonlocal osb_cons_T
                        tt, psT_t, copS = st2
                        par = tt % 2
                        cdb = psC if par == 0 else psD
                        for k in range(ncch):
                            cglob = cchs[k]
                            for dh in range(2):
                                w = ([(Sc, copS), (V, cd3_cons[par])] + [(Sc, tok) for tok in o_toks]) if (k == 0 and dh == 0) else ()
                                pb.emit("tensor", lambda t, k=k, dh=dh, cglob=cglob, cdb=cdb, psT_t=psT_t: t.matmul(
                                    cdb[:, dh * 512:(dh + 1) * 512],
                                    psT_t[:, k * 128:(k + 1) * 128],
                                    o_slot(cglob)[:, dh * 512:(dh + 1) * 512],
                                    start=(k == 0), stop=(k == ncch - 1)),
                                    waits=w, inc=(T if (k == ncch - 1 and dh == 1) else None))
                        tmm = T.n
                        ps_cons2[par] = tmm
                        osb_cons_T = tmm
                        return (tt, tmm, cdb)

                    def sc_yadd(st3):
                        pgt, ptmm, pcdb = st3
                        cd3_cons[pgt % 2] = pb.emit("vector", lambda v, pgt=pgt, pcdb=pcdb, ptmm=ptmm: v.tensor_tensor(
                            y_all[:, pgt * 1024:(pgt + 1) * 1024], pcdb[:, :],
                            y_all[:, pgt * 1024:(pgt + 1) * 1024], ALU.add),
                            waits=[(T, ptmm), (V, y_tok[pgt])], inc=V)
                        y_tok[pgt] = cd3_cons[pgt % 2]

                    built = None
                    mmres = None
                    for tt in range(NQT):
                        b2_ = sc_build(tt)
                        if mmres is not None:
                            sc_yadd(mmres)
                            mmres = None
                        if built is not None:
                            mmres = sc_mm(built)
                        built = b2_
                    if mmres is not None:
                        sc_yadd(mmres)
                    mmres = sc_mm(built)
                    sc_yadd(mmres)
                    ps_cons_T = max(ps_cons2)

                o_evac_S = [0, 0]
                stages = _E_STAGES.get(ex_idx[0], 99)
                if stages >= 1:
                    gather_ct(0)
                    gather_ct(1)
                if stages >= 2:
                    hS0 = h_stage(0)
                if stages >= 3:
                    toks0 = o_stage(0, hS0)
                if stages >= 4:
                    scatter_pass(0, [0, 1, 2, 3], toks0)
                if stages >= 5:
                    gather_ct(2)
                    hS1 = h_stage(1)
                    toks1 = o_stage(1, hS1)
                if stages >= 6:
                    scatter_pass(1, [4, 5], toks1)
                ex_idx[0] += 1

            if not _E_STAGES:
                t_wdec = pb.sdma(gW[0], lambda g: g.dma_start(
                    out=wdec_sb.rearrange("p (c n) -> p c n", c=8),
                    in_=d_wdec.rearrange("(c p) n -> p c n", p=128)),
                    waits=[(T, T.n)])
            pb.barrier()

        if nphases >= 3:
            rb_cons = [0, 0]
            for tt in range(NQT):
                pse0 = psF if tt % 2 == 0 else psE
                for fc in range(8):
                    pb.emit("tensor", lambda t, tt=tt, fc=fc, pse0=pse0: t.transpose(
                        pse0.bitcast(BF16)[:, fc * 128:(fc + 1) * 128],
                        xn_tm[:, tt * 1024 + fc * 128: tt * 1024 + (fc + 1) * 128], idb[:, :]),
                        waits=[(V, rb_cons[tt % 2])] if fc == 0 else (), inc=(T if fc == 7 else None))
                ttr = T.n
                rb_cons[tt % 2] = pb.emit("vector", lambda v, tt=tt, pse0=pse0: v.tensor_copy(
                    xnfm.rearrange("p (c n) -> p c n", c=8)[:, :, tt * 128:(tt + 1) * 128],
                    pse0.rearrange("p (c n) -> p c n", c=8)),
                    waits=[(T, ttr)], inc=V)
            xnfm_ready_v = V.n
            ab_cons_v = [0, 0]
            for i in range(32):
                fcp, ts4 = divmod(i, 4)
                bank = psA if i % 2 == 0 else psB
                for fc in range(8):
                    pb.emit("tensor", lambda t, fc=fc, fcp=fcp, ts4=ts4, bank=bank: t.matmul(
                        bank[:, :],
                        wdec_sb[:, fc * 1024 + fcp * 128: fc * 1024 + (fcp + 1) * 128],
                        xnfm[:, fc * 2048 + ts4 * 512: fc * 2048 + (ts4 + 1) * 512],
                        start=(fc == 0), stop=(fc == 7)),
                        waits=[(gW[0], t_wdec), (V, ab_cons_v[i % 2]), (V, xnfm_ready_v)] if fc == 0 else (),
                        inc=(T if fc == 7 else None))
                tmm = T.n
                ab_cons_v[i % 2] = pb.emit("vector", lambda v, fcp=fcp, ts4=ts4, bank=bank: v.tensor_copy(
                    q_fm[:, fcp * 2048 + ts4 * 512: fcp * 2048 + (ts4 + 1) * 512], bank[:, :]),
                    waits=[(T, tmm)], inc=V)
            q_done_v = V.n
            qproj_done_T = T.n
            first_pair = [True] * NQT
            prep_done_V = 0
            keybuf_cons_T = 0
            keybuf_kt_T = [0, 0, 0, 0]
            mask_loaded = [0, 0]
            mask_cons_V = [0, 0]
            ln_slot_v = [0, 0]
            ln_prev = [0, 0]
            sc_cons = [(0, 0), (0, 0)]
            delta_cons = [0, 0]
            pt_copy_S = 0
            psF_cons = psF_cons_V
            sm_cons_S = [0, 0]
            p_cons_T = [0, 0, 0, 0]
            pt_cons_T = [0, 0]
            pbufs = [pbuf[:, 0:512], pbuf[:, 512:1024], gsc[:, 0:512], gsc[:, 512:1024]]
            scr_cons = [(0, 0)] * 4
            qt_state_v = [0] * NQT
            pending = None

            def prep_chunk(ch):
                nonlocal prep_done_V, psF_cons
                mb = ch % 2
                mask_loaded[mb] = pb.emit("sync", lambda s_, ch=ch, mb=mb: s_.dma_start(
                    out=mw[:, mb * 2048:(mb + 1) * 2048], in_=d_mask[ch]),
                    waits=[(V, mask_cons_V[mb])], inc=dM[mb], inck=16)
                for kt in range(4):
                    sl = kt % 2
                    stg_t = stg[:, sl * 1024:(sl + 1) * 1024]
                    tl = pb.emit("sync", lambda s_, ch=ch, kt=kt, stg_t=stg_t: s_.dma_start(
                        out=stg_t, in_=d_xk[ch * 512 + kt * 128: ch * 512 + (kt + 1) * 128, :]),
                        waits=[(V, ln_slot_v[sl])], inc=dX[sl], inck=16)
                    so = (kt % 2) * 16
                    t1 = pb.emit("vector", lambda v, stg_t=stg_t, so=so: v.bn_stats(stats[:, so:so+6], stg_t[:, 0:512]),
                                 waits=[(dX[sl], tl), (V, ln_prev[kt % 2])], inc=V)
                    t2 = pb.emit("vector", lambda v, stg_t=stg_t, so=so: v.bn_stats(stats[:, so+6:so+12], stg_t[:, 512:1024]),
                                 waits=[(V, t1)], inc=V)
                    t3 = pb.emit("vector", lambda v, so=so: v.bn_aggr(stats[:, so+12:so+14], stats[:, so:so+12]),
                                 waits=[(V, t2)], inc=V)
                    t4 = pb.emit("vector", lambda v, so=so: v.tensor_scalar_add(stats[:, so+14:so+15], stats[:, so+13:so+14], 1e-5),
                                 waits=[(V, t3)], inc=V)
                    t5 = pb.emit("scalar", lambda s_, so=so: s_.activation(stats[:, so+14:so+15], stats[:, so+14:so+15], AF.Ln),
                                 waits=[(V, t4)], inc=Sc)
                    t6 = pb.emit("scalar", lambda s_, so=so: s_.activation(stats[:, so+14:so+15], stats[:, so+14:so+15], AF.Exp, scale=-0.5),
                                 waits=[(Sc, t5)], inc=Sc)
                    t7 = pb.emit("vector", lambda v, stg_t=stg_t, kt=kt, so=so: v.tensor_scalar(
                        keys[:, ktm_off + kt * 1024: ktm_off + (kt + 1) * 1024], stg_t,
                        stats[:, so+12:so+13], stats[:, so+14:so+15], ALU.subtract, ALU.mult),
                        waits=[(Sc, t6), (T, keybuf_kt_T[kt] if keybuf_kt_T[kt] else keybuf_cons_T)], inc=V)
                    ln_prev[kt % 2] = t7
                    ln_slot_v[sl] = t7
                    for fc in range(8):
                        pb.emit("tensor", lambda t, fc=fc, kt=kt: t.transpose(
                            psF.bitcast(BF16)[:, fc * 128:(fc + 1) * 128],
                            keys[:, ktm_off + kt * 1024 + fc * 128: ktm_off + kt * 1024 + (fc + 1) * 128],
                            idb[:, :]),
                            waits=[(V, t7), (Sc, psF_cons)], inc=T)
                    ttr = T.n
                    psF_cons = pb.emit("scalar", lambda s_, kt=kt: s_.copy(
                        keys[:, kfm_off: kfm_off + 4096].rearrange("p (c n) -> p c n", c=8)[:, :, kt * 128:(kt + 1) * 128],
                        psF.rearrange("p (c n) -> p c n", c=8)),
                        waits=[(T, ttr)], inc=Sc)
                prep_done_V = psF_cons

            def group_scores(ch, qts, slot, gi):
                maskable = (ch // 4 == slot)
                mb = ch % 2
                out = []
                banks = [psA, psB]
                for j, qt in enumerate(qts):
                    sbank = banks[j]
                    w0 = [(Sc, prep_done_V), (V, q_done_v),
                          (V, sc_cons[j][0]), (Sc, sc_cons[j][1])]
                    for fc in range(8):
                        pb.emit("tensor", lambda t, fc=fc, qt=qt, sbank=sbank: t.matmul(
                            sbank[:, :],
                            q_fm[:, fc * 2048 + qt * 128: fc * 2048 + (qt + 1) * 128],
                            keys[:, kfm_off + fc * 512: kfm_off + (fc + 1) * 512],
                            start=(fc == 0), stop=(fc == 7)),
                            waits=w0 if fc == 0 else (), inc=(T if fc == 7 else None))
                    out.append(dict(qt=qt, j=j, t_sc=T.n))
                nonlocal keybuf_cons_T
                keybuf_cons_T = T.n
                return dict(sts=out, gi=gi, maskable=maskable, mb=mb)

            def group_softmax(grp):
                gi = grp["gi"]; maskable = grp["maskable"]; mb = grp["mb"]
                banks = [psA, psB]
                for st in grp["sts"]:
                    j, qt, t_sc = st["j"], st["qt"], st["t_sc"]
                    sbank = banks[j]
                    sub = qt % 4
                    sm_t = smb[:, j * 512:(j + 1) * 512]
                    psl = (gi % 2) * 2 + j
                    p_t = pbufs[psl]
                    scr = 32 + ((gi * 2 + j) % 4) * 8
                    scw = [(V, scr_cons[(gi * 2 + j) % 4][0]), (Sc, scr_cons[(gi * 2 + j) % 4][1])]
                    if maskable:
                        t_ma = pb.emit("vector", lambda v, sm_t=sm_t, sub=sub, mb=mb, sbank=sbank: v.scalar_tensor_tensor(
                            sm_t, sbank[:, :], 1.0,
                            mw[:, mb * 2048 + sub * 512: mb * 2048 + (sub + 1) * 512], ALU.mult, ALU.add),
                            waits=[(T, t_sc), (dM[mb], mask_loaded[mb]), (Sc, sm_cons_S[j])], inc=V)
                        mask_cons_V[mb] = t_ma
                        src = sm_t
                        t_mx = pb.emit("vector", lambda v, sm_t=sm_t, scr=scr: v.reduce_max(
                            mlbuf[:, scr:scr + 1], sm_t, axis=AX.X), waits=[(V, t_ma)] + scw, inc=V)
                        sc_v = t_ma
                    else:
                        src = sbank[:, :]
                        t_mx = pb.emit("vector", lambda v, sbank=sbank, scr=scr: v.reduce_max(
                            mlbuf[:, scr:scr + 1], sbank[:, :], axis=AX.X),
                            waits=[(T, t_sc)] + scw, inc=V)
                        sc_v = t_mx
                    mcol = mlbuf[:, qt:qt + 1]
                    lcol = mlbuf[:, 16 + qt:17 + qt]
                    fp = first_pair[qt]
                    if fp:
                        first_pair[qt] = False
                        t_m = pb.emit("vector", lambda v, mcol=mcol, scr=scr: v.tensor_copy(mcol, mlbuf[:, scr:scr + 1]),
                                      waits=[(V, t_mx), (V, qt_state_v[qt])], inc=V)
                        t_n = pb.emit("scalar", lambda s_, scr=scr, mcol=mcol: s_.mul(
                            mlbuf[:, scr + 1:scr + 2], mcol, -1.0), waits=[(V, t_m)], inc=Sc)
                        t_e = pb.emit("scalar", lambda s_, src=src, p_t=p_t, scr=scr, lcol=lcol: s_.activation(
                            p_t, src, AF.Exp, bias=mlbuf[:, scr + 1:scr + 2], accum_out=lcol),
                            waits=[(Sc, t_n), (T, p_cons_T[psl])], inc=Sc)
                        t_l = 0
                    else:
                        t_d = pb.emit("vector", lambda v, mcol=mcol, scr=scr: v.tensor_scalar(
                            mlbuf[:, scr + 2:scr + 3], mcol, mlbuf[:, scr:scr + 1], 0.0, ALU.subtract, ALU.min),
                            waits=[(V, t_mx), (V, qt_state_v[qt])], inc=V)
                        t_cm = pb.emit("vector", lambda v, mcol=mcol, scr=scr: v.tensor_single_scalar(
                            mcol, mcol, mlbuf[:, scr + 2:scr + 3], ALU.subtract), waits=[(V, t_d)], inc=V)
                        t_al = pb.emit("scalar", lambda s_, scr=scr: s_.activation(
                            mlbuf[:, scr + 3:scr + 4], mlbuf[:, scr + 2:scr + 3], AF.Exp),
                            waits=[(V, t_d)], inc=Sc)
                        t_n = pb.emit("scalar", lambda s_, mcol=mcol, scr=scr: s_.mul(
                            mlbuf[:, scr:scr + 1], mcol, -1.0),
                            waits=[(Sc, t_al), (V, t_cm)], inc=Sc)
                        t_e = pb.emit("scalar", lambda s_, src=src, p_t=p_t, scr=scr: s_.activation(
                            p_t, src, AF.Exp, bias=mlbuf[:, scr:scr + 1], accum_out=mlbuf[:, scr + 4:scr + 5]),
                            waits=[(Sc, t_n), (T, p_cons_T[psl])], inc=Sc)
                        t_l = pb.emit("vector", lambda v, lcol=lcol, scr=scr: v.tensor_scalar(
                            lcol, lcol, mlbuf[:, scr + 3:scr + 4], mlbuf[:, scr + 4:scr + 5], ALU.mult, ALU.add),
                            waits=[(Sc, t_e)], inc=V)
                    sc_cons[j] = (sc_v, t_e)
                    if maskable:
                        sm_cons_S[j] = t_e
                    st.update(fp=fp, scr=scr, p_t=p_t, psl=psl, t_e=t_e, t_l=t_l)

            def group_tailA(grp):
                nonlocal pt_copy_S
                sts = grp["sts"]
                for st in sts:
                    j, t_e, p_t = st["j"], st["t_e"], st["p_t"]
                    for kt in range(4):
                        pb.emit("tensor", lambda t, kt=kt, p_t=p_t, j=j: t.transpose(
                            psE.bitcast(BF16)[:, j * 512 + kt * 128: j * 512 + (kt + 1) * 128],
                            p_t[:, kt * 128:(kt + 1) * 128], idb[:, :]),
                            waits=[(Sc, t_e), (Sc, pt_copy_S)] if kt == 0 else (),
                            inc=(T if (kt == 3 and j == 1) else None))
                t_tr = T.n
                for st in sts:
                    p_cons_T[st["psl"]] = t_tr
                pt_copy_S = pb.emit("scalar", lambda s_: s_.copy(ptb[:, 0:1024], psE.bitcast(BF16)[:, 0:1024]),
                                    waits=[(T, t_tr), (T, pt_cons_T[0]), (T, pt_cons_T[1])], inc=Sc)
                grp["copyS"] = pt_copy_S

            def group_tailB(grp, release_kts=False):
                nonlocal keybuf_cons_T
                sts = grp["sts"]
                dbanks = [psC, psD]
                for st in sts:
                    j, qt, scr, fp = st["j"], st["qt"], st["scr"], st["fp"]
                    dbank = dbanks[j]
                    pt_t = ptb[:, j * 512:(j + 1) * 512]
                    for kt in range(4):
                        for dh in range(2):
                            rel = release_kts and j == 1 and dh == 1
                            pb.emit("tensor", lambda t, kt=kt, dh=dh, pt_t=pt_t, dbank=dbank: t.matmul(
                                dbank[:, dh * 512:(dh + 1) * 512],
                                pt_t[:, kt * 128:(kt + 1) * 128],
                                keys[:, ktm_off + kt * 1024 + dh * 512: ktm_off + kt * 1024 + (dh + 1) * 512],
                                start=(kt == 0), stop=(kt == 3)),
                                waits=[(Sc, grp["copyS"]), (V, delta_cons[j])] if (kt == 0 and dh == 0) else (),
                                inc=(T if (rel or (kt == 3 and dh == 1)) else None))
                            if rel:
                                keybuf_kt_T[kt] = T.n
                    t_dl = T.n
                    pt_cons_T[j] = t_dl
                    keybuf_cons_T = t_dl
                    ctx_t = ctx_sb[:, qt * 1024:(qt + 1) * 1024]
                    if fp:
                        t_cx = pb.emit("vector", lambda v, ctx_t=ctx_t, dbank=dbank: v.tensor_copy(ctx_t, dbank[:, :]),
                                       waits=[(T, t_dl), (V, st["t_l"]), (T, qproj_done_T)], inc=V)
                    else:
                        t_cx = pb.emit("vector", lambda v, ctx_t=ctx_t, dbank=dbank, scr=scr: v.scalar_tensor_tensor(
                            ctx_t, ctx_t, mlbuf[:, scr + 3:scr + 4], dbank[:, :], ALU.mult, ALU.add),
                            waits=[(T, t_dl), (V, st["t_l"])], inc=V)
                    delta_cons[j] = t_cx
                    scr_cons[(grp["gi"] * 2 + j) % 4] = (t_cx, st["t_e"])
                    qt_state_v[qt] = t_cx

            gi = 0
            for ch in range(NCH):
                prep_chunk(ch)
                groups = [(slot * 4 + 2 * sp, slot * 4 + 2 * sp + 1, slot)
                          for slot in range(ch // 4, 4) for sp in range(2)]
                for idx, (qa, qb, slot) in enumerate(groups):
                    grp = group_scores(ch, (qa, qb), slot, gi)
                    if pending is not None:
                        group_tailA(pending)
                    group_softmax(grp)
                    if pending is not None:
                        group_tailB(pending)
                    pending = grp
                    gi += 1
                group_tailA(pending)
                group_tailB(pending, release_kts=True)
                pending = None
            pb.barrier()

        if nphases >= 4:
            t_ri = pb.emit("vector", lambda v: v.reciprocal(mlbuf[:, 48:64], mlbuf[:, 16:32]), inc=V)
            gsc_cons_T2 = [0, 0]
            ef_cons = [0, 0]
            psEF = [psE, psF]
            gscs = [gsc[:, :], mw[:, 0:1024]]
            for qt in range(NQT):
                gb = gscs[qt % 2]
                t_s2 = pb.emit("vector", lambda v, qt=qt, gb=gb: v.tensor_scalar_mul(
                    gb, ctx_sb[:, qt * 1024:(qt + 1) * 1024], mlbuf[:, 48 + qt:49 + qt]),
                    waits=[(V, t_ri), (T, gsc_cons_T2[qt % 2])], inc=V)
                pe = psEF[qt % 2]
                for fc in range(8):
                    pb.emit("tensor", lambda t, fc=fc, pe=pe, gb=gb: t.transpose(
                        pe.bitcast(BF16)[:, fc * 128:(fc + 1) * 128],
                        gb[:, fc * 128:(fc + 1) * 128], idb[:, :]),
                        waits=[(V, t_s2), (V, ef_cons[qt % 2])] if fc == 0 else (), inc=(T if fc == 7 else None))
                gsc_cons_T2[qt % 2] = T.n
                ef_cons[qt % 2] = pb.emit("vector", lambda v, qt=qt, pe=pe: v.tensor_copy(
                    ctx_fm.rearrange("p (c n) -> p c n", c=8)[:, :, qt * 128:(qt + 1) * 128],
                    pe.rearrange("p (c n) -> p c n", c=8)),
                    waits=[(T, gsc_cons_T2[qt % 2])], inc=V)
            ctxfm_done_v = V.n

            t_bout = pb.sdma(gW[0], lambda g: g.dma_start(out=bout_sb[0:1, 0:1024], in_=d_bout),
                             waits=[(V, ctxfm_done_v)])
            wstg_cons_T = [0, 0]
            wef_cons = [ef_cons[0], ef_cons[1]]
            wstgs = [wstg, mw[:, 1024:3072]]
            for dt in range(8):
                wsl = dt % 2
                wst = wstgs[wsl]
                tw = pb.sdma(gV[dt % 4], lambda g, dt=dt, wst=wst: g.dma_start(
                    out=wst[:, 0:2048], in_=d_wout[dt * 128:(dt + 1) * 128, :]),
                    waits=[(T, wstg_cons_T[wsl]), (V, ctxfm_done_v)])
                for half in range(2):
                    pe = psEF[half]
                    for k in range(8):
                        f2c = half * 8 + k
                        pb.emit("tensor", lambda t, f2c=f2c, k=k, pe=pe, tw=tw, dt=dt, wst=wst: t.transpose(
                            pe.bitcast(BF16)[:, k * 128:(k + 1) * 128],
                            wst[:, f2c * 128:(f2c + 1) * 128], idb[:, :]),
                            waits=[(gV[dt % 4], tw), (V, wef_cons[half])] if k == 0 else (),
                            inc=(T if k == 7 else None))
                    t_tr = T.n
                    wef_cons[half] = pb.emit("vector", lambda v, dt=dt, half=half, pe=pe: v.tensor_copy(
                        wout_fm.rearrange("p (c n) -> p c n", c=16)[:, half * 8:(half + 1) * 8, dt * 128:(dt + 1) * 128],
                        pe.rearrange("p (c n) -> p c n", c=8)),
                        waits=[(T, t_tr)], inc=V)
                wstg_cons_T[wsl] = T.n
            wout_ready_v = V.n

            gtiles = [scr8[:, 2048:4096].bitcast(BF16), gsc[:, :]]
            cd_cons_S = [0, 0]
            stile_cons_V = [0, 0]
            gtile_cons_V = [0, 0]
            stg_slot_v = [0, 0]
            ost_dma = [0, 0]
            for qt in range(NQT):
                sl = qt % 2
                cd = psC if qt % 2 == 0 else psD
                for dh in range(2):
                    for fc in range(16):
                        lhs = (xnfm[:, fc * 2048 + qt * 128: fc * 2048 + (qt + 1) * 128] if fc < 8
                               else ctx_fm[:, (fc - 8) * 2048 + qt * 128: (fc - 8) * 2048 + (qt + 1) * 128])
                        pb.emit("tensor", lambda t, lhs=lhs, dh=dh, fc=fc, cd=cd: t.matmul(
                            cd[:, dh * 512:(dh + 1) * 512], lhs,
                            wout_fm[:, fc * 1024 + dh * 512: fc * 1024 + (dh + 1) * 512],
                            start=(fc == 0), stop=False),
                            waits=[(V, ctxfm_done_v), (V, wout_ready_v), (V, t_ones),
                                   (gW[0], t_bout), (Sc, cd_cons_S[qt % 2])] if (fc == 0 and dh == 0) else (),
                            inc=None)
                    pb.emit("tensor", lambda t, dh=dh, cd=cd: t.matmul(
                        cd[:, dh * 512:(dh + 1) * 512], cnst_sb[0:1, 128:256],
                        bout_sb[0:1, dh * 512:(dh + 1) * 512], start=False, stop=True),
                        inc=(T if dh == 1 else None))
                tmm = T.n
                tl = pb.emit("sync", lambda s_, qt=qt, sl=sl: s_.dma_start(
                    out=stg[:, sl * 1024:(sl + 1) * 1024], in_=d_xq[qt * 128:(qt + 1) * 128, :]),
                    waits=[(V, stg_slot_v[sl]), (dO[sl], ost_dma[sl])], inc=dX[sl], inck=16)
                t_ta = pb.emit("scalar", lambda s_, cd=cd, sl=sl: s_.activation(
                    stile_b[sl][:, 0:1024], cd[:, :], AF.Tanh),
                    waits=[(T, tmm), (V, stile_cons_V[sl])], inc=Sc)
                cd_cons_S[qt % 2] = t_ta
                t_si = pb.emit("scalar", lambda s_, qt=qt, sl=sl: s_.activation(
                    gtiles[sl][:, 0:1024], y_all[:, qt * 1024:(qt + 1) * 1024], AF.Silu),
                    waits=[(V, gtile_cons_V[sl])], inc=Sc)
                t_mu = pb.emit("vector", lambda v, sl=sl: v.tensor_mul(
                    stile_b[sl][:, 0:1024], stile_b[sl][:, 0:1024], gtiles[sl][:, 0:1024]),
                    waits=[(Sc, t_ta), (Sc, t_si)], inc=V)
                gtile_cons_V[sl] = t_mu
                t_ad = pb.emit("vector", lambda v, sl=sl: v.tensor_add(
                    stg[:, sl * 1024:(sl + 1) * 1024], stg[:, sl * 1024:(sl + 1) * 1024], stile_b[sl][:, 0:1024]),
                    waits=[(V, t_mu), (dX[sl], tl)], inc=V)
                stile_cons_V[sl] = t_ad
                ost_dma[sl] = pb.emit("sync", lambda s_, qt=qt, sl=sl: s_.dma_start(
                    out=d_y[qt * 128:(qt + 1) * 128, :], in_=stg[:, sl * 1024:(sl + 1) * 1024]),
                    waits=[(V, t_ad)], inc=dO[sl], inck=16)
                stg_slot_v[sl] = t_ad
            pb.wait("sync", (dO[0], dO[0].n), (dO[1], dO[1].n))
            pb.barrier()

        if debug:
            pb.barrier()
            def dump(name, src):
                pb.sdma(gD, lambda g, name=name, src=src: g.dma_start(out=dbg[name], in_=src))
            dump("dbg_xnfm", xn_fm[:, :])
            if nphases == 3:
                dump("dbg_qfm", q_fm[:, :])
            dump("dbg_gates", gates[:, :])
            if nphases in (3,):
                dump("dbg_ctx", ctx_sb[:, :])
                dump("dbg_ml", mlbuf[:, 0:32])
            if nphases >= 2:
                dump("dbg_yffn", y_all[:, :])

            pb.wait("gpsimd", (gD, gD.n))

        pb.build()
    return nc


def _host_inputs(inputs):
    x = np.ascontiguousarray(np.asarray(inputs["L0_attn_hidden_states"]), dtype=np.float32)
    shared = {
        "w_dec": np.ascontiguousarray(np.asarray(inputs["L0_attn_w_dec_attn"], np.float32)),
        "w_out": np.ascontiguousarray(np.asarray(inputs["L0_attn_w_out"], np.float32)),
        "b_out": np.ascontiguousarray(np.asarray(inputs["L0_attn_b_out"], np.float32)).reshape(1, D),
        "Wg": np.ascontiguousarray(np.asarray(inputs["L0_ffn_Wg"], np.float32)),
        "W1": np.ascontiguousarray(np.asarray(inputs["L0_ffn_W1"], np.float32)),
        "b1": np.ascontiguousarray(np.asarray(inputs["L0_ffn_b1"], np.float32)).reshape(8, 32, 128),
        "W2": np.ascontiguousarray(np.asarray(inputs["L0_ffn_W2"], np.float32)),
        "b2": np.ascontiguousarray(np.asarray(inputs["L0_ffn_b2"], np.float32)),
        "ident_bf": np.eye(128, dtype=ml_dtypes.bfloat16),
        "ident_f32": np.eye(128, dtype=np.float32),
        "iota768": np.broadcast_to(np.arange(512, dtype=np.float32), (128, 512)).copy(),
        "cnst_bf": np.concatenate([
            np.triu(np.ones((128, 128), dtype=np.float32), k=1),
            np.ones((128, 128), dtype=np.float32)], axis=1).astype(ml_dtypes.bfloat16),
    }
    in_maps = []
    jj = np.arange(512)
    for core in range(8):
        s, m = core // 4, core % 4
        blocks = [m, 7 - m, 8 + m, 15 - m]
        xq = np.concatenate([x[s, b * 512:(b + 1) * 512] for b in blocks], axis=0)
        masks = np.zeros((NCH, 128, 4 * 512), dtype=ml_dtypes.bfloat16)
        for ch in range(NCH):
            slot = ch // 4
            b = blocks[slot]
            for sub in range(4):
                i_glob = b * 512 + sub * 128 + np.arange(128)
                j_glob = ch * 512 + jj
                allow = j_glob[None, :] < i_glob[:, None]
                if b == 0 and sub == 0 and ch == 0:
                    allow[0, :] = False
                    allow[0, 0] = True
                masks[ch, :, sub * 512:(sub + 1) * 512] = np.where(
                    allow, np.float32(0.0), np.float32(NEG)).astype(ml_dtypes.bfloat16)
        im = dict(shared)
        im["xq"] = np.ascontiguousarray(xq)
        im["xk"] = np.ascontiguousarray(x[s])
        im["masks"] = masks
        in_maps.append(im)
    return in_maps


_nc_cache = {}

def kernel(**inputs):
    if "full" not in _nc_cache:
        _nc_cache["full"] = build_nc()
    nc = _nc_cache["full"]
    in_maps = _host_inputs(inputs)
    res = run_bass_kernel_spmd(nc, in_maps, core_ids=list(range(8)))
    out = np.empty((2, S, D), dtype=np.float32)
    for core in range(8):
        s, m = core // 4, core % 4
        blocks = [m, 7 - m, 8 + m, 15 - m]
        shard = res.results[core]["y"]
        for k, b in enumerate(blocks):
            out[s, b * 512:(b + 1) * 512] = shard[k * 512:(k + 1) * 512]
    return out


# version 20
# speedup vs baseline: 1.0227x; 1.0040x over previous
import numpy as np
import ml_dtypes
_E_STAGES = {}
_R_STAGES = {}
from contextlib import ExitStack
import concourse.bass as bass
import concourse.mybir as mybir
from concourse.bass_utils import run_bass_kernel_spmd

F32 = mybir.dt.float32
BF16 = mybir.dt.bfloat16
AF = mybir.ActivationFunctionType
ALU = mybir.AluOpType
AX = mybir.AxisListType

D = 1024
N_EXPERTS = 8
EXPERT_LIST = None
S = 8192
NCH = 16
TQ = 2048
NQT = 16
NEG = -3.0e38

ENGINES = ("sync", "scalar", "vector", "tensor", "gpsimd")


class Ctr:
    __slots__ = ("sem", "n", "name")
    def __init__(self, sem, name):
        self.sem = sem; self.n = 0; self.name = name


class PB:
    def __init__(self, nc):
        self.nc = nc
        self.q = {e: [] for e in ENGINES}
        self.waited = {e: {} for e in ENGINES}
        self.ctrs = []

    def ctr(self, stack, name):
        sem = stack.enter_context(self.nc.semaphore(name))
        c = Ctr(sem, name)
        self.ctrs.append(c)
        return c

    def emit(self, eng, fn, waits=(), inc=None, inck=1):
        w = [(c, v) for (c, v) in waits if v > 0 and v > self.waited[eng].get(c.name, 0)]
        for c, v in w:
            self.waited[eng][c.name] = v
        tok = None
        if inc is not None:
            inc.n += inck
            tok = inc.n
        def run(e, fn=fn, w=tuple(w), inc=inc, inck=inck):
            for c, v in w:
                e.wait_ge(c.sem, v)
            inst = fn(e)
            if inc is not None:
                inst.then_inc(inc.sem, inck)
        self.q[eng].append(run)
        return tok

    def wait(self, eng, *waits):
        w = [(c, v) for (c, v) in waits if v > 0 and v > self.waited[eng].get(c.name, 0)]
        for c, v in w:
            self.waited[eng][c.name] = v
        if not w:
            return
        def run(e, w=tuple(w)):
            for c, v in w:
                e.wait_ge(c.sem, v)
        self.q[eng].append(run)

    def sdma(self, ctr, fn, waits=()):
        self.wait("gpsimd", (ctr, ctr.n))
        return self.emit("gpsimd", fn, waits=waits, inc=ctr, inck=16)

    def barrier(self, skip=()):
        snap = [(c, c.n) for c in self.ctrs]
        for e in ENGINES:
            if e in skip:
                continue
            self.wait(e, *snap)

    def build(self):
        with self.nc.Block() as block:
            for eng in ENGINES:
                if not self.q[eng]:
                    continue
                def body(e, q=self.q[eng]):
                    for f in q:
                        f(e)
                getattr(block, eng)(body)


def build_nc(nphases=4, debug=False):
    nc = bass.Bass("TRN2", target_bir_lowering=False)
    pb = PB(nc)

    d_xq   = nc.dram_tensor("xq", [TQ, D], F32, kind="ExternalInput").ap()
    d_xk   = nc.dram_tensor("xk", [S, D], F32, kind="ExternalInput").ap()
    d_mask = nc.dram_tensor("masks", [NCH, 128, 4 * 512], BF16, kind="ExternalInput").ap()
    d_wdec = nc.dram_tensor("w_dec", [D, D], F32, kind="ExternalInput").ap()
    d_wout = nc.dram_tensor("w_out", [D, 2 * D], F32, kind="ExternalInput").ap()
    d_bout = nc.dram_tensor("b_out", [1, D], F32, kind="ExternalInput").ap()
    d_wg   = nc.dram_tensor("Wg", [8, D], F32, kind="ExternalInput").ap()
    d_w1   = nc.dram_tensor("W1", [8, D, 4096], F32, kind="ExternalInput").ap()
    d_b1   = nc.dram_tensor("b1", [8, 32, 128], F32, kind="ExternalInput").ap()
    d_w2   = nc.dram_tensor("W2", [8, 4096, D], F32, kind="ExternalInput").ap()
    d_b2   = nc.dram_tensor("b2", [8, D], F32, kind="ExternalInput").ap()
    d_idb  = nc.dram_tensor("ident_bf", [128, 128], BF16, kind="ExternalInput").ap()
    d_iota = nc.dram_tensor("iota768", [128, 512], F32, kind="ExternalInput").ap()
    d_cnst = nc.dram_tensor("cnst_bf", [128, 256], BF16, kind="ExternalInput").ap()
    d_idf  = nc.dram_tensor("ident_f32", [128, 128], F32, kind="ExternalInput").ap()
    d_y    = nc.dram_tensor("y", [TQ, D], F32, kind="ExternalOutput").ap()
    dbg = {}
    if debug:
        for nm, shp in [("dbg_xnfm", [128, 8 * 2048]), ("dbg_qfm", [128, 8 * 2048]),
                        ("dbg_gates", [128, 16 * 8]), ("dbg_ctx", [128, 16 * 1024]),
                        ("dbg_ml", [128, 32]), ("dbg_yffn", [128, 16 * 1024])]:
            dbg[nm] = nc.dram_tensor(nm, shp, F32, kind="ExternalOutput").ap()

    with ExitStack() as stack:
        e = stack.enter_context

        xn_fm = e(nc.sbuf_tensor("xn_fm", [128, 8 * 2048], BF16))
        s_sb  = e(nc.sbuf_tensor("s_sb", [128, 16 * 1024], BF16))
        q_h   = e(nc.sbuf_tensor("q_h", [128, 16 * 1024], BF16))
        big   = e(nc.sbuf_tensor("big", [128, 32 * 1024], BF16))
        keys  = e(nc.sbuf_tensor("keys", [128, 8 * 1024], BF16))
        stg   = e(nc.sbuf_tensor("stg", [128, 2 * 1024], F32))
        scr8  = e(nc.sbuf_tensor("scr8", [128, 8 * 1024], mybir.dt.uint8))
        mw    = e(nc.sbuf_tensor("mw", [128, 2 * 2048], BF16))
        mlbuf = e(nc.sbuf_tensor("mlbuf", [128, 64], F32))
        stats = e(nc.sbuf_tensor("stats", [128, 48], F32))
        gates = e(nc.sbuf_tensor("gates", [128, 16 * 8], F32))
        gatesb= e(nc.sbuf_tensor("gatesb", [128, 16 * 8], BF16))
        small = e(nc.sbuf_tensor("small", [128, 128], F32))
        logit2 = small[:, 0:128]
        b1T   = e(nc.sbuf_tensor("b1T", [128, 8 * 32], F32))
        wgT   = e(nc.sbuf_tensor("wgT", [128, 64], BF16))
        idb   = e(nc.sbuf_tensor("idb", [128, 128], BF16))
        gsc   = e(nc.sbuf_tensor("gsc", [128, 1024], BF16))
        iota_sb = e(nc.sbuf_tensor("iota_sb", [128, 512], F32))
        cnst_sb = e(nc.sbuf_tensor("cnst_sb", [128, 256], BF16))
        route  = small[:, 0:64]
        routeb = small[:, 64:96].bitcast(BF16)

        rA      = xn_fm
        rQ      = q_h
        xnfm    = rQ[:, :]
        xn_tm   = rA[:, :]
        q_fm    = rA[:, :]
        ctx_fm  = rA[:, :]
        h_sb    = rQ[:, :]
        w2res   = big[:, :]
        ctx_sb  = big[:, :].bitcast(F32)
        wdec_sb = big[:, 16 * 1024: 24 * 1024]
        wout_fm = big[:, 0:16 * 1024]
        y_all   = s_sb[:, :]
        w1bufs  = [keys[:, i * 4096:(i + 1) * 4096] for i in range(2)]
        xg      = mw[:, :]
        kfm_off = 0
        ktm_off = 4096
        wg_sb  = scr8[:, 0:2048].bitcast(BF16)
        idf    = scr8[:, 2048:2560].bitcast(F32)
        b1raw  = scr8[:, 2560:4608].bitcast(F32)
        gTt    = scr8[:, 4608:6656].bitcast(BF16)
        b2_sb  = scr8[:, 0:2048].bitcast(BF16)
        smb    = scr8[:, 0:4096].bitcast(F32)
        pbuf   = scr8[:, 4096:6144].bitcast(BF16)
        ptb    = scr8[:, 6144:8192].bitcast(BF16)
        wstg   = scr8[:, 0:4096].bitcast(BF16)
        bout_sb= scr8[:, 4096:6144].bitcast(BF16)
        stile_b= [scr8[:, 0:2048].bitcast(BF16), scr8[:, 6144:8192].bitcast(BF16)]
        gtile_b= None

        psA = e(nc.psum_tensor("psA", [128, 512], F32))
        psB = e(nc.psum_tensor("psB", [128, 512], F32))
        psC = e(nc.psum_tensor("psC", [128, 1024], F32))
        psD = e(nc.psum_tensor("psD", [128, 1024], F32))
        psE = e(nc.psum_tensor("psE", [128, 1024], BF16))
        psF = e(nc.psum_tensor("psF", [128, 1024], BF16))

        V = pb.ctr(stack, "V"); Sc = pb.ctr(stack, "S"); T = pb.ctr(stack, "T"); Pl = pb.ctr(stack, "P")
        dC = pb.ctr(stack, "dC")
        dX = [pb.ctr(stack, "dX0"), pb.ctr(stack, "dX1")]
        dM = [pb.ctr(stack, "dM0"), pb.ctr(stack, "dM1")]
        dO = [pb.ctr(stack, "dO0"), pb.ctr(stack, "dO1")]
        gW = [pb.ctr(stack, f"gW{i}") for i in range(4)]
        gV = [pb.ctr(stack, f"gV{i}") for i in range(4)]
        gD = pb.ctr(stack, "gD")

        psEF0_cons = [(V, 0), (V, 0)]
        pb.emit("sync", lambda s_: s_.dma_start(out=idb[:, :], in_=d_idb), inc=dC, inck=16)
        pb.emit("sync", lambda s_: s_.dma_start(out=idf[:, 0:128], in_=d_idf), inc=dC, inck=16)
        tk_id = dC.n
        t_wg = pb.sdma(gW[1], lambda g: g.dma_start(out=wg_sb[0:8, :], in_=d_wg))
        t_ones = 0

        for fc in range(8):
            pb.emit("tensor", lambda t, fc=fc: t.transpose(
                psE.bitcast(BF16)[:, fc * 8:(fc + 1) * 8],
                wg_sb[0:8, fc * 128:(fc + 1) * 128], idb[0:8, 0:8]),
                waits=[(gW[1], t_wg), (dC, tk_id)], inc=T)
        t_wgT_tr = T.n
        t_wgT = pb.emit("scalar", lambda s_: s_.copy(wgT[:, :], psE.bitcast(BF16)[:, 0:64]),
                        waits=[(T, t_wgT_tr)], inc=Sc)
        psEF0_cons[1] = (Sc, t_wgT)

        prev_b1_tr = 0
        prev_b1_cp = 0
        for ex in range(8):
            tl = pb.emit("sync", lambda s_, ex=ex: s_.dma_start(out=b1raw[0:32, 0:128], in_=d_b1[ex]),
                         waits=[(T, prev_b1_tr), (dC, tk_id)], inc=dC, inck=16)
            prev_b1_tr = pb.emit("tensor", lambda t: t.transpose(psC[:, 0:32], b1raw[0:32, 0:128], idf[0:32, 0:32]),
                                 waits=[(dC, tl), (Sc, prev_b1_cp)], inc=T)
            prev_b1_cp = pb.emit("scalar", lambda s_, ex=ex: s_.copy(b1T[:, ex * 32:(ex + 1) * 32], psC[:, 0:32]),
                                 waits=[(T, prev_b1_tr)], inc=Sc)
        b1_done_T = prev_b1_tr
        b1_done_V = prev_b1_cp

        ln_prev = [0, 0]
        slot_apply_v = [0, 0]
        gsc_cons_T = 0
        psF_cons_V = 0
        psCD_cons = [(Sc, b1_done_V), (V, 0)]
        gates_last_v = 0
        chain_pend = None

        def gates_chain(qt, tmm):
            nonlocal gates_last_v
            lg = gates[:, qt * 8:(qt + 1) * 8]
            l2 = logit2[:, qt * 8:(qt + 1) * 8]
            cdb = psC if qt % 2 == 0 else psD
            c0 = pb.emit("vector", lambda v, lg=lg, cdb=cdb: v.tensor_copy(lg, cdb[:, 0:8]),
                         waits=[(T, tmm), (V, gates_last_v)], inc=V)
            psCD_cons[qt % 2] = (V, c0)
            c1 = pb.emit("vector", lambda v, lg=lg: v.reduce_max(stats[:, 32:33], lg, axis=AX.X),
                         waits=[(V, c0)], inc=V)
            c2 = pb.emit("vector", lambda v, lg=lg, l2=l2: v.tensor_single_scalar(l2, lg, stats[:, 32:33], ALU.is_ge),
                         waits=[(V, c1)], inc=V)
            c3 = pb.emit("vector", lambda v, lg=lg, l2=l2: v.scalar_tensor_tensor(l2, l2, -1.0e30, lg, ALU.mult, ALU.add),
                         waits=[(V, c2)], inc=V)
            c4 = pb.emit("vector", lambda v, l2=l2: v.reduce_max(stats[:, 33:34], l2, axis=AX.X),
                         waits=[(V, c3)], inc=V)
            c5 = pb.emit("vector", lambda v: v.tensor_sub(stats[:, 34:35], stats[:, 33:34], stats[:, 32:33]),
                         waits=[(V, c4)], inc=V)
            c6 = pb.emit("scalar", lambda s_: s_.activation(stats[:, 35:36], stats[:, 34:35], AF.Exp),
                         waits=[(V, c5)], inc=Sc)
            c7 = pb.emit("vector", lambda v: v.tensor_scalar_add(stats[:, 35:36], stats[:, 35:36], 1.0),
                         waits=[(Sc, c6)], inc=V)
            c8 = pb.emit("vector", lambda v: v.reciprocal(stats[:, 36:37], stats[:, 35:36]), waits=[(V, c7)], inc=V)
            c9 = pb.emit("vector", lambda v: v.tensor_scalar(stats[:, 37:38], stats[:, 36:37], -1.0, 1.0, ALU.mult, ALU.add),
                         waits=[(V, c8)], inc=V)
            ca = pb.emit("vector", lambda v, l2=l2: v.tensor_single_scalar(l2, l2, stats[:, 33:34], ALU.is_ge),
                         waits=[(V, c9)], inc=V)
            cb = pb.emit("vector", lambda v, lg=lg: v.tensor_single_scalar(lg, lg, stats[:, 32:33], ALU.is_ge),
                         waits=[(V, ca)], inc=V)
            cc = pb.emit("vector", lambda v, lg=lg: v.tensor_scalar_mul(lg, lg, stats[:, 36:37]),
                         waits=[(V, cb)], inc=V)
            cd = pb.emit("vector", lambda v, lg=lg, l2=l2: v.scalar_tensor_tensor(lg, l2, stats[:, 37:38], lg, ALU.mult, ALU.add),
                         waits=[(V, cc)], inc=V)
            ce = pb.emit("vector", lambda v, qt=qt, lg=lg: v.tensor_copy(gatesb[:, qt * 8:(qt + 1) * 8], lg),
                         waits=[(V, cd)], inc=V)
            gates_last_v = ce

        for tt in range(NQT):
            sl = tt % 2
            stg_t = stg[:, sl * 1024:(sl + 1) * 1024]
            tl = pb.emit("sync", lambda s_, tt=tt, stg_t=stg_t: s_.dma_start(
                out=stg_t, in_=d_xq[tt * 128:(tt + 1) * 128, :]),
                waits=[(V, slot_apply_v[sl])], inc=dX[sl], inck=16)
            so = (tt % 2) * 16
            t1 = pb.emit("vector", lambda v, stg_t=stg_t, so=so: v.bn_stats(stats[:, so:so+6], stg_t[:, 0:512]),
                         waits=[(dX[sl], tl), (V, ln_prev[tt % 2])], inc=V)
            t2 = pb.emit("vector", lambda v, stg_t=stg_t, so=so: v.bn_stats(stats[:, so+6:so+12], stg_t[:, 512:1024]),
                         waits=[(V, t1)], inc=V)
            t3 = pb.emit("vector", lambda v, so=so: v.bn_aggr(stats[:, so+12:so+14], stats[:, so:so+12]), waits=[(V, t2)], inc=V)
            t4 = pb.emit("vector", lambda v, so=so: v.tensor_scalar_add(stats[:, so+14:so+15], stats[:, so+13:so+14], 1e-5),
                         waits=[(V, t3)], inc=V)
            t5 = pb.emit("scalar", lambda s_, so=so: s_.activation(stats[:, so+14:so+15], stats[:, so+14:so+15], AF.Ln),
                         waits=[(V, t4)], inc=Sc)
            t6 = pb.emit("scalar", lambda s_, so=so: s_.activation(stats[:, so+14:so+15], stats[:, so+14:so+15], AF.Exp, scale=-0.5),
                         waits=[(Sc, t5)], inc=Sc)
            xnt = xn_tm[:, tt * 1024:(tt + 1) * 1024]
            if chain_pend is not None:
                gates_chain(*chain_pend)
                chain_pend = None
            t7 = pb.emit("vector", lambda v, stg_t=stg_t, so=so, xnt=xnt: v.tensor_scalar(
                xnt, stg_t, stats[:, so+12:so+13], stats[:, so+14:so+15], ALU.subtract, ALU.mult),
                waits=[(Sc, t6), (T, gsc_cons_T)], inc=V)
            ln_prev[tt % 2] = t7
            slot_apply_v[sl] = t7
            pse0 = psF if tt % 2 == 0 else psE
            for fc in range(8):
                pb.emit("tensor", lambda t, fc=fc, xnt=xnt, pse0=pse0: t.transpose(
                    pse0.bitcast(BF16)[:, fc * 128:(fc + 1) * 128],
                    xnt[:, fc * 128:(fc + 1) * 128], idb[:, :]),
                    waits=[(V, t7), (dC, tk_id), psEF0_cons[tt % 2]], inc=T)
            gsc_cons_T = T.n
            psEF0_cons[tt % 2] = (Sc, pb.emit("scalar", lambda s_, tt=tt, pse0=pse0: s_.copy(
                xnfm.rearrange("p (c n) -> p c n", c=8)[:, :, tt * 128:(tt + 1) * 128],
                pse0.rearrange("p (c n) -> p c n", c=8)),
                waits=[(T, gsc_cons_T)], inc=Sc))
            cdb = psC if tt % 2 == 0 else psD
            for fc in range(8):
                pb.emit("tensor", lambda t, tt=tt, fc=fc, cdb=cdb: t.matmul(
                    cdb[:, 0:8],
                    xnfm[:, fc * 2048 + tt * 128: fc * 2048 + (tt + 1) * 128],
                    wgT[:, fc * 8:(fc + 1) * 8],
                    start=(fc == 0), stop=(fc == 7)),
                    waits=[psEF0_cons[tt % 2], (Sc, t_wgT), psCD_cons[tt % 2], (T, b1_done_T)] if fc == 0 else (),
                    inc=(T if fc == 7 else None))
            tmm = T.n
            chain_pend = (tt, tmm)
        gates_chain(*chain_pend)
        chain_pend = None

        gatesT = mw[:, :]
        psE_gc = 0
        for grp in range(2):
            for k in range(8):
                qt = grp * 8 + k
                pb.emit("tensor", lambda t, qt=qt, k=k: t.transpose(
                    psE.bitcast(BF16)[0:8, k * 128:(k + 1) * 128],
                    gatesb[:, qt * 8:(qt + 1) * 8], idb[:, :]),
                    waits=[(V, gates_last_v), (V, psE_gc), psEF0_cons[1]] if k == 0 else (),
                    inc=(T if k == 7 else None))
            ttr = T.n
            psE_gc = pb.emit("vector", lambda v, grp=grp: v.tensor_copy(
                gatesT[0:8, grp * 1024:(grp + 1) * 1024], psE.bitcast(BF16)[0:8, 0:1024]),
                waits=[(T, ttr)], inc=V)
        gatesT_done_v = psE_gc

        pb.barrier(skip=("gpsimd", "sync"))

        if nphases >= 2:
            CAP = 768
            pb.emit("sync", lambda s_: s_.dma_start(out=iota_sb[:, :], in_=d_iota), inc=dC, inck=16)
            t_cn = pb.emit("sync", lambda s_: s_.dma_start(out=cnst_sb[:, :], in_=d_cnst), inc=dC, inck=16)
            t_io = t_cn
            tri = cnst_sb[:, 0:128]
            allo = cnst_sb[:, 128:256]
            t_b2 = pb.sdma(gW[3], lambda g: g.dma_start(out=b2_sb[0:8, 0:1024], in_=d_b2),
                           waits=[(T, t_wgT_tr), (T, b1_done_T)])
            cd_cons_V = [0, 0]
            y_tok = [0] * NQT
            for gt in range(NQT):
                cdb = psC if gt % 2 == 0 else psD
                for dh in range(2):
                    pb.emit("tensor", lambda t, gt=gt, dh=dh, cdb=cdb: t.matmul(
                        cdb[:, dh * 512:(dh + 1) * 512],
                        gatesT[0:8, gt * 128:(gt + 1) * 128],
                        b2_sb[0:8, dh * 512:(dh + 1) * 512], start=True, stop=True),
                        waits=[(V, gatesT_done_v), (gW[3], t_b2), (V, cd_cons_V[gt % 2])] if dh == 0 else (),
                        inc=(T if dh == 1 else None))
                tmm = T.n
                cd_cons_V[gt % 2] = pb.emit("vector", lambda v, gt=gt, cdb=cdb: v.tensor_copy(
                    y_all[:, gt * 1024:(gt + 1) * 1024], cdb[:, :]),
                    waits=[(T, tmm), (V, y_tok[gt])], inc=V)
                y_tok[gt] = cd_cons_V[gt % 2]
            gatesT_mm_T = T.n
            P_ = Pl
            shift_cons_P = [0]

            hps = [psA, psB]
            hps_cons_S = [0, 0]
            w1_cons_T = [0, 0]
            w2_cons_T = [0, 0]
            xg_cons_T = 0
            pg_cons_T = 0
            osb_cons_T = 0
            ps_cons_T = 0
            ps_cons2 = [0, 0]
            psEF_c = [0, 0]
            pg_tt_T = [0] * NQT
            pos_cons = 0
            route_S = 0
            cd3_cons = [0, 0]
            psE_c = 0

            def o_slot(c):
                if c < 4:
                    return stg[:, :].bitcast(BF16)[:, c * 1024:(c + 1) * 1024]
                return scr8[:, 2048:6144].bitcast(BF16)[:, (c - 4) * 1024:(c - 3) * 1024]

            ex_idx = [0]
            for ex in (EXPERT_LIST if EXPERT_LIST is not None else range(N_EXPERTS)):
                _stg = _E_STAGES.get(ex_idx[0], 99)
                w2_tok = [[0, 0], [0, 0]]
                for half in range(2 if _stg >= 1 else 0):
                    for j in range(2):
                        w2_tok[half][j] = pb.sdma(gW[half * 2 + j], lambda g, ex=ex, half=half, j=j: g.dma_start(
                            out=w2res[:, (half * 2 + j) * 8192:(half * 2 + j + 1) * 8192].rearrange("p (c n) -> p c n", c=8),
                            in_=d_w2[ex][(half * 2 + j) * 1024:(half * 2 + j + 1) * 1024, :].rearrange(
                                "(c p) n -> p c n", p=128)),
                            waits=[(T, w2_cons_T[half])])
                _rstg = _R_STAGES.get(ex_idx[0], 99) if _R_STAGES else 99
                gsl = gates.rearrange("p (t e) -> p t e", e=8)[:, :, ex]
                _rt = stats[:, 32:48] if _rstg == 1 else route[:, 0:16]
                r0 = pb.emit("vector", lambda v, gsl=gsl, _rt=_rt: v.tensor_single_scalar(_rt, gsl, 0.0, ALU.is_gt),
                             waits=[(V, max(cd3_cons)), (V, pos_cons), (V, gates_last_v)], inc=V)
                if _rstg == 1:
                    pos_cons = r0
                    ex_idx[0] += 1
                    continue
                r1 = pb.emit("vector", lambda v: v.tensor_copy(routeb[:, 0:16], route[:, 0:16]),
                             waits=[(V, r0)], inc=V)
                if _rstg < 2:
                    pos_cons = r1
                    ex_idx[0] += 1
                    continue
                mA = pb.emit("tensor", lambda t: t.matmul(psC[:, 0:16], tri, routeb[:, 0:16],
                             start=True, stop=False),
                             waits=[(V, r1), (dC, t_cn), (V, cd3_cons[0]), (Sc, route_S)], inc=T)
                if _rstg < 3:
                    pb.emit("tensor", lambda t: t.matmul(psC[:, 0:16], tri[:, 0:16], routeb[:, 0:16],
                                 start=False, stop=True), inc=T)
                    pos_cons = r1
                    ex_idx[0] += 1
                    continue
                mB = pb.emit("tensor", lambda t: t.matmul(psD[0:1, 0:16], allo[:, 0:1], routeb[:, 0:16],
                             start=True, stop=True), waits=[(V, cd3_cons[1])], inc=T)
                sC = pb.emit("scalar", lambda s_: s_.copy(routeb[0:1, 16:32], psD[0:1, 0:16]),
                             waits=[(T, mB)], inc=Sc)
                mC = pb.emit("tensor", lambda t: t.transpose(psE.bitcast(BF16)[0:16, 0:1], routeb[0:1, 16:32],
                             idb[0:1, 0:1]), waits=[(Sc, sC), (Sc, psE_c), (Sc, psEF_c[0])], inc=T)
                sD = pb.emit("scalar", lambda s_: s_.copy(small[0:16, 96:97], psE.bitcast(BF16)[0:16, 0:1]),
                             waits=[(T, mC)], inc=Sc)
                vB = pb.emit("vector", lambda v: v.tensor_scalar_mul(routeb[0:16, 34:50], tri[0:16, 0:16],
                             small[0:16, 96:97]), waits=[(Sc, sD)], inc=V)
                mD = pb.emit("tensor", lambda t: t.matmul(psC[:, 0:16], allo[0:16, 0:128], routeb[0:16, 34:50],
                             start=False, stop=True), waits=[(V, vB)], inc=T)
                vP = pb.emit("vector", lambda v: v.tensor_copy(route[:, 16:32], psC[:, 0:16]),
                             waits=[(T, mD)], inc=V)
                v1 = pb.emit("vector", lambda v: v.tensor_scalar_add(route[:, 32:48], route[:, 16:32], 1.0),
                             waits=[(V, vP), (V, shift_cons_P[0])], inc=V)
                v2 = pb.emit("vector", lambda v: v.tensor_mul(route[:, 32:48], route[:, 32:48], route[:, 0:16]),
                             waits=[(V, v1)], inc=V)
                v3 = pb.emit("vector", lambda v: v.tensor_scalar_add(route[:, 32:48], route[:, 32:48], -1.0),
                             waits=[(V, v2)], inc=V)
                cd3_cons[0] = vP
                pos_cons = v3

                xg_tok_S = [0, 0]

                def gather_ct(ct):
                    nonlocal pg_cons_T, xg_cons_T
                    vsh = pb.emit("vector", lambda v, ct=ct: v.tensor_scalar(
                        route[:, 48:64], route[:, 32:48], float(ct * 256), None, ALU.subtract, ALU.bypass),
                        waits=[(V, pos_cons), (V, shift_cons_P[0])], inc=V)
                    for tt in range(NQT):
                        pb.emit("vector", lambda v, tt=tt: v.tensor_single_scalar(
                            stg[:, :].bitcast(BF16)[:, tt * 256:(tt + 1) * 256], iota_sb[:, 0:256],
                            route[:, 48 + tt:49 + tt], ALU.is_equal),
                            waits=[(V, vsh), (T, pg_tt_T[tt] if pg_tt_T[tt] else pg_cons_T),
                                   (T, osb_cons_T), (dC, t_io)],
                            inc=(V if tt == NQT - 1 else None))
                    pg_v = V.n
                    shift_cons_P[0] = pg_v
                    cc, sub = (0, ct) if ct < 2 else (1, 0)
                    for fc in range(8):
                        bank = hps[fc % 2]
                        for tt in range(NQT):
                            last_fc = (fc == 7)
                            pb.emit("tensor", lambda t, fc=fc, tt=tt, bank=bank: t.matmul(
                                bank[:, 0:256],
                                xn_tm[:, tt * 1024 + fc * 128: tt * 1024 + (fc + 1) * 128],
                                stg[:, :].bitcast(BF16)[:, tt * 256:(tt + 1) * 256],
                                start=(tt == 0), stop=(tt == 15)),
                                waits=[(V, pg_v), (Sc, hps_cons_S[fc % 2]), (T, gatesT_mm_T)] if tt == 0 else (),
                                inc=(T if (tt == 15 or last_fc) else None))
                            if last_fc:
                                pg_tt_T[tt] = T.n
                        tmm = T.n
                        pg_cons_T = tmm
                        if cc == 0:
                            dst = xg[:, fc * 512 + sub * 256: fc * 512 + (sub + 1) * 256]
                        else:
                            dst = xg[:, fc * 256:(fc + 1) * 256]
                        hps_cons_S[fc % 2] = pb.emit("scalar", lambda s_, dst=dst, bank=bank: s_.copy(
                            dst, bank[:, 0:256]),
                            waits=[(T, tmm), (T, xg_cons_T), (T, ps_cons_T)], inc=Sc)
                    xg_tok_S[cc] = Sc.n

                def h_stage(cc):
                    nonlocal xg_cons_T
                    clen = 512 if cc == 0 else 256
                    for j8 in range(8):
                        sl = j8 % 2
                        tw1 = pb.sdma(gV[j8 % 4], lambda g, ex=ex, j8=j8, sl=sl: g.dma_start(
                            out=w1bufs[sl].rearrange("p (c i) -> p c i", c=8),
                            in_=d_w1[ex][:, j8 * 512:(j8 + 1) * 512].rearrange("(c p) i -> p c i", p=128)),
                            waits=[(T, w1_cons_T[sl])])
                        for k in range(4):
                            ich = j8 * 4 + k
                            hb = hps[ich % 2]
                            for fc in range(8):
                                pb.emit("tensor", lambda t, sl=sl, k=k, fc=fc, hb=hb, clen=clen: t.matmul(
                                    hb[:, 0:clen],
                                    w1bufs[sl][:, fc * 512 + k * 128: fc * 512 + (k + 1) * 128],
                                    xg[:, fc * clen:(fc + 1) * clen],
                                    start=(fc == 0), stop=(fc == 7)),
                                    waits=[(gV[j8 % 4], tw1), (Sc, hps_cons_S[ich % 2]), (Sc, xg_tok_S[cc])] if fc == 0 else (),
                                    inc=(T if fc == 7 else None))
                            tmm = T.n
                            if k == 3:
                                w1_cons_T[sl] = tmm
                            xg_cons_T = tmm
                            hps_cons_S[ich % 2] = pb.emit("scalar", lambda s_, ex=ex, ich=ich, hb=hb, cc=cc, clen=clen: s_.activation(
                                h_sb[:, ich * clen:(ich + 1) * clen], hb[:, 0:clen], AF.Gelu,
                                bias=b1T[:, ex * 32 + ich: ex * 32 + ich + 1]),
                                waits=[(T, tmm)], inc=Sc)
                    return Sc.n

                def o_stage(cc, h_S):
                    nonlocal osb_cons_T
                    clen = 512 if cc == 0 else 256
                    cchs = range(4) if cc == 0 else range(2)
                    toks = []
                    for cl in cchs:
                        cglob = cl if cc == 0 else 4 + cl
                        cdb = psC if cl % 2 == 0 else psD
                        for ich in range(32):
                            half = ich // 16
                            for dh in range(2):
                                if ich == 0 and dh == 0:
                                    w = [(Sc, h_S), (V, cd3_cons[cl % 2]),
                                         (gW[0], w2_tok[0][0]), (gW[1], w2_tok[0][1]), (Sc, o_evac_S[cl % 2])]
                                elif ich == 16 and dh == 0:
                                    w = [(gW[2], w2_tok[1][0]), (gW[3], w2_tok[1][1])]
                                else:
                                    w = ()
                                pb.emit("tensor", lambda t, ich=ich, dh=dh, cl=cl, cdb=cdb, clen=clen: t.matmul(
                                    cdb[:, dh * 512:(dh + 1) * 512],
                                    h_sb[:, ich * clen + cl * 128: ich * clen + (cl + 1) * 128],
                                    w2res[:, ich * 1024 + dh * 512: ich * 1024 + (dh + 1) * 512],
                                    start=(ich == 0), stop=(ich == 31)),
                                    waits=w, inc=(T if (ich == 31 and dh == 1) else None))
                        tmm = T.n
                        if cc == 1 and cl == 1:
                            w2_cons_T[0] = tmm
                            w2_cons_T[1] = tmm
                        o_evac_S[cl % 2] = pb.emit("scalar", lambda s_, cglob=cglob, cdb=cdb: s_.copy(
                            o_slot(cglob), cdb[:, :]),
                            waits=[(T, tmm), (T, osb_cons_T)], inc=Sc)
                        toks.append(o_evac_S[cl % 2])
                    return toks

                def scatter_pass(p, cchs, o_toks):
                    nonlocal ps_cons_T, osb_cons_T, psE_c, route_S
                    ncch = len(cchs)
                    wide = ncch * 128
                    base_c = cchs[0] * 128
                    vsh = pb.emit("vector", lambda v, base_c=base_c: v.tensor_scalar(
                        route[:, 48:64], route[:, 32:48], float(base_c), None, ALU.subtract, ALU.bypass),
                        waits=[(V, pos_cons), (V, shift_cons_P[0])], inc=V)
                    def sc_build(tt):
                        par = tt % 2
                        ps_t = xg[:, par * 1024: par * 1024 + wide]
                        psT_t = xg[:, 2048 + par * 1024: 2048 + par * 1024 + wide]
                        c1 = pb.emit("vector", lambda v, tt=tt, ps_t=ps_t, wide=wide: v.tensor_single_scalar(
                            ps_t, iota_sb[:, 0:wide], route[:, 48 + tt:49 + tt], ALU.is_equal),
                            waits=[(V, vsh), (T, ps_cons2[par]), (T, xg_cons_T)], inc=V)
                        c2 = pb.emit("vector", lambda v, tt=tt, ps_t=ps_t, ex=ex: v.tensor_scalar_mul(
                            ps_t, ps_t, gates[:, tt * 8 + ex: tt * 8 + ex + 1]),
                            waits=[(V, c1)], inc=V)
                        shift_cons_P[0] = c1
                        pse = psE if par == 0 else psF
                        for k in range(ncch):
                            pb.emit("tensor", lambda t, k=k, ps_t=ps_t, pse=pse: t.transpose(
                                pse.bitcast(BF16)[:, k * 128:(k + 1) * 128],
                                ps_t[:, k * 128:(k + 1) * 128], idb[:, :]),
                                waits=[(V, c2), (Sc, psEF_c[par])] if k == 0 else (),
                                inc=(T if k == ncch - 1 else None))
                        t_tr = T.n
                        psEF_c[par] = pb.emit("scalar", lambda s_, psT_t=psT_t, wide=wide, pse=pse: s_.copy(
                            psT_t, pse.bitcast(BF16)[:, 0:wide]),
                            waits=[(T, t_tr), (T, ps_cons2[par])], inc=Sc)
                        return (tt, psT_t, psEF_c[par])

                    def sc_mm(st2):
                        nonlocal osb_cons_T
                        tt, psT_t, copS = st2
                        par = tt % 2
                        cdb = psC if par == 0 else psD
                        for k in range(ncch):
                            cglob = cchs[k]
                            for dh in range(2):
                                w = ([(Sc, copS), (V, cd3_cons[par])] + [(Sc, tok) for tok in o_toks]) if (k == 0 and dh == 0) else ()
                                pb.emit("tensor", lambda t, k=k, dh=dh, cglob=cglob, cdb=cdb, psT_t=psT_t: t.matmul(
                                    cdb[:, dh * 512:(dh + 1) * 512],
                                    psT_t[:, k * 128:(k + 1) * 128],
                                    o_slot(cglob)[:, dh * 512:(dh + 1) * 512],
                                    start=(k == 0), stop=(k == ncch - 1)),
                                    waits=w, inc=(T if (k == ncch - 1 and dh == 1) else None))
                        tmm = T.n
                        ps_cons2[par] = tmm
                        osb_cons_T = tmm
                        return (tt, tmm, cdb)

                    def sc_yadd(st3):
                        pgt, ptmm, pcdb = st3
                        cd3_cons[pgt % 2] = pb.emit("vector", lambda v, pgt=pgt, pcdb=pcdb, ptmm=ptmm: v.tensor_tensor(
                            y_all[:, pgt * 1024:(pgt + 1) * 1024], pcdb[:, :],
                            y_all[:, pgt * 1024:(pgt + 1) * 1024], ALU.add),
                            waits=[(T, ptmm), (V, y_tok[pgt])], inc=V)
                        y_tok[pgt] = cd3_cons[pgt % 2]

                    built = None
                    mmres = None
                    for tt in range(NQT):
                        b2_ = sc_build(tt)
                        if mmres is not None:
                            sc_yadd(mmres)
                            mmres = None
                        if built is not None:
                            mmres = sc_mm(built)
                        built = b2_
                    if mmres is not None:
                        sc_yadd(mmres)
                    mmres = sc_mm(built)
                    sc_yadd(mmres)
                    ps_cons_T = max(ps_cons2)

                o_evac_S = [0, 0]
                stages = _E_STAGES.get(ex_idx[0], 99)
                if stages >= 1:
                    gather_ct(0)
                    gather_ct(1)
                if stages >= 2:
                    hS0 = h_stage(0)
                if stages >= 3:
                    toks0 = o_stage(0, hS0)
                if stages >= 4:
                    scatter_pass(0, [0, 1, 2, 3], toks0)
                if stages >= 5:
                    gather_ct(2)
                    hS1 = h_stage(1)
                    toks1 = o_stage(1, hS1)
                if stages >= 6:
                    scatter_pass(1, [4, 5], toks1)
                ex_idx[0] += 1

            if not _E_STAGES:
                t_wdec = pb.sdma(gW[0], lambda g: g.dma_start(
                    out=wdec_sb.rearrange("p (c n) -> p c n", c=8),
                    in_=d_wdec.rearrange("(c p) n -> p c n", p=128)),
                    waits=[(T, T.n)])
            pb.barrier()

        if nphases >= 3:
            rb_cons = [0, 0]
            for tt in range(NQT):
                pse0 = psF if tt % 2 == 0 else psE
                for fc in range(8):
                    pb.emit("tensor", lambda t, tt=tt, fc=fc, pse0=pse0: t.transpose(
                        pse0.bitcast(BF16)[:, fc * 128:(fc + 1) * 128],
                        xn_tm[:, tt * 1024 + fc * 128: tt * 1024 + (fc + 1) * 128], idb[:, :]),
                        waits=[(V, rb_cons[tt % 2])] if fc == 0 else (), inc=(T if fc == 7 else None))
                ttr = T.n
                rb_cons[tt % 2] = pb.emit("vector", lambda v, tt=tt, pse0=pse0: v.tensor_copy(
                    xnfm.rearrange("p (c n) -> p c n", c=8)[:, :, tt * 128:(tt + 1) * 128],
                    pse0.rearrange("p (c n) -> p c n", c=8)),
                    waits=[(T, ttr)], inc=V)
            xnfm_ready_v = V.n
            ab_cons_v = [0, 0]
            for i in range(32):
                fcp, ts4 = divmod(i, 4)
                bank = psA if i % 2 == 0 else psB
                for fc in range(8):
                    pb.emit("tensor", lambda t, fc=fc, fcp=fcp, ts4=ts4, bank=bank: t.matmul(
                        bank[:, :],
                        wdec_sb[:, fc * 1024 + fcp * 128: fc * 1024 + (fcp + 1) * 128],
                        xnfm[:, fc * 2048 + ts4 * 512: fc * 2048 + (ts4 + 1) * 512],
                        start=(fc == 0), stop=(fc == 7)),
                        waits=[(gW[0], t_wdec), (V, ab_cons_v[i % 2]), (V, xnfm_ready_v)] if fc == 0 else (),
                        inc=(T if fc == 7 else None))
                tmm = T.n
                ab_cons_v[i % 2] = pb.emit("vector", lambda v, fcp=fcp, ts4=ts4, bank=bank: v.tensor_copy(
                    q_fm[:, fcp * 2048 + ts4 * 512: fcp * 2048 + (ts4 + 1) * 512], bank[:, :]),
                    waits=[(T, tmm)], inc=V)
            q_done_v = V.n
            qproj_done_T = T.n
            first_pair = [True] * NQT
            prep_done_V = 0
            keybuf_cons_T = 0
            keybuf_kt_T = [0, 0, 0, 0]
            mask_loaded = [0, 0]
            mask_cons_V = [0, 0]
            ln_slot_v = [0, 0]
            ln_prev = [0, 0]
            sc_cons = [(0, 0), (0, 0)]
            delta_cons = [0, 0]
            pt_copy_S = 0
            psF_cons = psF_cons_V
            sm_cons_S = [0, 0]
            p_cons_T = [0, 0, 0, 0]
            pt_cons_T = [0, 0]
            pbufs = [pbuf[:, 0:512], pbuf[:, 512:1024], gsc[:, 0:512], gsc[:, 512:1024]]
            scr_cons = [(0, 0)] * 4
            qt_state_v = [0] * NQT
            pending = None

            def prep_chunk(ch):
                nonlocal prep_done_V, psF_cons
                mb = ch % 2
                mask_loaded[mb] = pb.emit("sync", lambda s_, ch=ch, mb=mb: s_.dma_start(
                    out=mw[:, mb * 2048:(mb + 1) * 2048], in_=d_mask[ch]),
                    waits=[(V, mask_cons_V[mb])], inc=dM[mb], inck=16)
                for kt in range(4):
                    sl = kt % 2
                    stg_t = stg[:, sl * 1024:(sl + 1) * 1024]
                    tl = pb.emit("sync", lambda s_, ch=ch, kt=kt, stg_t=stg_t: s_.dma_start(
                        out=stg_t, in_=d_xk[ch * 512 + kt * 128: ch * 512 + (kt + 1) * 128, :]),
                        waits=[(V, ln_slot_v[sl])], inc=dX[sl], inck=16)
                    so = (kt % 2) * 16
                    t1 = pb.emit("vector", lambda v, stg_t=stg_t, so=so: v.bn_stats(stats[:, so:so+6], stg_t[:, 0:512]),
                                 waits=[(dX[sl], tl), (V, ln_prev[kt % 2])], inc=V)
                    t2 = pb.emit("vector", lambda v, stg_t=stg_t, so=so: v.bn_stats(stats[:, so+6:so+12], stg_t[:, 512:1024]),
                                 waits=[(V, t1)], inc=V)
                    t3 = pb.emit("vector", lambda v, so=so: v.bn_aggr(stats[:, so+12:so+14], stats[:, so:so+12]),
                                 waits=[(V, t2)], inc=V)
                    t4 = pb.emit("vector", lambda v, so=so: v.tensor_scalar_add(stats[:, so+14:so+15], stats[:, so+13:so+14], 1e-5),
                                 waits=[(V, t3)], inc=V)
                    t5 = pb.emit("scalar", lambda s_, so=so: s_.activation(stats[:, so+14:so+15], stats[:, so+14:so+15], AF.Ln),
                                 waits=[(V, t4)], inc=Sc)
                    t6 = pb.emit("scalar", lambda s_, so=so: s_.activation(stats[:, so+14:so+15], stats[:, so+14:so+15], AF.Exp, scale=-0.5),
                                 waits=[(Sc, t5)], inc=Sc)
                    t7 = pb.emit("vector", lambda v, stg_t=stg_t, kt=kt, so=so: v.tensor_scalar(
                        keys[:, ktm_off + kt * 1024: ktm_off + (kt + 1) * 1024], stg_t,
                        stats[:, so+12:so+13], stats[:, so+14:so+15], ALU.subtract, ALU.mult),
                        waits=[(Sc, t6), (T, keybuf_kt_T[kt] if keybuf_kt_T[kt] else keybuf_cons_T)], inc=V)
                    ln_prev[kt % 2] = t7
                    ln_slot_v[sl] = t7
                    for fc in range(8):
                        pb.emit("tensor", lambda t, fc=fc, kt=kt: t.transpose(
                            psF.bitcast(BF16)[:, fc * 128:(fc + 1) * 128],
                            keys[:, ktm_off + kt * 1024 + fc * 128: ktm_off + kt * 1024 + (fc + 1) * 128],
                            idb[:, :]),
                            waits=[(V, t7), (Sc, psF_cons)], inc=T)
                    ttr = T.n
                    psF_cons = pb.emit("scalar", lambda s_, kt=kt: s_.copy(
                        keys[:, kfm_off: kfm_off + 4096].rearrange("p (c n) -> p c n", c=8)[:, :, kt * 128:(kt + 1) * 128],
                        psF.rearrange("p (c n) -> p c n", c=8)),
                        waits=[(T, ttr)], inc=Sc)
                prep_done_V = psF_cons

            def group_scores(ch, qts, slot, gi):
                maskable = (ch // 4 == slot)
                mb = ch % 2
                out = []
                banks = [psA, psB]
                for j, qt in enumerate(qts):
                    sbank = banks[j]
                    w0 = [(Sc, prep_done_V), (V, q_done_v),
                          (V, sc_cons[j][0]), (Sc, sc_cons[j][1])]
                    for fc in range(8):
                        pb.emit("tensor", lambda t, fc=fc, qt=qt, sbank=sbank: t.matmul(
                            sbank[:, :],
                            q_fm[:, fc * 2048 + qt * 128: fc * 2048 + (qt + 1) * 128],
                            keys[:, kfm_off + fc * 512: kfm_off + (fc + 1) * 512],
                            start=(fc == 0), stop=(fc == 7)),
                            waits=w0 if fc == 0 else (), inc=(T if fc == 7 else None))
                    out.append(dict(qt=qt, j=j, t_sc=T.n))
                nonlocal keybuf_cons_T
                keybuf_cons_T = T.n
                return dict(sts=out, gi=gi, maskable=maskable, mb=mb)

            def group_softmax(grp):
                gi = grp["gi"]; maskable = grp["maskable"]; mb = grp["mb"]
                banks = [psA, psB]
                for st in grp["sts"]:
                    j, qt, t_sc = st["j"], st["qt"], st["t_sc"]
                    sbank = banks[j]
                    sub = qt % 4
                    sm_t = smb[:, j * 512:(j + 1) * 512]
                    psl = (gi % 2) * 2 + j
                    p_t = pbufs[psl]
                    scr = 32 + ((gi * 2 + j) % 4) * 8
                    scw = [(V, scr_cons[(gi * 2 + j) % 4][0]), (Sc, scr_cons[(gi * 2 + j) % 4][1])]
                    if maskable:
                        t_ma = pb.emit("vector", lambda v, sm_t=sm_t, sub=sub, mb=mb, sbank=sbank: v.scalar_tensor_tensor(
                            sm_t, sbank[:, :], 1.0,
                            mw[:, mb * 2048 + sub * 512: mb * 2048 + (sub + 1) * 512], ALU.mult, ALU.add),
                            waits=[(T, t_sc), (dM[mb], mask_loaded[mb]), (Sc, sm_cons_S[j])], inc=V)
                        mask_cons_V[mb] = t_ma
                        src = sm_t
                        t_mx = pb.emit("vector", lambda v, sm_t=sm_t, scr=scr: v.reduce_max(
                            mlbuf[:, scr:scr + 1], sm_t, axis=AX.X), waits=[(V, t_ma)] + scw, inc=V)
                        sc_v = t_ma
                    else:
                        src = sbank[:, :]
                        t_mx = pb.emit("vector", lambda v, sbank=sbank, scr=scr: v.reduce_max(
                            mlbuf[:, scr:scr + 1], sbank[:, :], axis=AX.X),
                            waits=[(T, t_sc)] + scw, inc=V)
                        sc_v = t_mx
                    mcol = mlbuf[:, qt:qt + 1]
                    lcol = mlbuf[:, 16 + qt:17 + qt]
                    fp = first_pair[qt]
                    if fp:
                        first_pair[qt] = False
                        t_m = pb.emit("vector", lambda v, mcol=mcol, scr=scr: v.tensor_copy(mcol, mlbuf[:, scr:scr + 1]),
                                      waits=[(V, t_mx), (V, qt_state_v[qt])], inc=V)
                        t_n = pb.emit("scalar", lambda s_, scr=scr, mcol=mcol: s_.mul(
                            mlbuf[:, scr + 1:scr + 2], mcol, -1.0), waits=[(V, t_m)], inc=Sc)
                        t_e = pb.emit("scalar", lambda s_, src=src, p_t=p_t, scr=scr, lcol=lcol: s_.activation(
                            p_t, src, AF.Exp, bias=mlbuf[:, scr + 1:scr + 2], accum_out=lcol),
                            waits=[(Sc, t_n), (T, p_cons_T[psl])], inc=Sc)
                        t_l = 0
                    else:
                        t_d = pb.emit("vector", lambda v, mcol=mcol, scr=scr: v.tensor_scalar(
                            mlbuf[:, scr + 2:scr + 3], mcol, mlbuf[:, scr:scr + 1], 0.0, ALU.subtract, ALU.min),
                            waits=[(V, t_mx), (V, qt_state_v[qt])], inc=V)
                        t_cm = pb.emit("vector", lambda v, mcol=mcol, scr=scr: v.tensor_single_scalar(
                            mcol, mcol, mlbuf[:, scr + 2:scr + 3], ALU.subtract), waits=[(V, t_d)], inc=V)
                        t_al = pb.emit("scalar", lambda s_, scr=scr: s_.activation(
                            mlbuf[:, scr + 3:scr + 4], mlbuf[:, scr + 2:scr + 3], AF.Exp),
                            waits=[(V, t_d)], inc=Sc)
                        t_n = pb.emit("scalar", lambda s_, mcol=mcol, scr=scr: s_.mul(
                            mlbuf[:, scr:scr + 1], mcol, -1.0),
                            waits=[(Sc, t_al), (V, t_cm)], inc=Sc)
                        t_e = pb.emit("scalar", lambda s_, src=src, p_t=p_t, scr=scr: s_.activation(
                            p_t, src, AF.Exp, bias=mlbuf[:, scr:scr + 1], accum_out=mlbuf[:, scr + 4:scr + 5]),
                            waits=[(Sc, t_n), (T, p_cons_T[psl])], inc=Sc)
                        t_l = pb.emit("vector", lambda v, lcol=lcol, scr=scr: v.tensor_scalar(
                            lcol, lcol, mlbuf[:, scr + 3:scr + 4], mlbuf[:, scr + 4:scr + 5], ALU.mult, ALU.add),
                            waits=[(Sc, t_e)], inc=V)
                    sc_cons[j] = (sc_v, t_e)
                    if maskable:
                        sm_cons_S[j] = t_e
                    st.update(fp=fp, scr=scr, p_t=p_t, psl=psl, t_e=t_e, t_l=t_l)

            def group_tailA(grp):
                nonlocal pt_copy_S
                sts = grp["sts"]
                for st in sts:
                    j, t_e, p_t = st["j"], st["t_e"], st["p_t"]
                    for kt in range(4):
                        pb.emit("tensor", lambda t, kt=kt, p_t=p_t, j=j: t.transpose(
                            psE.bitcast(BF16)[:, j * 512 + kt * 128: j * 512 + (kt + 1) * 128],
                            p_t[:, kt * 128:(kt + 1) * 128], idb[:, :]),
                            waits=[(Sc, t_e), (Sc, pt_copy_S)] if kt == 0 else (),
                            inc=(T if (kt == 3 and j == 1) else None))
                t_tr = T.n
                for st in sts:
                    p_cons_T[st["psl"]] = t_tr
                pt_copy_S = pb.emit("scalar", lambda s_: s_.copy(ptb[:, 0:1024], psE.bitcast(BF16)[:, 0:1024]),
                                    waits=[(T, t_tr), (T, pt_cons_T[0]), (T, pt_cons_T[1])], inc=Sc)
                grp["copyS"] = pt_copy_S

            def group_tailB(grp, release_kts=False):
                nonlocal keybuf_cons_T
                sts = grp["sts"]
                dbanks = [psC, psD]
                for st in sts:
                    j, qt, scr, fp = st["j"], st["qt"], st["scr"], st["fp"]
                    dbank = dbanks[j]
                    pt_t = ptb[:, j * 512:(j + 1) * 512]
                    for kt in range(4):
                        for dh in range(2):
                            rel = release_kts and j == 1 and dh == 1
                            pb.emit("tensor", lambda t, kt=kt, dh=dh, pt_t=pt_t, dbank=dbank: t.matmul(
                                dbank[:, dh * 512:(dh + 1) * 512],
                                pt_t[:, kt * 128:(kt + 1) * 128],
                                keys[:, ktm_off + kt * 1024 + dh * 512: ktm_off + kt * 1024 + (dh + 1) * 512],
                                start=(kt == 0), stop=(kt == 3)),
                                waits=[(Sc, grp["copyS"]), (V, delta_cons[j])] if (kt == 0 and dh == 0) else (),
                                inc=(T if (rel or (kt == 3 and dh == 1)) else None))
                            if rel:
                                keybuf_kt_T[kt] = T.n
                    t_dl = T.n
                    pt_cons_T[j] = t_dl
                    keybuf_cons_T = t_dl
                    ctx_t = ctx_sb[:, qt * 1024:(qt + 1) * 1024]
                    if fp:
                        t_cx = pb.emit("vector", lambda v, ctx_t=ctx_t, dbank=dbank: v.tensor_copy(ctx_t, dbank[:, :]),
                                       waits=[(T, t_dl), (V, st["t_l"]), (T, qproj_done_T)], inc=V)
                    else:
                        t_cx = pb.emit("vector", lambda v, ctx_t=ctx_t, dbank=dbank, scr=scr: v.scalar_tensor_tensor(
                            ctx_t, ctx_t, mlbuf[:, scr + 3:scr + 4], dbank[:, :], ALU.mult, ALU.add),
                            waits=[(T, t_dl), (V, st["t_l"])], inc=V)
                    delta_cons[j] = t_cx
                    scr_cons[(grp["gi"] * 2 + j) % 4] = (t_cx, st["t_e"])
                    qt_state_v[qt] = t_cx

            gi = 0
            for ch in range(NCH):
                prep_chunk(ch)
                groups = [(slot * 4 + 2 * sp, slot * 4 + 2 * sp + 1, slot)
                          for slot in range(ch // 4, 4) for sp in range(2)]
                for idx, (qa, qb, slot) in enumerate(groups):
                    grp = group_scores(ch, (qa, qb), slot, gi)
                    if pending is not None:
                        group_tailA(pending)
                    group_softmax(grp)
                    if pending is not None:
                        group_tailB(pending)
                    pending = grp
                    gi += 1
                group_tailA(pending)
                group_tailB(pending, release_kts=True)
                pending = None
            pb.barrier()

        if nphases >= 4:
            t_ri = pb.emit("vector", lambda v: v.reciprocal(mlbuf[:, 48:64], mlbuf[:, 16:32]), inc=V)
            gsc_cons_T2 = [0, 0]
            ef_cons = [0, 0]
            psEF = [psE, psF]
            gscs = [gsc[:, :], mw[:, 0:1024]]
            for qt in range(NQT):
                gb = gscs[qt % 2]
                t_s2 = pb.emit("vector", lambda v, qt=qt, gb=gb: v.tensor_scalar_mul(
                    gb, ctx_sb[:, qt * 1024:(qt + 1) * 1024], mlbuf[:, 48 + qt:49 + qt]),
                    waits=[(V, t_ri), (T, gsc_cons_T2[qt % 2])], inc=V)
                pe = psEF[qt % 2]
                for fc in range(8):
                    pb.emit("tensor", lambda t, fc=fc, pe=pe, gb=gb: t.transpose(
                        pe.bitcast(BF16)[:, fc * 128:(fc + 1) * 128],
                        gb[:, fc * 128:(fc + 1) * 128], idb[:, :]),
                        waits=[(V, t_s2), (V, ef_cons[qt % 2])] if fc == 0 else (), inc=(T if fc == 7 else None))
                gsc_cons_T2[qt % 2] = T.n
                ef_cons[qt % 2] = pb.emit("vector", lambda v, qt=qt, pe=pe: v.tensor_copy(
                    ctx_fm.rearrange("p (c n) -> p c n", c=8)[:, :, qt * 128:(qt + 1) * 128],
                    pe.rearrange("p (c n) -> p c n", c=8)),
                    waits=[(T, gsc_cons_T2[qt % 2])], inc=V)
            ctxfm_done_v = V.n

            t_bout = pb.sdma(gW[0], lambda g: g.dma_start(out=bout_sb[0:1, 0:1024], in_=d_bout),
                             waits=[(V, ctxfm_done_v)])
            wstg_cons_T = [0, 0]
            wef_cons = [ef_cons[0], ef_cons[1]]
            wstgs = [wstg, mw[:, 1024:3072]]
            for dt in range(8):
                wsl = dt % 2
                wst = wstgs[wsl]
                tw = pb.sdma(gV[dt % 4], lambda g, dt=dt, wst=wst: g.dma_start(
                    out=wst[:, 0:2048], in_=d_wout[dt * 128:(dt + 1) * 128, :]),
                    waits=[(T, wstg_cons_T[wsl]), (V, ctxfm_done_v)])
                for half in range(2):
                    pe = psEF[half]
                    for k in range(8):
                        f2c = half * 8 + k
                        pb.emit("tensor", lambda t, f2c=f2c, k=k, pe=pe, tw=tw, dt=dt, wst=wst: t.transpose(
                            pe.bitcast(BF16)[:, k * 128:(k + 1) * 128],
                            wst[:, f2c * 128:(f2c + 1) * 128], idb[:, :]),
                            waits=[(gV[dt % 4], tw), (V, wef_cons[half])] if k == 0 else (),
                            inc=(T if k == 7 else None))
                    t_tr = T.n
                    wef_cons[half] = pb.emit("vector", lambda v, dt=dt, half=half, pe=pe: v.tensor_copy(
                        wout_fm.rearrange("p (c n) -> p c n", c=16)[:, half * 8:(half + 1) * 8, dt * 128:(dt + 1) * 128],
                        pe.rearrange("p (c n) -> p c n", c=8)),
                        waits=[(T, t_tr)], inc=V)
                wstg_cons_T[wsl] = T.n
            wout_ready_v = V.n

            gtiles = [scr8[:, 2048:4096].bitcast(BF16), gsc[:, :]]
            cd_cons_S = [0, 0]
            stile_cons_V = [0, 0]
            gtile_cons_V = [0, 0]
            stg_slot_v = [0, 0]
            ost_dma = [0, 0]
            for qt in range(NQT):
                sl = qt % 2
                cd = psC if qt % 2 == 0 else psD
                for dh in range(2):
                    for fc in range(16):
                        lhs = (xnfm[:, fc * 2048 + qt * 128: fc * 2048 + (qt + 1) * 128] if fc < 8
                               else ctx_fm[:, (fc - 8) * 2048 + qt * 128: (fc - 8) * 2048 + (qt + 1) * 128])
                        pb.emit("tensor", lambda t, lhs=lhs, dh=dh, fc=fc, cd=cd: t.matmul(
                            cd[:, dh * 512:(dh + 1) * 512], lhs,
                            wout_fm[:, fc * 1024 + dh * 512: fc * 1024 + (dh + 1) * 512],
                            start=(fc == 0), stop=False),
                            waits=[(V, ctxfm_done_v), (V, wout_ready_v), (V, t_ones),
                                   (gW[0], t_bout), (Sc, cd_cons_S[qt % 2])] if (fc == 0 and dh == 0) else (),
                            inc=None)
                    pb.emit("tensor", lambda t, dh=dh, cd=cd: t.matmul(
                        cd[:, dh * 512:(dh + 1) * 512], cnst_sb[0:1, 128:256],
                        bout_sb[0:1, dh * 512:(dh + 1) * 512], start=False, stop=True),
                        inc=(T if dh == 1 else None))
                tmm = T.n
                tl = pb.emit("sync", lambda s_, qt=qt, sl=sl: s_.dma_start(
                    out=stg[:, sl * 1024:(sl + 1) * 1024], in_=d_xq[qt * 128:(qt + 1) * 128, :]),
                    waits=[(V, stg_slot_v[sl]), (dO[sl], ost_dma[sl])], inc=dX[sl], inck=16)
                t_ta = pb.emit("scalar", lambda s_, cd=cd, sl=sl: s_.activation(
                    stile_b[sl][:, 0:1024], cd[:, :], AF.Tanh),
                    waits=[(T, tmm), (V, stile_cons_V[sl])], inc=Sc)
                cd_cons_S[qt % 2] = t_ta
                t_si = pb.emit("scalar", lambda s_, qt=qt, sl=sl: s_.activation(
                    gtiles[sl][:, 0:1024], y_all[:, qt * 1024:(qt + 1) * 1024], AF.Silu),
                    waits=[(V, gtile_cons_V[sl])], inc=Sc)
                t_mu = pb.emit("vector", lambda v, sl=sl: v.tensor_mul(
                    stile_b[sl][:, 0:1024], stile_b[sl][:, 0:1024], gtiles[sl][:, 0:1024]),
                    waits=[(Sc, t_ta), (Sc, t_si)], inc=V)
                gtile_cons_V[sl] = t_mu
                t_ad = pb.emit("vector", lambda v, sl=sl: v.tensor_add(
                    stg[:, sl * 1024:(sl + 1) * 1024], stg[:, sl * 1024:(sl + 1) * 1024], stile_b[sl][:, 0:1024]),
                    waits=[(V, t_mu), (dX[sl], tl)], inc=V)
                stile_cons_V[sl] = t_ad
                ost_dma[sl] = pb.emit("sync", lambda s_, qt=qt, sl=sl: s_.dma_start(
                    out=d_y[qt * 128:(qt + 1) * 128, :], in_=stg[:, sl * 1024:(sl + 1) * 1024]),
                    waits=[(V, t_ad)], inc=dO[sl], inck=16)
                stg_slot_v[sl] = t_ad
            pb.wait("sync", (dO[0], dO[0].n), (dO[1], dO[1].n))
            pb.barrier()

        if debug:
            pb.barrier()
            def dump(name, src):
                pb.sdma(gD, lambda g, name=name, src=src: g.dma_start(out=dbg[name], in_=src))
            dump("dbg_xnfm", xn_fm[:, :])
            if nphases == 3:
                dump("dbg_qfm", q_fm[:, :])
            dump("dbg_gates", gates[:, :])
            if nphases in (3,):
                dump("dbg_ctx", ctx_sb[:, :])
                dump("dbg_ml", mlbuf[:, 0:32])
            if nphases >= 2:
                dump("dbg_yffn", y_all[:, :])

            pb.wait("gpsimd", (gD, gD.n))

        pb.build()
    return nc


def _host_inputs(inputs):
    x = np.ascontiguousarray(np.asarray(inputs["L0_attn_hidden_states"]), dtype=np.float32)
    shared = {
        "w_dec": np.ascontiguousarray(np.asarray(inputs["L0_attn_w_dec_attn"], np.float32)),
        "w_out": np.ascontiguousarray(np.asarray(inputs["L0_attn_w_out"], np.float32)),
        "b_out": np.ascontiguousarray(np.asarray(inputs["L0_attn_b_out"], np.float32)).reshape(1, D),
        "Wg": np.ascontiguousarray(np.asarray(inputs["L0_ffn_Wg"], np.float32)),
        "W1": np.ascontiguousarray(np.asarray(inputs["L0_ffn_W1"], np.float32)),
        "b1": np.ascontiguousarray(np.asarray(inputs["L0_ffn_b1"], np.float32)).reshape(8, 32, 128),
        "W2": np.ascontiguousarray(np.asarray(inputs["L0_ffn_W2"], np.float32)),
        "b2": np.ascontiguousarray(np.asarray(inputs["L0_ffn_b2"], np.float32)),
        "ident_bf": np.eye(128, dtype=ml_dtypes.bfloat16),
        "ident_f32": np.eye(128, dtype=np.float32),
        "iota768": np.broadcast_to(np.arange(512, dtype=np.float32), (128, 512)).copy(),
        "cnst_bf": np.concatenate([
            np.triu(np.ones((128, 128), dtype=np.float32), k=1),
            np.ones((128, 128), dtype=np.float32)], axis=1).astype(ml_dtypes.bfloat16),
    }
    in_maps = []
    jj = np.arange(512)
    for core in range(8):
        s, m = core // 4, core % 4
        blocks = [m, 7 - m, 8 + m, 15 - m]
        xq = np.concatenate([x[s, b * 512:(b + 1) * 512] for b in blocks], axis=0)
        masks = np.zeros((NCH, 128, 4 * 512), dtype=ml_dtypes.bfloat16)
        for ch in range(NCH):
            slot = ch // 4
            b = blocks[slot]
            for sub in range(4):
                i_glob = b * 512 + sub * 128 + np.arange(128)
                j_glob = ch * 512 + jj
                allow = j_glob[None, :] < i_glob[:, None]
                if b == 0 and sub == 0 and ch == 0:
                    allow[0, :] = False
                    allow[0, 0] = True
                masks[ch, :, sub * 512:(sub + 1) * 512] = np.where(
                    allow, np.float32(0.0), np.float32(NEG)).astype(ml_dtypes.bfloat16)
        im = dict(shared)
        im["xq"] = np.ascontiguousarray(xq)
        im["xk"] = np.ascontiguousarray(x[s])
        im["masks"] = masks
        in_maps.append(im)
    return in_maps


_nc_cache = {}

def kernel(**inputs):
    if "full" not in _nc_cache:
        _nc_cache["full"] = build_nc()
    nc = _nc_cache["full"]
    in_maps = _host_inputs(inputs)
    res = run_bass_kernel_spmd(nc, in_maps, core_ids=list(range(8)))
    out = np.empty((2, S, D), dtype=np.float32)
    for core in range(8):
        s, m = core // 4, core % 4
        blocks = [m, 7 - m, 8 + m, 15 - m]
        shard = res.results[core]["y"]
        for k, b in enumerate(blocks):
            out[s, b * 512:(b + 1) * 512] = shard[k * 512:(k + 1) * 512]
    return out
